# version 11
# baseline (speedup 1.0000x reference)
import os
os.environ["NEURON_SCRATCHPAD_PAGE_SIZE"] = "512"
import numpy as np
import ml_dtypes
from contextlib import ExitStack

import concourse.bass as bass
import concourse.mybir as mybir
from concourse.bass_utils import run_bass_kernel_spmd

F32 = mybir.dt.float32
BF16 = mybir.dt.bfloat16
FP8 = mybir.dt.float8e4
I32 = mybir.dt.int32
AF = mybir.ActivationFunctionType
ALU = mybir.AluOpType
AX = mybir.AxisListType.X
PM = mybir.MatmulPerfMode

B, N, D = 4, 4096, 2048
H, DH = 16, 128
BLK = 256
T = 2048
NB = T // BLK
IC = 5632
EPS1 = 1e-5
EPS2 = 1e-6

_x_own_elems = 0
GR_E = 128 * 16 * 512
N_WQK = 16
_GRAN = []
for _i in range(4):
    _GRAN.append(("wout", _i, GR_E))
for _i in range(11):
    _GRAN.append(("wg", _i, GR_E))
for _i in range(11):
    _GRAN.append(("wu", _i, GR_E))
for _i in range(16):
    _GRAN.append(("wd", _i, 44 * 128 * 128))
S_SH = 5_636_096
_big = [g for g in _GRAN if g[0] != "wd"]
_wdg = [g for g in _GRAN if g[0] == "wd"]
assert len(_big) == 26 and len(_wdg) == 16
_assign = []
_bi = 0
for _s in range(8):
    _nb = 4 if _s < 2 else 3
    _assign.append(_big[_bi:_bi + _nb] + _wdg[2 * _s:2 * _s + 2])
    _bi += _nb
GRAN_OFF = {}
for _s, _gl in enumerate(_assign):
    _off = _x_own_elems
    for _g in _gl:
        GRAN_OFF[(_g[0], _g[1])] = _s * S_SH + _off
        _off += _g[2]
    assert _off <= S_SH
PAY_TOT = 8 * S_SH


class _Buf:
    __slots__ = ("name", "last_write", "readers", "xread", "fence")

    def __init__(self, name, xread, fence=()):
        self.name = name
        self.last_write = None
        self.readers = []
        self.xread = xread
        self.fence = list(fence)


class _Op:
    __slots__ = ("engine", "emit", "inc_amount", "needs_inc", "cum", "deps",
                 "tag", "barrier", "phase")

    def __init__(self, engine, emit, inc_amount, tag, phase):
        self.engine = engine
        self.emit = emit
        self.inc_amount = inc_amount
        self.needs_inc = False
        self.cum = -1
        self.deps = []
        self.tag = tag
        self.barrier = False
        self.phase = phase


class Sched:
    SECTION = {"T": "tensor", "V": "vector", "S": "scalar", "Q": "sync",
               "G": "gpsimd", "C": "gpsimd", "P": "gpsimd", "R": "scalar"}

    def __init__(self):
        self.pending = {k: [] for k in self.SECTION.values()}
        self.bufs = {}
        self.phase = 0
        self.tails = {}
        self.free_fence = {}
        self.sems = None
        self.cum = {k: 0 for k in ("T", "V", "S", "Q", "G", "C", "P", "R")}
        self.waited = {sec: {k: 0 for k in ("T", "V", "S", "Q", "G", "C", "P", "R")}
                       for sec in set(self.SECTION.values())}
        self.prev_dma = {"Q": 0, "G": 0, "R": 0}

    def buf(self, name, xread=False):
        b = self.bufs.get(name)
        if b is None:
            b = self.bufs[name] = _Buf(name, xread, self.free_fence.values())
        if xread:
            b.xread = True
        return b

    def op(self, engine, emit, reads=(), writes=(), tag="", barrier=False):
        rd = [self.buf(b) if isinstance(b, str) else b for b in reads]
        wr = [self.buf(b) if isinstance(b, str) else b for b in writes]
        o = _Op(engine, emit, 16 if engine in "QGR" else 1, tag, self.phase)
        o.barrier = barrier
        if engine in "QGCR" and not barrier:
            o.needs_inc = True
        deps = []
        for b in rd:
            if b.last_write is not None:
                deps.append(b.last_write)
            if b.xread:
                for r in b.readers:
                    if r.engine != engine:
                        deps.append(r)
        for b in wr:
            deps.extend(b.readers)
            if b.last_write is not None:
                deps.append(b.last_write)
            elif not b.readers:
                deps.extend(b.fence)
        res = []
        for d in deps:
            if d is o:
                continue
            if d.phase == self.phase:
                if d.barrier:
                    raise RuntimeError(f"dep on barrier op {d.tag} <- {tag}")
                d.needs_inc = True
                res.append(d)
            elif d.needs_inc:
                res.append(d)
            else:
                res.extend(self.tails.values())
        seen = set()
        o.deps = [d for d in res if not (id(d) in seen or seen.add(id(d)))]
        for b in rd:
            b.readers.append(o)
        for b in wr:
            b.last_write = o
            b.readers = []
        self.pending[self.SECTION[engine]].append(o)
        return o

    def emit_phase(self, nc, ctx):
        if self.sems is None:
            self.sems = {k: ctx.enter_context(nc.semaphore(f"sem_{k}"))
                         for k in ("T", "V", "S", "Q", "G", "C", "P", "R")}
        for sec, ops in self.pending.items():
            if ops and not ops[-1].barrier:
                ops[-1].needs_inc = True
        for sec, ops in self.pending.items():
            for o in ops:
                if o.needs_inc:
                    self.cum[o.engine] += o.inc_amount
                    o.cum = self.cum[o.engine]
        sched = self

        def make_body(section):
            def body(eng):
                waited = sched.waited[section]
                for o in sched.pending[section]:
                    if o.engine in sched.prev_dma:
                        pd = sched.prev_dma[o.engine]
                        if pd > waited[o.engine]:
                            eng.wait_ge(sched.sems[o.engine], pd)
                            waited[o.engine] = pd
                    for d in o.deps:
                        if d.cum <= 0:
                            raise RuntimeError(f"dep no inc: {d.tag}<-{o.tag}")
                        if d.cum > waited[d.engine]:
                            eng.wait_ge(sched.sems[d.engine], d.cum)
                            waited[d.engine] = d.cum
                    inst = o.emit(eng)
                    if o.needs_inc:
                        if inst is None:
                            raise RuntimeError(f"{o.tag}: inc w/o inst")
                        inst.then_inc(sched.sems[o.engine], o.inc_amount)
                        if o.engine in sched.prev_dma:
                            sched.prev_dma[o.engine] = o.cum
            return body

        active = [sec for sec in ("sync", "gpsimd", "vector", "scalar", "tensor")
                  if self.pending[sec]]
        if active:
            with nc.Block() as blk:
                for sec in active:
                    getattr(blk, sec)(make_body(sec))
        for sec, ops in self.pending.items():
            if ops:
                for o in reversed(ops):
                    if o.needs_inc:
                        self.tails[o.engine] = o
                        break
        self.pending = {k: [] for k in self.SECTION.values()}
        self.phase += 1

    def note_free(self):
        self.free_fence = dict(self.tails)


def build_program(phases=99, probes=()):
    nc = bass.Bass("TRN2", target_bir_lowering=False)

    shard = nc.dram_tensor("shard", [1, S_SH], F32, kind="ExternalInput")
    wqk_d = nc.dram_tensor("wqk", [1, N_WQK * GR_E], F32, kind="ExternalInput")
    xin_d = nc.dram_tensor("xin", [T, D], F32, kind="ExternalInput")
    xpr_d = nc.dram_tensor("xpr", [T, D], F32, kind="ExternalInput")
    cpk_d = nc.dram_tensor("cpk", [128, 24], F32, kind="ExternalInput")
    arrq_d = nc.dram_tensor("arrq", [128, 256], F32, kind="ExternalInput")
    cmask_d = nc.dram_tensor("cmask", [2, 128, 256], F32, kind="ExternalInput")
    eye32_d = nc.dram_tensor("eye32", [128, 128], F32, kind="ExternalInput")
    eyebf_d = nc.dram_tensor("eyebf", [128, 128], BF16, kind="ExternalInput")
    eye8_d = nc.dram_tensor("eye8", [128, 128], FP8, kind="ExternalInput")
    onesbf_d = nc.dram_tensor("onesbf", [128, 128], BF16, kind="ExternalInput")
    ones32_d = nc.dram_tensor("ones32", [1, 128], F32, kind="ExternalInput")
    out_d = nc.dram_tensor("out", [T, D], F32, kind="ExternalOutput")

    shard_b = nc.dram_tensor("shard_b", [1, S_SH], F32)
    pay = nc.dram_tensor("pay", [8, S_SH], F32, addr_space="Shared")

    q_sc = nc.dram_tensor("q_sc", [H, 128, T], BF16)
    k_sc = nc.dram_tensor("k_sc", [H, 128, T], BF16)
    u_sc = nc.dram_tensor("u_sc", [H, 128, T], BF16)
    kp_sc = nc.dram_tensor("kp_sc", [H, 128, T], BF16)
    v_sc = nc.dram_tensor("v_sc", [4, 16, 128, 512], BF16)
    vp_sc = nc.dram_tensor("vp_sc", [4, 16, 128, 512], BF16)
    x2_sc = nc.dram_tensor("x2_sc", [16, 128, D], F32)
    t_sc = nc.dram_tensor("t_sc", [11, 128, 16, 512], BF16)

    s = Sched()

    with ExitStack() as ctx:
        e = ctx.enter_context

        cpk = e(nc.sbuf_tensor("cpk_s", [128, 24], F32))
        eye32 = e(nc.sbuf_tensor("eye32_s", [128, 128], F32))
        eyebf = e(nc.sbuf_tensor("eyebf_s", [128, 128], BF16))
        eye8 = e(nc.sbuf_tensor("eye8_s", [128, 128], FP8))
        onesbf = e(nc.sbuf_tensor("onesbf_s", [128, 128], BF16))
        ones32 = e(nc.sbuf_tensor("ones32_s", [1, 128], F32))
        mxs = e(nc.sbuf_tensor("mxs", [128, 16], F32))
        m2s = e(nc.sbuf_tensor("m2s", [128, 16], F32))
        m2f = e(nc.sbuf_tensor("m2f", [128, 16], F32))
        tqs = e(nc.sbuf_tensor("tqs", [128, 16], F32))
        rxs = e(nc.sbuf_tensor("rxs", [128, 16], F32))

        for nm, dst, src in (("cpk", cpk, cpk_d),
                             ("eye32", eye32, eye32_d),
                             ("eyebf", eyebf, eyebf_d), ("eye8", eye8, eye8_d),
                             ("onesbf", onesbf, onesbf_d), ("ones32", ones32, ones32_d)):
            s.op("Q", (lambda dst, src: lambda g: g.dma_start(out=dst[...], in_=src[...]))(dst, src),
                 writes=[nm], tag=f"ld_{nm}")

        s.op("G", lambda g: g.dma_start(out=shard_b[:, :], in_=shard[:, :]),
             writes=["shard_b"], tag="bounce")
        s.op("C", lambda g: g.collective_compute(
                "AllGather", ALU.bypass, replica_groups=[list(range(8))],
                ins=[shard_b.ap().opt()], outs=[pay.ap().opt()]),
             reads=["shard_b"], writes=["pay"], tag="ag")


        def pay_dma(eng, dst_ap, elem_off, dims, base=None):
            return eng.dma_start(out=dst_ap, in_=bass.AP(base if base is not None else pay,
                                                         elem_off, dims))

        def s2ap(guard, n=128):
            h = guard[0:1, 0:1].tensor
            return bass.AP(h, 0, [[2 * n, 128], [2, n]])

        sc12 = ExitStack()
        xl = sc12.enter_context(nc.sbuf_tensor("xl", [128, 16, T], BF16))
        xlp = sc12.enter_context(nc.sbuf_tensor("xlp", [128, 16, T], BF16))
        ph2 = ExitStack()
        wb2 = [ph2.enter_context(nc.sbuf_tensor(f"wb2_{i}", [128, 16, 512], BF16)) for i in range(2)]
        stg = [ph2.enter_context(nc.sbuf_tensor(f"stg{i}", [128, T], BF16)) for i in range(2)]
        vst = [ph2.enter_context(nc.sbuf_tensor(f"vst{i}", [128, 512], BF16)) for i in range(4)]
        wstg = ph2.enter_context(nc.sbuf_tensor("wstg", [128, 4, 512], F32))
        psp = [ph2.enter_context(nc.psum_tensor(f"psp{i}", [128, 512], F32)) for i in range(4)]
        s.buf("wstg")
        for i in range(2):
            s.buf(f"wb2_{i}")
            s.buf(f"stg{i}")
        for i in range(4):
            s.buf(f"vst{i}")
            s.buf(f"psp{i}", xread=True)
        ph1 = ExitStack()
        xin = [ph1.enter_context(nc.sbuf_tensor(f"xin{i}", [128, D], F32)) for i in range(2)]
        xlt = [ph1.enter_context(nc.sbuf_tensor(f"xlt{i}", [128, D], BF16)) for i in range(2)]
        st1 = [ph1.enter_context(nc.sbuf_tensor(f"st1_{i}", [128, 8], F32)) for i in range(2)]
        ps1 = [ph1.enter_context(nc.psum_tensor(f"ps1t_{i}", [128, 128], BF16)) for i in range(2)]
        s.buf("ps1t_0", xread=True)
        s.buf("ps1t_1", xread=True)

        def ln_chunk(which, tc, dest, destn):
            i2 = (which * 16 + tc) % 2
            xi, xt = xin[i2], xlt[i2]
            xn, tn = f"xin{i2}", f"xlt{i2}"
            st1_ = st1[i2]
            sn1 = f"st1_{i2}"
            tg = f"{which}_{tc}"
            xsrc = xin_d if which == 0 else xpr_d
            xsn = "xin_d" if which == 0 else "xpr_d"
            ldq = "R" if (which == 0 and tc % 2 == 1) or which == 1 else "Q"
            s.op(ldq, (lambda xi, tc, xsrc: lambda g: g.dma_start(
                    out=xi[:, :], in_=xsrc.ap()[tc * 128:(tc + 1) * 128, :]))(xi, tc, xsrc),
                 reads=[xsn], writes=[xn], tag=f"ldx{tg}")
            s.op("V", (lambda xi, st1_: lambda v: v.reduce_sum(st1_[:, 0:1], xi[:, :], axis=AX))(xi, st1_),
                 reads=[xn], writes=[sn1], tag=f"sum{tg}")
            s.op("S", (lambda xi, st1_, xt: lambda sc: sc.activation(
                    xt[:, :], xi[:, :], AF.Square, accum_out=st1_[:, 1:2]))(xi, st1_, xt),
                 reads=[xn], writes=[tn, sn1], tag=f"sq{tg}")

            def fstat(st1_):
                def f(v):
                    v.tensor_scalar(st1_[:, 2:3], st1_[:, 0:1], 1.0 / D, None, op0=ALU.mult)
                    return v.tensor_scalar(st1_[:, 3:4], st1_[:, 1:2], 1.0 / D, None, op0=ALU.mult)
                return f
            s.op("V", fstat(st1_), reads=[sn1], writes=[sn1], tag=f"st{tg}")
            s.op("V", (lambda st1_: lambda v: v.tensor_tensor(
                    st1_[:, 4:5], st1_[:, 2:3], st1_[:, 2:3], op=ALU.mult))(st1_),
                 reads=[sn1], writes=[sn1], tag=f"st2{tg}")
            s.op("V", (lambda st1_: lambda v: v.tensor_tensor(
                    st1_[:, 3:4], st1_[:, 3:4], st1_[:, 4:5], op=ALU.subtract))(st1_),
                 reads=[sn1], writes=[sn1], tag=f"st3{tg}")
            s.op("S", (lambda st1_: lambda sc: sc.activation(
                    st1_[:, 5:6], st1_[:, 3:4], AF.Sqrt, bias=cpk[:, 19:20]))(st1_),
                 reads=[sn1, "cpk"], writes=[sn1], tag=f"sqr{tg}")
            s.op("V", (lambda st1_: lambda v: v.reciprocal(st1_[:, 6:7], st1_[:, 5:6]))(st1_),
                 reads=[sn1], writes=[sn1], tag=f"rc{tg}")
            s.op("V", (lambda xi, xt, st1_: lambda v: v.tensor_scalar(
                    xt[:, :], xi[:, :], st1_[:, 2:3], st1_[:, 6:7],
                    op0=ALU.subtract, op1=ALU.mult))(xi, xt, st1_),
                 reads=[xn, sn1], writes=[tn], tag=f"nm{tg}")
            for a in range(16):
                ai = a % 2
                s.op("T", (lambda a, xt, ai: lambda t: t.transpose(
                        ps1[ai][:, :], xt[:, a * 128:(a + 1) * 128], eyebf[:, :]))(a, xt, ai),
                     reads=[tn, "eyebf"], writes=[f"ps1t_{ai}"], tag=f"tr{tg}_{a}")
                if ai == 0:
                    s.op("S", (lambda a, tc, dest, ai: lambda sc: sc.activation(
                            dest[:, a, tc * 128:(tc + 1) * 128], ps1[ai][:, :], AF.Identity))(a, tc, dest, ai),
                         reads=[f"ps1t_{ai}"], writes=[f"{destn}0"], tag=f"te{tg}_{a}")
                else:
                    s.op("V", (lambda a, tc, dest, ai: lambda v: v.tensor_copy(
                            dest[:, a, tc * 128:(tc + 1) * 128], ps1[ai][:, :]))(a, tc, dest, ai),
                         reads=[f"ps1t_{ai}"], writes=[f"{destn}1"], tag=f"te{tg}_{a}")

        CAST_SET = (0, 1, 2, 3, 4, 5, 6, 7, 12, 13, 14, 15)

        def emit_oc(oc):
                wb, wn = wb2[oc % 2], f"wb2_{oc%2}"
                if oc in CAST_SET:
                    for h4 in range(4):
                        s.op("Q", (lambda oc, h4: lambda g: g.dma_start(
                                out=wstg[:, :, :],
                                in_=bass.AP(wqk_d, oc * GR_E + h4 * 4 * 512,
                                            [[16 * 512, 128], [512, 4], [1, 512]])))(oc, h4),
                             reads=["wqk"], writes=["wstg"], tag=f"lws{oc}_{h4}")
                        s.op("V", (lambda wb, h4: lambda v: v.tensor_copy(
                                wb[:, 4 * h4:4 * h4 + 4, :], wstg[:, :, :]))(wb, h4),
                             reads=["wstg"], writes=[wn], tag=f"cst{oc}_{h4}")
                else:
                    s.op("G", (lambda oc, wb: lambda g: pay_dma(
                            g, wb[:, :, :], oc * GR_E, [[16 * 512, 128], [512, 16], [1, 512]],
                            base=wqk_d))(oc, wb),
                         reads=["wqk"], writes=[wn], tag=f"ldw{oc}")
                fam = ("q", "k", "v", "u")[oc // 4]
                if fam != "v":
                    srcs = [("o", xl, "xl")]
                    if fam == "k":
                        srcs.append(("p", xlp, "xlp"))
                    for tagp, xsrc, xsn in srcs:
                        for sub in range(4):
                            head = (oc % 4) * 4 + sub
                            sg, sn = stg[sub % 2], f"stg{sub%2}"
                            for tch in range(4):
                                pi = tch % 4

                                def fmm(wb, sub, xsrc, tch, pi):
                                    def f(t):
                                        r = None
                                        for a in range(16):
                                            r = t.matmul(psp[pi][:, :],
                                                         wb[:, a, sub * 128:(sub + 1) * 128],
                                                         xsrc[:, a, tch * 512:(tch + 1) * 512],
                                                         start=(a == 0), stop=(a == 15))
                                        return r
                                    return f
                                s.op("T", fmm(wb, sub, xsrc, tch, pi),
                                     reads=[wn, f"{xsn}0", f"{xsn}1"],
                                     writes=[f"psp{pi}"], tag=f"mm{oc}{tagp}_{sub}_{tch}")
                                func = AF.Silu if fam in ("q", "k") else AF.Identity
                                s.op("S", (lambda sg, tch, pi, func: lambda sc: sc.activation(
                                        sg[:, tch * 512:(tch + 1) * 512], psp[pi][:, :], func))(sg, tch, pi, func),
                                     reads=[f"psp{pi}"], writes=[sn], tag=f"ev{oc}{tagp}_{sub}_{tch}")
                            dst = {"q": q_sc, "k": k_sc, "u": u_sc}[fam]
                            dnm = fam + "_sc"
                            if tagp == "p":
                                dst, dnm = kp_sc, "kp_sc"
                            s.op("G", (lambda dst, head, sg: lambda g: g.dma_start(
                                    out=dst[head], in_=sg[:, :]))(dst, head, sg),
                                 reads=[sn], writes=[dnm], tag=f"sp{fam}{tagp}{head}")
                else:
                    vcc = oc - 8
                    for tagp, xsrc, xsn, dst, dnm in (("o", xl, "xl", v_sc, "v_sc"),
                                                      ("p", xlp, "xlp", vp_sc, "vp_sc")):
                        for tc in range(16):
                            pi = tc % 4

                            def fmmv(wb, tc, xsrc, pi):
                                def f(t):
                                    r = None
                                    for a in range(16):
                                        r = t.matmul(psp[pi][:, :],
                                                     xsrc[:, a, tc * 128:(tc + 1) * 128],
                                                     wb[:, a, :], start=(a == 0), stop=(a == 15))
                                    return r
                                return f
                            s.op("T", fmmv(wb, tc, xsrc, pi),
                                 reads=[wn, f"{xsn}0", f"{xsn}1"],
                                 writes=[f"psp{pi}"], tag=f"vm{oc}{tagp}_{tc}")
                            vs, vn = vst[tc % 4], f"vst{tc%4}"
                            s.op("S", (lambda vs, pi: lambda sc: sc.activation(
                                    vs[:, :], psp[pi][:, :], AF.Identity))(vs, pi),
                                 reads=[f"psp{pi}"], writes=[vn], tag=f"vev{oc}{tagp}_{tc}")
                            s.op("G", (lambda dst, vcc, tc, vs: lambda g: g.dma_start(
                                    out=dst[vcc, tc], in_=vs[:, :]))(dst, vcc, tc, vs),
                                 reads=[vn], writes=[dnm], tag=f"vsp{oc}{tagp}_{tc}")
        if phases >= 1:
            for tc in range(16):
                ln_chunk(0, tc, xl, "xl")
        if phases >= 2:
            for k, oc in enumerate((0, 1, 2, 3)):
                emit_oc(oc)
                if phases >= 1:
                    for j in range(4):
                        ln_chunk(1, 4 * k + j, xlp, "xlp")
            for oc in (4, 5, 6, 7, 12, 13, 14, 15, 8, 9, 10, 11):
                emit_oc(oc)
        elif phases >= 1:
            for tc in range(16):
                ln_chunk(1, tc, xlp, "xlp")
        s.emit_phase(nc, ctx)
        ph1.close()
        ph2.close()
        sc12.close()
        s.note_free()

        sc5bc = ExitStack()
        wsd_row = sc5bc.enter_context(nc.sbuf_tensor("wsd_row", [1, D], F32))
        sc34 = ExitStack()
        og = sc34.enter_context(nc.sbuf_tensor("og", [128, 16, T], BF16))
        ph3 = ExitStack()
        if phases >= 3:
            arrq = ph3.enter_context(nc.sbuf_tensor("arrq_s", [128, 256], F32))
            cmask = ph3.enter_context(nc.sbuf_tensor("cmask_s", [128, 2, 256], F32))
            s.op("Q", lambda g: g.dma_start(out=arrq[:, :], in_=arrq_d[:, :]),
                 writes=["arrq"], tag="ld_arrq")
            s.op("Q", lambda g: g.dma_start(out=cmask[:, :, :],
                                            in_=cmask_d.ap().rearrange("a p t -> p a t")),
                 writes=["cmask"], tag="ld_cmask")
            qdec = ph3.enter_context(nc.sbuf_tensor("qdec", [128, 16, 256], BF16))
            mask = ph3.enter_context(nc.sbuf_tensor("mask", [128, 16, 2, 256], BF16))
            kdec = ph3.enter_context(nc.sbuf_tensor("kdec", [128, 16, 2], F32))
            bdc = ph3.enter_context(nc.sbuf_tensor("bdc", [128, 16], F32))
            tmpd = ph3.enter_context(nc.sbuf_tensor("tmpd", [128, 2, 256], F32))
            kv = ph3.enter_context(nc.sbuf_tensor("kv", [128, 16, 128], F32))
            kvb = [ph3.enter_context(nc.sbuf_tensor(f"kvb{i}", [128, 128], BF16)) for i in range(2)]
            qb = [ph3.enter_context(nc.sbuf_tensor(f"qb{i}", [128, 16, 256], BF16)) for i in range(2)]
            kb = [ph3.enter_context(nc.sbuf_tensor(f"kb{i}", [128, 16, 256], BF16)) for i in range(2)]
            vb = [ph3.enter_context(nc.sbuf_tensor(f"vb{i}", [128, 2, 2048], BF16)) for i in range(2)]
            ub = [ph3.enter_context(nc.sbuf_tensor("ub0", [128, 16, 256], BF16))]
            kqm = [ph3.enter_context(nc.sbuf_tensor(f"kqm{i}", [128, 2, 256], BF16)) for i in range(2)]
            ktda = [ph3.enter_context(nc.sbuf_tensor(f"ktda{i}", [128, 2, 16, 128], BF16)) for i in range(2)]
            qdq = [ph3.enter_context(nc.sbuf_tensor(f"qdq{i}", [128, 256], BF16)) for i in range(2)]
            obf = ph3.enter_context(nc.sbuf_tensor("obf", [128, 16, 256], BF16))
            osq = ph3.enter_context(nc.sbuf_tensor("osq", [128, 16, 256], BF16))
            rsq = ph3.enter_context(nc.sbuf_tensor("rsq", [128, 256], F32))
            ogt = ph3.enter_context(nc.sbuf_tensor("ogt", [128, 2, 256], F32))
            ps_kq = [ph3.enter_context(nc.psum_tensor(f"ps_kq{i}", [128, 512], F32)) for i in range(2)]
            ps_o = [ph3.enter_context(nc.psum_tensor(f"ps_o{i}", [128, 256], F32)) for i in range(2)]
            ps_kv = ph3.enter_context(nc.psum_tensor("ps_kv", [128, 128], F32))
            ps_kt = [ph3.enter_context(nc.psum_tensor(f"ps_kt{i}", [128, 256], BF16)) for i in range(2)]
            ps_rms = ph3.enter_context(nc.psum_tensor("ps_rms", [128, 256], F32))
            for n in ("ps_kq0", "ps_kq1", "ps_o0", "ps_o1", "ps_kv", "ps_kt0", "ps_kt1", "ps_rms"):
                s.buf(n, xread=True)

            for h in range(16):
                s.op("V", (lambda h: lambda v: v.tensor_scalar(
                        tmpd[:, 0, :], arrq[:, :], cpk[:, 2 + h:3 + h], None, op0=ALU.mult))(h),
                     reads=["arrq", "cpk"], writes=["tmpd"], tag=f"qd0_{h}")
                s.op("S", (lambda h: lambda sc: sc.activation(
                        qdec[:, h, :], tmpd[:, 0, :], AF.Exp, scale=-1.0))(h),
                     reads=["tmpd"], writes=["qdec"], tag=f"qd1_{h}")

                def fm(h):
                    def g(v):
                        v.tensor_scalar(tmpd[:, 0, :], cmask[:, 0, :], cpk[:, 2 + h:3 + h], None, op0=ALU.mult)
                        return v.tensor_scalar(tmpd[:, 1, :], cmask[:, 1, :], cpk[:, 2 + h:3 + h], None, op0=ALU.mult)
                    return g
                s.op("V", fm(h), reads=["cmask", "cpk"], writes=["tmpd"], tag=f"mk0_{h}")

                def fe(h):
                    def g(sc):
                        sc.activation(mask[:, h, 0, :], tmpd[:, 0, :], AF.Exp, scale=-1.0)
                        return sc.activation(mask[:, h, 1, :], tmpd[:, 1, :], AF.Exp, scale=-1.0)
                    return g
                s.op("S", fe(h), reads=["tmpd"], writes=["mask"], tag=f"mk1_{h}")
                s.op("V", (lambda h: lambda v: v.tensor_scalar(
                        tmpd[:, 0, 0:2], cpk[:, 0:2], cpk[:, 2 + h:3 + h], None, op0=ALU.mult))(h),
                     reads=["cpk"], writes=["tmpd"], tag=f"kd0_{h}")
                s.op("S", (lambda h: lambda sc: sc.activation(
                        kdec[:, h, :], tmpd[:, 0, 0:2], AF.Exp, scale=-1.0))(h),
                     reads=["tmpd"], writes=["kdec"], tag=f"kd1_{h}")
            s.op("S", lambda sc: sc.activation(bdc[:, :], cpk[:, 2:18], AF.Exp, scale=-256.0),
                 reads=["cpk"], writes=["bdc"], tag="bd")
            s.op("V", lambda v: v.memset(kv[:, :, :], 0.0),
                 writes=[f"kv_h{h}" for h in range(16)], tag="kv0")

            def load_blk(b, par, pre):
                ksrc, kn = (kp_sc, "kp_sc") if pre else (k_sc, "k_sc")
                vsrc, vn = (vp_sc, "vp_sc") if pre else (v_sc, "v_sc")
                if not pre:
                    s.op("Q", (lambda b, par: lambda g: g.dma_start(
                            out=qb[par][:, :, :],
                            in_=q_sc.ap()[:, :, b * 256:(b + 1) * 256].rearrange("h p t -> p h t")))(b, par),
                         reads=["q_sc"], writes=[f"qb{par}"], tag=f"lq{b}")
                    s.op("Q", (lambda b: lambda g: g.dma_start(
                            out=ub[0][:, :, :],
                            in_=u_sc.ap()[:, :, b * 256:(b + 1) * 256].rearrange("h p t -> p h t")))(b),
                         reads=["u_sc"], writes=["ub0"], tag=f"lu{b}")
                s.op("Q", (lambda b, par, ksrc: lambda g: g.dma_start(
                        out=kb[par][:, :, :],
                        in_=ksrc.ap()[:, :, b * 256:(b + 1) * 256].rearrange("h p t -> p h t")))(b, par, ksrc),
                     reads=[kn], writes=[f"kb{par}"], tag=f"lk{pre}{b}")
                for vcc in range(4):
                    s.op("Q", (lambda b, par, vcc, vsrc: lambda g: g.dma_start(
                            out=vb[par][:, :, vcc * 512:(vcc + 1) * 512],
                            in_=vsrc.ap()[vcc, 2 * b:2 * b + 2, :, :].rearrange("c p n -> p c n")))(b, par, vcc, vsrc),
                         reads=[vn], writes=[f"vb{par}"], tag=f"lv{pre}{b}_{vcc}")

            def ktrans_blk(b, par, pre):
                kn = "kp_sc" if pre else "k_sc"
                for h in range(16):
                    pk = ps_kt[(2 * b + h) % 2]
                    pkn = f"ps_kt{(2 * b + h) % 2}"

                    def ftr(h, par, pk):
                        def f(t):
                            t.transpose(pk[:, 0:128], kb[par][:, h, 0:128], eyebf[:, :])
                            return t.transpose(pk[:, 128:256], kb[par][:, h, 128:256], eyebf[:, :])
                        return f
                    s.op("T", ftr(h, par, pk), reads=[f"kb{par}", "eyebf"], writes=[pkn],
                         tag=f"kt{pre}{b}_{h}")

                    def fkd(h, par, pk):
                        def f(sc):
                            sc.activation(ktda[par][:, 0, h, :], pk[:, 0:128], AF.Identity,
                                          scale=kdec[:, h, 0:1])
                            return sc.activation(ktda[par][:, 1, h, :], pk[:, 128:256], AF.Identity,
                                                 scale=kdec[:, h, 1:2])
                        return f
                    s.op("S", fkd(h, par, pk), reads=[pkn, "kdec"], writes=[f"ktda{par}_h{h}"],
                         tag=f"kd{pre}{b}_{h}")

            def km_op(h, b, par, pre, pk, pkn):
                def fkv(h, par, pk):
                    def f(t):
                        t.matmul(pk[:, :], ktda[par][:, 0, h, :], vb[par][:, 0, h * 128:(h + 1) * 128],
                                 start=True, stop=False)
                        return t.matmul(pk[:, :], ktda[par][:, 1, h, :], vb[par][:, 1, h * 128:(h + 1) * 128],
                                        start=False, stop=True)
                    return f
                s.op("T", fkv(h, par, pk), reads=[f"ktda{par}_h{h}", f"vb{par}"], writes=[pkn],
                     tag=f"km{pre}{b}_{h}")

            def ku_op(h, b, pre, pk, pkn):
                s.op("V", (lambda h, pk: lambda v: v.scalar_tensor_tensor(
                        kv[:, h, :], kv[:, h, :], bdc[:, h:h + 1], pk[:, :],
                        op0=ALU.mult, op1=ALU.add))(h, pk),
                     reads=[pkn, "bdc", f"kv_h{h}"], writes=[f"kv_h{h}"], tag=f"ku{pre}{b}_{h}")

            pre_pk = lambda h: (ps_kq[h % 2][:, 0:128], f"ps_kq{h % 2}")
            for b in range(NB):
                load_blk(b, b % 2, True)
                ktrans_blk(b, b % 2, True)
                for h in range(16):
                    if h >= 1:
                        ku_op(h - 1, b, True, *pre_pk(h - 1))
                    km_op(h, b, b % 2, True, *pre_pk(h))
                ku_op(15, b, True, *pre_pk(15))
            s.op("V", lambda v: v.tensor_scalar(kv[:, :, :], kv[:, :, :], cpk[:, 18:19], None, op0=ALU.mult),
                 reads=["cpk"] + [f"kv_h{h}" for h in range(16)],
                 writes=[f"kv_h{h}" for h in range(16)], tag="kvflag")

            def kq_mm(h, b, par):
                i = h % 2
                def fkq(h, par, i):
                    def f(t):
                        t.matmul(ps_kq[i][:, 0:256], kb[par][:, h, 0:128], qb[par][:, h, :],
                                 start=True, stop=True)
                        return t.matmul(ps_kq[i][:, 256:512], kb[par][:, h, 128:256], qb[par][:, h, :],
                                        start=True, stop=True)
                    return f
                s.op("T", fkq(h, par, i), reads=[f"kb{par}", f"qb{par}"],
                     writes=[f"ps_kq{i}"], tag=f"kq{b}_{h}")

            def mkq_qdq(h, b, par):
                i = h % 2
                def fmk(h, i):
                    def f(v):
                        v.tensor_tensor(kqm[i][:, 0, :], ps_kq[i][:, 0:256], mask[:, h, 0, :], op=ALU.mult)
                        return v.tensor_tensor(kqm[i][:, 1, :], ps_kq[i][:, 256:512], mask[:, h, 1, :], op=ALU.mult)
                    return f
                s.op("V", fmk(h, i), reads=[f"ps_kq{i}", "mask"], writes=[f"kqm{i}"], tag=f"mq{b}_{h}")
                s.op("P", (lambda h, par, i: lambda g: g.tensor_tensor(
                        qdq[i][:, :], qb[par][:, h, :], qdec[:, h, :], op=ALU.mult))(h, par, i),
                     reads=[f"qb{par}", "qdec"], writes=[f"qdq{i}"], tag=f"qq{b}_{h}")

            def kc_op(h, b):
                i = h % 2
                s.op("S", (lambda h, i: lambda sc: sc.activation(
                        kvb[i][:, :], kv[:, h, :], AF.Identity))(h, i),
                     reads=[f"kv_h{h}"], writes=[f"kvb{i}"], tag=f"kc{b}_{h}")

            def o_mm(h, b, par):
                i = h % 2
                def fo(h, par, i):
                    def f(t):
                        t.matmul(ps_o[i][:, 0:256], kvb[i][:, :], qdq[i][:, :], start=True, stop=False)
                        t.matmul(ps_o[i][:, 0:256], vb[par][:, 0, h * 128:(h + 1) * 128],
                                 kqm[i][:, 0, :], start=False, stop=False)
                        return t.matmul(ps_o[i][:, 0:256], vb[par][:, 1, h * 128:(h + 1) * 128],
                                        kqm[i][:, 1, :], start=False, stop=True)
                    return f
                s.op("T", fo(h, par, i), reads=[f"kvb{i}", f"qdq{i}", f"vb{par}", f"kqm{i}"],
                     writes=[f"ps_o{i}"], tag=f"om{b}_{h}")

            def o_evac(h, b):
                i = h % 2
                s.op("S", (lambda h, i: lambda sc: sc.activation(obf[:, h, :], ps_o[i][:, 0:256],
                                                                 AF.Identity))(h, i),
                     reads=[f"ps_o{i}"], writes=[f"obf_h{h}"], tag=f"oe{b}_{h}")
                s.op("S", (lambda h, i: lambda sc: sc.activation(osq[:, h, :], ps_o[i][:, 0:256],
                                                                 AF.Square))(h, i),
                     reads=[f"ps_o{i}"], writes=[f"osq_h{h}"], tag=f"os{b}_{h}")

            for b in range(NB):
                par = b % 2
                load_blk(b, par, False)
                ktrans_blk(b, par, False)
                kq_mm(0, b, par)
                kc_op(0, b)
                for h in range(16):
                    mkq_qdq(h, b, par)
                    if h + 1 < 16:
                        kq_mm(h + 1, b, par)
                        kc_op(h + 1, b)
                    o_mm(h, b, par)
                    o_evac(h, b)
                    if h >= 1:
                        ku_op(h - 1, b, False, ps_kv, "ps_kv")
                    km_op(h, b, par, False, ps_kv, "ps_kv")
                ku_op(15, b, False, ps_kv, "ps_kv")

                def frms(t):
                    r = None
                    for ct in range(16):
                        r = t.matmul(ps_rms[:, :], onesbf[:, :], osq[:, ct, :],
                                     start=(ct == 0), stop=(ct == 15))
                    return r
                s.op("T", frms, reads=["onesbf"] + [f"osq_h{h}" for h in range(16)],
                     writes=["ps_rms"], tag=f"rm{b}")
                s.op("S", lambda sc: sc.activation(rsq[:, :], ps_rms[:, :], AF.Sqrt,
                                                   scale=1.0 / D, bias=cpk[:, 20:21]),
                     reads=["ps_rms", "cpk"], writes=["rsq"], tag=f"rs{b}")
                s.op("V", lambda v: v.reciprocal(rsq[:, :], rsq[:, :]),
                     reads=["rsq"], writes=["rsq"], tag=f"rr{b}")
                for ct in range(16):
                    s.op("V", (lambda ct, b: lambda v: v.tensor_tensor(
                            ogt[:, ct % 2, :], obf[:, ct, :], rsq[:, :], op=ALU.mult))(ct, b),
                         reads=[f"obf_h{ct}", "rsq"], writes=["ogt"], tag=f"ogA{b}_{ct}")
                    s.op("V", (lambda ct, b: lambda v: v.tensor_tensor(
                            og[:, ct, b * 256:(b + 1) * 256], ogt[:, ct % 2, :],
                            ub[0][:, ct, :], op=ALU.mult))(ct, b),
                         reads=["ogt", "ub0"], writes=["og"], tag=f"ogB{b}_{ct}")
        s.emit_phase(nc, ctx)
        ph3.close()
        s.note_free()

        ph4 = ExitStack()
        if phases >= 4:
            wo = ph4.enter_context(nc.sbuf_tensor("wo", [128, 16, D], BF16))
            x2s = [ph4.enter_context(nc.sbuf_tensor(f"x2s{i}", [128, D], F32)) for i in range(2)]
            ps4 = [ph4.enter_context(nc.psum_tensor(f"ps4_{i}", [128, 512], F32)) for i in range(4)]
            for i in range(4):
                s.buf(f"ps4_{i}", xread=True)
            for wi_ in range(4):
                s.op("G", (lambda wi_: lambda g: pay_dma(
                        g, wo[:, :, wi_ * 512:(wi_ + 1) * 512], GRAN_OFF[("wout", wi_)],
                        [[16 * 512, 128], [512, 16], [1, 512]]))(wi_),
                     reads=["pay"], writes=["wo"], tag=f"ldwo{wi_}")
            for tc in range(16):
                i2 = tc % 2
                s.op("Q", (lambda tc, i2: lambda g: g.dma_start(
                        out=x2s[i2][:, :], in_=xin_d.ap()[tc * 128:(tc + 1) * 128, :]))(tc, i2),
                     reads=["xin_d"], writes=[f"x2s{i2}"], tag=f"lr{tc}")
                for fc in range(4):
                    pi = fc

                    def fmm(tc, fc, pi):
                        def f(t):
                            r = None
                            for cc in range(16):
                                r = t.matmul(ps4[pi][:, :], og[:, cc, tc * 128:(tc + 1) * 128],
                                             wo[:, cc, fc * 512:(fc + 1) * 512],
                                             start=(cc == 0), stop=(cc == 15))
                            return r
                        return f
                    s.op("T", fmm(tc, fc, pi), reads=["og", "wo"], writes=[f"ps4_{pi}"],
                         tag=f"o2_{tc}_{fc}")
                    s.op("V", (lambda tc, fc, pi, i2: lambda v: v.tensor_tensor(
                            x2s[i2][:, fc * 512:(fc + 1) * 512], ps4[pi][:, :],
                            x2s[i2][:, fc * 512:(fc + 1) * 512], op=ALU.add))(tc, fc, pi, i2),
                         reads=[f"ps4_{pi}", f"x2s{i2}"], writes=[f"x2s{i2}"], tag=f"x2_{tc}_{fc}")
                s.op("Q", (lambda tc, i2: lambda g: g.dma_start(
                        out=x2_sc[tc], in_=x2s[i2][:, :]))(tc, i2),
                     reads=[f"x2s{i2}"], writes=["x2_sc"], tag=f"sx2{tc}")
            wdm = ph4.enter_context(nc.sbuf_tensor("wdm", [128, 512], F32))
            wdm2 = ph4.enter_context(nc.sbuf_tensor("wdm2", [128, 512], F32))
            wdi = [ph4.enter_context(nc.sbuf_tensor("wdi0", [128, 11, 512], F32))]
            ps4w = ph4.enter_context(nc.psum_tensor("ps4w", [128, 512], F32))
            s.buf("ps4w", xread=True)
            for fc in range(4):
                for g4 in range(4):
                    off = GRAN_OFF[("wd", fc * 4 + g4)]
                    s.op("G", (lambda off: lambda g: pay_dma(
                            g, wdi[0][:, :, :], off, [[512, 128], [128 * 512, 11], [1, 512]]))(off),
                         reads=["pay"], writes=["wdi0"], tag=f"wd{fc}_{g4}")

                    for j in range(11):
                        if g4 == 0 and j == 0:
                            s.op("V", lambda v: v.scalar_tensor_tensor(
                                    wdm[:, :], wdi[0][:, 0, :], -1.0, wdi[0][:, 0, :],
                                    op0=ALU.mult, op1=ALU.max),
                                 reads=["wdi0"], writes=["wdm"], tag=f"wm{fc}_{g4}_0")
                        else:
                            s.op("V", (lambda j: lambda v: v.scalar_tensor_tensor(
                                    wdm2[:, :], wdi[0][:, j, :], -1.0, wdi[0][:, j, :],
                                    op0=ALU.mult, op1=ALU.max))(j),
                                 reads=["wdi0"], writes=["wdm2"], tag=f"wmA{fc}_{g4}_{j}")
                            s.op("V", lambda v: v.tensor_tensor(
                                    wdm[:, :], wdm[:, :], wdm2[:, :], op=ALU.max),
                                 reads=["wdm", "wdm2"], writes=["wdm"], tag=f"wmB{fc}_{g4}_{j}")

                def ftr5(t):
                    r = None
                    for j in range(4):
                        r = t.transpose(ps4w[:, j * 128:(j + 1) * 128],
                                        wdm[:, j * 128:(j + 1) * 128], eye32[:, :])
                    return r
                s.op("T", ftr5, reads=["wdm", "eye32"], writes=["ps4w"], tag=f"wt{fc}")

                def fred(v):
                    r = None
                    for j in range(4):
                        r = v.reduce_max(wdm[:, j:j + 1], ps4w[:, j * 128:(j + 1) * 128], axis=AX)
                    return r
                s.op("V", fred, reads=["ps4w"], writes=["wdm"], tag=f"wr{fc}")
                s.op("V", lambda v: v.tensor_scalar(wdm[:, 0:4], wdm[:, 0:4], 1.0 / 448.0, None,
                                                    op0=ALU.mult),
                     reads=["wdm"], writes=["wdm"], tag=f"ws{fc}")
                for jj in range(4):
                    s.op("T", (lambda jj: lambda t: t.transpose(
                            ps4w[0:1, 256:384], wdm[:, jj:jj + 1], eye32[:, :]))(jj),
                         reads=["wdm", "eye32"], writes=["ps4w"], tag=f"wfT{fc}_{jj}")
                    s.op("V", (lambda fc, jj: lambda v: v.tensor_copy(
                            wsd_row[0:1, fc * 512 + jj * 128:fc * 512 + (jj + 1) * 128],
                            ps4w[0:1, 256:384]))(fc, jj),
                         reads=["ps4w"], writes=["wsd_row"], tag=f"wfC{fc}_{jj}")


        s.emit_phase(nc, ctx)
        ph4.close()
        sc34.close()
        s.note_free()

        qt = sc5bc.enter_context(nc.sbuf_tensor("qt", [128, 44, T], FP8))
        sc5 = ExitStack()
        qx = sc5.enter_context(nc.sbuf_tensor("qx", [128, 16, T], FP8))
        ph5a = ExitStack()
        if phases >= 5:
            x2i = [ph5a.enter_context(nc.sbuf_tensor(f"x2i{i}", [128, D], F32)) for i in range(3)]
            xl2 = [ph5a.enter_context(nc.sbuf_tensor(f"xl2_{i}", [128, D], F32)) for i in range(3)]
            q8t = [ph5a.enter_context(nc.sbuf_tensor(f"q8t_{i}", [128, D], FP8)) for i in range(3)]
            st5 = [ph5a.enter_context(nc.sbuf_tensor(f"st5_{i}", [128, 8], F32)) for i in range(3)]
            ps5 = ph5a.enter_context(nc.psum_tensor("ps5", [128, 512], F32))
            ps5q = [ph5a.enter_context(nc.psum_tensor(f"ps5q_{i}", [128, 256], FP8)) for i in range(2)]
            s.buf("ps5", xread=True)
            s.buf("ps5q_0", xread=True)
            s.buf("ps5q_1", xread=True)

            for tc in range(16):
                i2 = tc % 3
                xi, xn = x2i[i2], f"x2i{i2}"
                sv, svn = st5[i2], f"st5_{i2}"
                xlv, xlvn = xl2[i2], f"xl2_{i2}"
                qv, qvn = q8t[i2], f"q8t_{i2}"
                s.op("Q", (lambda tc, xi: lambda g: g.dma_start(
                        out=xi[:, :], in_=x2_sc[tc]))(tc, xi),
                     reads=["x2_sc"], writes=[xn], tag=f"l2{tc}")
                s.op("V", (lambda xi, sv: lambda v: v.reduce_sum(sv[:, 0:1], xi[:, :], axis=AX))(xi, sv),
                     reads=[xn], writes=[svn], tag=f"a{tc}")
                s.op("S", (lambda xi, sv, xlv: lambda sc: sc.activation(
                        xlv[:, :], xi[:, :], AF.Square, accum_out=sv[:, 1:2]))(xi, sv, xlv),
                     reads=[xn], writes=[xlvn, svn], tag=f"b{tc}")

                def fst(sv):
                    def f(v):
                        v.tensor_scalar(sv[:, 2:3], sv[:, 0:1], 1.0 / D, None, op0=ALU.mult)
                        return v.tensor_scalar(sv[:, 3:4], sv[:, 1:2], 1.0 / D, None, op0=ALU.mult)
                    return f
                s.op("V", fst(sv), reads=[svn], writes=[svn], tag=f"c{tc}")
                s.op("V", (lambda sv: lambda v: v.tensor_tensor(
                        sv[:, 4:5], sv[:, 2:3], sv[:, 2:3], op=ALU.mult))(sv),
                     reads=[svn], writes=[svn], tag=f"c2{tc}")
                s.op("V", (lambda sv: lambda v: v.tensor_tensor(
                        sv[:, 3:4], sv[:, 3:4], sv[:, 4:5], op=ALU.subtract))(sv),
                     reads=[svn], writes=[svn], tag=f"c3{tc}")
                s.op("S", (lambda sv: lambda sc: sc.activation(
                        sv[:, 5:6], sv[:, 3:4], AF.Sqrt, bias=cpk[:, 19:20]))(sv),
                     reads=[svn, "cpk"], writes=[svn], tag=f"d{tc}")
                s.op("V", (lambda sv: lambda v: v.reciprocal(sv[:, 6:7], sv[:, 5:6]))(sv),
                     reads=[svn], writes=[svn], tag=f"e{tc}")
                s.op("V", (lambda xi, sv, xlv: lambda v: v.tensor_scalar(
                        xlv[:, :], xi[:, :], sv[:, 2:3], sv[:, 6:7],
                        op0=ALU.subtract, op1=ALU.mult))(xi, sv, xlv),
                     reads=[xn, svn], writes=[xlvn], tag=f"f{tc}")
                s.op("V", (lambda sv, xlv: lambda v: v.reduce_max(
                        sv[:, 0:1], xlv[:, :], axis=AX, apply_absolute_value=True))(sv, xlv),
                     reads=[xlvn], writes=[svn], tag=f"g{tc}")

                s.op("V", (lambda sv: lambda v: v.reciprocal(sv[:, 1:2], sv[:, 0:1]))(sv),
                     reads=[svn], writes=[svn], tag=f"h{tc}")
                s.op("V", (lambda sv: lambda v: v.tensor_scalar(
                        sv[:, 1:2], sv[:, 1:2], 224.0, None, op0=ALU.mult))(sv),
                     reads=[svn], writes=[svn], tag=f"h2{tc}")
                s.op("S", (lambda sv, xlv, qv: lambda sc: sc.activation(
                        qv[:, :], xlv[:, :], AF.Identity, scale=sv[:, 1:2]))(sv, xlv, qv),
                     reads=[xlvn, svn], writes=[qvn], tag=f"i{tc}")
                s.op("V", (lambda tc, sv: lambda v: v.tensor_copy(mxs[:, tc:tc + 1], sv[:, 0:1]))(tc, sv),
                     reads=[svn], writes=["mxs"], tag=f"j{tc}")
                for a in range(16):
                    ai = a % 2
                    s.op("T", (lambda a, qv, ai: lambda t: t.transpose(
                            s2ap(ps5q[ai]), qv[:, a * 128:(a + 1) * 128], eye8[:, :]))(a, qv, ai),
                         reads=[qvn, "eye8"], writes=[f"ps5q_{ai}"], tag=f"k{tc}_{a}")
                    if ai == 0:
                        s.op("S", (lambda a, tc, ai: lambda sc: sc.activation(
                                qx[:, a, tc * 128:(tc + 1) * 128], s2ap(ps5q[ai]), AF.Identity))(a, tc, ai),
                             reads=[f"ps5q_{ai}"], writes=["qx0"], tag=f"l{tc}_{a}")
                    else:
                        s.op("V", (lambda a, tc, ai: lambda v: v.tensor_copy(
                                qx[:, a, tc * 128:(tc + 1) * 128], s2ap(ps5q[ai])))(a, tc, ai),
                             reads=[f"ps5q_{ai}"], writes=["qx1"], tag=f"l{tc}_{a}")
        s.emit_phase(nc, ctx)
        ph5a.close()
        s.note_free()

        ph5b = ExitStack()
        if phases >= 6:
            wmac = ph5b.enter_context(nc.sbuf_tensor("wmac", [128, 128], F32))
            wabs = ph5b.enter_context(nc.sbuf_tensor("wabs", [128, 8, 128], F32))
            wred = ph5b.enter_context(nc.sbuf_tensor("wred", [128, 2], F32))
            wrow = ph5b.enter_context(nc.sbuf_tensor("wrow", [1, 512], F32))
            wibc = ph5b.enter_context(nc.sbuf_tensor("wibc", [128, 2, 512], F32))
            wsbc = ph5b.enter_context(nc.sbuf_tensor("wsbc", [128, 4, 512], F32))
            gtmp = [ph5b.enter_context(nc.sbuf_tensor(f"gtmp{i}", [128, 512], F32)) for i in range(2)]
            tstg = [ph5b.enter_context(nc.sbuf_tensor(f"tstg{i}", [128, 512], F32)) for i in range(2)]
            tq8 = [ph5b.enter_context(nc.sbuf_tensor(f"tq8_{i}", [128, 512], FP8))
                   for i in range(2)]
            mtmp = ph5b.enter_context(nc.sbuf_tensor("mtmp", [128, 2], F32))
            gup = ExitStack()
            wgi = gup.enter_context(nc.sbuf_tensor("wgi", [128, 16, 128], F32))
            wui = gup.enter_context(nc.sbuf_tensor("wui", [128, 16, 128], F32))
            qwg = [gup.enter_context(nc.sbuf_tensor(f"qwg{i}", [128, 16, 512], FP8)) for i in range(2)]
            qwu = [gup.enter_context(nc.sbuf_tensor(f"qwu{i}", [128, 16, 512], FP8)) for i in range(2)]
            wrowt = gup.enter_context(nc.sbuf_tensor("wrowt", [1, 256], F32))
            psg = [gup.enter_context(nc.psum_tensor(f"psg{i}", [128, 512], F32)) for i in range(3)]
            psu = [gup.enter_context(nc.psum_tensor(f"psu{i}", [128, 512], F32)) for i in range(3)]
            psb = gup.enter_context(nc.psum_tensor("psb", [128, 512], F32))
            psw = gup.enter_context(nc.psum_tensor("psw", [128, 512], F32))
            for n in ("psg0", "psg1", "psg2", "psu0", "psu1", "psu2", "psb", "psw"):
                s.buf(n, xread=True)

            s.op("V", lambda v: v.memset(m2s[:, :], 0.0), writes=["m2s"], tag="m2z")
            s.op("V", lambda v: v.tensor_scalar(rxs[:, :], mxs[:, :], 1.0 / 112.0, None, op0=ALU.mult),
                 reads=["mxs"], writes=["rxs"], tag="rxs")

            def wprep_sub(icc, sub, mat, wi, win, qw, qwn, slot, off):
                ib = icc % 2
                """Load one [2048f x 128ic] fp32 sub-chunk, compute its ws,
                bcast scale slices, quantize into qw[:, :, sub*128...]."""
                s.op("G", (lambda off, wi: lambda g: pay_dma(
                        g, wi[:, :, :], off, [[16 * 512, 128], [512, 16], [1, 128]]))(off, wi),
                     reads=["pay"], writes=[win], tag=f"lw{mat}{icc}_{sub}")

                for rnd in range(2):
                    s.op("S", (lambda wi, rnd: lambda sc: sc.activation(
                            wabs[:, :, :], wi[:, 8 * rnd:8 * rnd + 8, :], AF.Abs))(wi, rnd),
                         reads=[win, "wmac"] if rnd else [win], writes=["wabs"],
                         tag=f"wx0{mat}{icc}_{sub}_{rnd}")
                    for lvl, st in ((1, 1), (2, 2)):
                        def ftree(st):
                            def f(v):
                                r = None
                                for a in range(0, 8, 2 * st):
                                    r = v.tensor_tensor(wabs[:, a, :], wabs[:, a, :],
                                                        wabs[:, a + st, :], op=ALU.max)
                                return r
                            return f
                        s.op("V", ftree(st), reads=["wabs"], writes=["wabs"],
                             tag=f"wx{lvl}{mat}{icc}_{sub}_{rnd}")
                    if rnd == 0:
                        s.op("V", lambda v: v.tensor_tensor(wmac[:, :], wabs[:, 0, :],
                                                            wabs[:, 4, :], op=ALU.max),
                             reads=["wabs"], writes=["wmac"], tag=f"wx3{mat}{icc}_{sub}_0")
                    else:
                        def flast(v):
                            v.tensor_tensor(wabs[:, 0, :], wabs[:, 0, :], wabs[:, 4, :], op=ALU.max)
                            return None
                        s.op("V", lambda v: v.tensor_tensor(wabs[:, 0, :], wabs[:, 0, :],
                                                            wabs[:, 4, :], op=ALU.max),
                             reads=["wabs"], writes=["wabs"], tag=f"wx3{mat}{icc}_{sub}_1")
                        s.op("V", lambda v: v.tensor_tensor(wmac[:, :], wmac[:, :],
                                                            wabs[:, 0, :], op=ALU.max),
                             reads=["wabs", "wmac"], writes=["wmac"], tag=f"wx4{mat}{icc}_{sub}")
                s.op("T", lambda t: t.transpose(psw[:, 0:128], wmac[:, :], eye32[:, :]),
                     reads=["wmac", "eye32"], writes=["psw"], tag=f"wt{mat}{icc}_{sub}")

                s.op("V", lambda v: v.reduce_max(wred[:, 0:1], psw[:, 0:128], axis=AX),
                     reads=["psw"], writes=["wred"], tag=f"wr{mat}{icc}_{sub}")
                s.op("V", lambda v: v.tensor_scalar(wred[:, 1:2], wred[:, 0:1], 1.0 / 448.0, None,
                                                    op0=ALU.mult),
                     reads=["wred"], writes=["wred"], tag=f"wr2{mat}{icc}_{sub}")
                s.op("T", lambda t: t.transpose(psw[0:1, 128:256], wred[:, 1:2], eye32[:, :]),
                     reads=["wred", "eye32"], writes=["psw"], tag=f"wfT{mat}{icc}_{sub}")
                s.op("V", (lambda sub: lambda v: v.tensor_copy(
                        wrow[0:1, sub * 128:(sub + 1) * 128], psw[0:1, 128:256]))(sub),
                     reads=["psw"], writes=["wrow"], tag=f"wfC{mat}{icc}_{sub}")

                s.op("V", (lambda sub: lambda v: v.reciprocal(
                        wrowt[0:1, 0:128], wrow[0:1, sub * 128:(sub + 1) * 128]))(sub),
                     reads=["wrow"], writes=["wrowt"], tag=f"wi{mat}{icc}_{sub}")
                s.op("V", lambda v: v.tensor_scalar(wrowt[0:1, 128:256], wrowt[0:1, 0:128], 0.5, None,
                                                    op0=ALU.mult),
                     reads=["wrowt"], writes=["wrowt"], tag=f"wi2{mat}{icc}_{sub}")

                def fbc(t):
                    t.matmul(psb[:, 0:128], ones32[0:1, 0:128], wrowt[0:1, 128:256], start=True, stop=True)
                    return t.matmul(psb[:, 128:256], ones32[0:1, 0:128],
                                    wrow[0:1, sub * 128:(sub + 1) * 128], start=True, stop=True)
                s.op("T", fbc, reads=["wrowt", "wrow", "ones32"], writes=["psb"], tag=f"wb{mat}{icc}_{sub}")

                sl = slot * 2 + ib

                def fev(v):
                    v.tensor_copy(wibc[:, slot, sub * 128:(sub + 1) * 128], psb[:, 0:128])
                    return v.tensor_copy(wsbc[:, sl, sub * 128:(sub + 1) * 128], psb[:, 128:256])
                s.op("V", fev, reads=["psb"], writes=["wibc", "wsbc"], tag=f"we{mat}{icc}_{sub}")

                def fq(wi, qw, sub, slot):
                    def f(v):
                        r = None
                        for a in range(16):
                            r = v.tensor_tensor(qw[:, a, sub * 128:(sub + 1) * 128], wi[:, a, :],
                                                wibc[:, slot, sub * 128:(sub + 1) * 128], op=ALU.mult)
                        return r
                    return f
                s.op("P", fq(wi, qw, sub, slot), reads=[win, "wibc"], writes=[qwn],
                     tag=f"wq{mat}{icc}_{sub}")

            def wprep_piece(icc2, j):
                ib2 = icc2 % 2
                sub, mat = j // 2, ("g", "u")[j % 2]
                if mat == "g":
                    off = GRAN_OFF[("wg", icc2)] + sub * 128
                    wprep_sub(icc2, sub, "g", wgi, "wgi", qwg[ib2], f"qwg{ib2}", 0, off)
                else:
                    off = GRAN_OFF[("wu", icc2)] + sub * 128
                    wprep_sub(icc2, sub, "u", wui, "wui", qwu[ib2], f"qwu{ib2}", 1, off)

            for j in range(8):
                wprep_piece(0, j)
            for icc in range(11):
                ib = icc % 2
                def tail(tc):
                    pt = tc % 3
                    i2 = tc % 2
                    s.op("V", (lambda ib, i2: lambda v: v.tensor_tensor(
                            gtmp[i2][:, :], gtmp[i2][:, :], wsbc[:, 2 + ib, :], op=ALU.mult))(ib, i2),
                         reads=[f"gtmp{i2}", "wsbc"], writes=[f"gtmp{i2}"], tag=f"ue{icc}_{tc}")
                    s.op("V", (lambda tc, pt, i2: lambda v: v.scalar_tensor_tensor(
                            tstg[i2][:, :], psu[pt][:, :], rxs[:, tc:tc + 1], gtmp[i2][:, :],
                            op0=ALU.mult, op1=ALU.mult))(tc, pt, i2),
                         reads=[f"psu{pt}", "rxs", f"gtmp{i2}"], writes=[f"tstg{i2}"],
                         tag=f"tm{icc}_{tc}")
                    s.op("V", (lambda tc, i2: lambda v: v.reduce_max(
                            mtmp[:, i2:i2 + 1], tstg[i2][:, :], axis=AX,
                            apply_absolute_value=True))(tc, i2),
                         reads=[f"tstg{i2}"], writes=["mtmp"], tag=f"mxa{icc}_{tc}")
                    s.op("V", (lambda tc, i2: lambda v: v.tensor_tensor(
                            m2s[:, tc:tc + 1], m2s[:, tc:tc + 1], mtmp[:, i2:i2 + 1], op=ALU.max))(tc, i2),
                         reads=["mtmp", "m2s"], writes=["m2s"], tag=f"mxb{icc}_{tc}")
                    s.op("G", (lambda icc, tc, i2: lambda g: g.dma_start(
                            out=t_sc.ap()[icc, :, tc, :], in_=tstg[i2][:, :]))(icc, tc, i2),
                         reads=[f"tstg{i2}"], writes=["t_sc"], tag=f"ts{icc}_{tc}")

                for tc in range(16):
                    pt = tc % 3
                    i2 = tc % 2
                    def fg(tc, ib, pt):
                        def f(t):
                            r = None
                            for a in range(8):
                                r = t.matmul(psg[pt][:, :], qx[:, 2 * a:2 * a + 2, tc * 128:(tc + 1) * 128],
                                             qwg[ib][:, 2 * a:2 * a + 2, :], start=(a == 0), stop=(a == 7),
                                             perf_mode=PM.DoubleRow)
                            return r
                        return f
                    s.op("T", fg(tc, ib, pt), reads=["qx0", "qx1", f"qwg{ib}"], writes=[f"psg{pt}"],
                         tag=f"gm{icc}_{tc}")

                    def fu(tc, ib, pt):
                        def f(t):
                            r = None
                            for a in range(8):
                                r = t.matmul(psu[pt][:, :], qx[:, 2 * a:2 * a + 2, tc * 128:(tc + 1) * 128],
                                             qwu[ib][:, 2 * a:2 * a + 2, :], start=(a == 0), stop=(a == 7),
                                             perf_mode=PM.DoubleRow)
                            return r
                        return f
                    s.op("T", fu(tc, ib, pt), reads=["qx0", "qx1", f"qwu{ib}"], writes=[f"psu{pt}"],
                         tag=f"um{icc}_{tc}")
                    s.op("V", (lambda tc, ib, pt, i2: lambda v: v.scalar_tensor_tensor(
                            gtmp[i2][:, :], psg[pt][:, :], rxs[:, tc:tc + 1], wsbc[:, ib, :],
                            op0=ALU.mult, op1=ALU.mult))(tc, ib, pt, i2),
                         reads=[f"psg{pt}", "rxs", "wsbc"], writes=[f"gtmp{i2}"], tag=f"ge{icc}_{tc}")
                    s.op("S", (lambda i2: lambda sc: sc.activation(
                            gtmp[i2][:, :], gtmp[i2][:, :], AF.Silu))(i2),
                         reads=[f"gtmp{i2}"], writes=[f"gtmp{i2}"], tag=f"gs{icc}_{tc}")
                    if tc >= 1:
                        tail(tc - 1)
                    if icc + 1 < 11 and tc % 2 == 1:
                        wprep_piece(icc + 1, tc // 2)
                tail(15)
            s.emit_phase(nc, ctx)
            gup.close()
            s.note_free()
            psw8 = [ph5b.enter_context(nc.psum_tensor(f"psw8_{i}", [128, 256], FP8)) for i in range(8)]
            for i in range(8):
                s.buf(f"psw8_{i}", xread=True)

            s.op("V", lambda v: v.reciprocal(tqs[:, :], m2s[:, :]),
                 reads=["m2s"], writes=["tqs"], tag="m2inv0")
            s.op("V", lambda v: v.tensor_scalar(tqs[:, :], tqs[:, :], 224.0, None, op0=ALU.mult),
                 reads=["tqs"], writes=["tqs"], tag="m2inv1")
            s.op("V", lambda v: v.tensor_scalar(m2f[:, :], m2s[:, :], 1.0 / 112.0, None, op0=ALU.mult),
                 reads=["m2s"], writes=["m2f"], tag="m2fold")

            tri = [ph5b.enter_context(nc.sbuf_tensor(f"tri{i}", [128, 8, 512], BF16)) for i in range(2)]
            for icc in range(11):
                for half in range(2):
                    i2 = half
                    ti, tn = tri[i2], f"tri{i2}"
                    s.op("Q", (lambda icc, half, ti: lambda g: g.dma_start(
                            out=ti[:, :, :], in_=t_sc.ap()[icc, :, half * 8:(half + 1) * 8, :]))(icc, half, ti),
                         reads=["t_sc"], writes=[tn], tag=f"tl{icc}_{half}")
                    for tcl in range(8):
                        tc = half * 8 + tcl
                        kt8 = (icc * 16 + tc) % 2
                        bs = 4 * kt8
                        s.op("S", (lambda ti, tcl, tc, kt8: lambda sc: sc.activation(
                                tq8[kt8][:, :], ti[:, tcl, :], AF.Identity,
                                scale=tqs[:, tc:tc + 1]))(ti, tcl, tc, kt8),
                             reads=[tn, "tqs"], writes=[f"tq8_{kt8}"], tag=f"tq{icc}_{tc}")
                        for j in range(4):
                            s.op("T", (lambda j, kt8, bs: lambda t: t.transpose(
                                    s2ap(psw8[bs + j]), tq8[kt8][:, j * 128:(j + 1) * 128],
                                    eye8[:, :]))(j, kt8, bs),
                                 reads=[f"tq8_{kt8}", "eye8"], writes=[f"psw8_{bs + j}"],
                                 tag=f"tt{icc}_{tc}_{j}")

                        def fte(icc, tc, bs):
                            def f(v):
                                r = None
                                for j in range(4):
                                    r = v.tensor_copy(qt[:, 4 * icc + j, tc * 128:(tc + 1) * 128],
                                                      s2ap(psw8[bs + j]))
                                return r
                            return f
                        s.op("V", fte(icc, tc, bs),
                             reads=[f"psw8_{bs + j}" for j in range(4)],
                             writes=["qt"], tag=f"te{icc}_{tc}")
        s.emit_phase(nc, ctx)
        ph5b.close()
        sc5.close()
        s.note_free()

        ph5c = ExitStack()
        if phases >= 7:
            wdi2 = [ph5c.enter_context(nc.sbuf_tensor("wdi2_0", [128, 11, 512], F32))]
            wdq = [ph5c.enter_context(nc.sbuf_tensor(f"wdq{i}", [128, 44, 512], FP8)) for i in range(2)]
            ibcd = [ph5c.enter_context(nc.sbuf_tensor(f"ibcd{i}", [128, 512], F32)) for i in range(2)]
            sbcd = [ph5c.enter_context(nc.sbuf_tensor(f"sbcd{i}", [128, 512], F32)) for i in range(2)]
            rowt = ph5c.enter_context(nc.sbuf_tensor("rowt", [1, 512], F32))
            x2r = [ph5c.enter_context(nc.sbuf_tensor(f"x2r{i}", [128, 512], F32)) for i in range(2)]
            outs = [ph5c.enter_context(nc.sbuf_tensor(f"outs{i}", [128, 512], F32)) for i in range(2)]
            outt = [ph5c.enter_context(nc.sbuf_tensor(f"outt{i}", [128, 512], F32)) for i in range(2)]
            psd = [ph5c.enter_context(nc.psum_tensor(f"psd{i}", [128, 512], F32)) for i in range(4)]
            psb2 = ph5c.enter_context(nc.psum_tensor("psb2", [128, 512], F32))
            for i in range(4):
                s.buf(f"psd{i}", xread=True)
            s.buf("psb2", xread=True)

            for fc in range(4):
                fcp = fc % 2
                s.op("V", (lambda fc: lambda v: v.reciprocal(
                        rowt[0:1, :], wsd_row[0:1, fc * 512:(fc + 1) * 512]))(fc),
                     reads=["wsd_row"], writes=["rowt"], tag=f"div{fc}")
                s.op("V", lambda v: v.tensor_scalar(rowt[0:1, :], rowt[0:1, :], 0.5, None, op0=ALU.mult),
                     reads=["rowt"], writes=["rowt"], tag=f"div2{fc}")
                s.op("T", lambda t: t.matmul(psb2[:, :], ones32[0:1, 0:128], rowt[0:1, :],
                                             start=True, stop=True),
                     reads=["rowt", "ones32"], writes=["psb2"], tag=f"dbc{fc}")
                s.op("V", (lambda fcp: lambda v: v.tensor_copy(ibcd[fcp][:, :], psb2[:, :]))(fcp),
                     reads=["psb2"], writes=[f"ibcd{fcp}"], tag=f"dbe{fc}")
                s.op("V", (lambda fc: lambda v: v.tensor_copy(
                        rowt[0:1, :], wsd_row[0:1, fc * 512:(fc + 1) * 512]))(fc),
                     reads=["wsd_row"], writes=["rowt"], tag=f"dsv{fc}")
                s.op("T", lambda t: t.matmul(psb2[:, :], ones32[0:1, 0:128], rowt[0:1, :],
                                             start=True, stop=True),
                     reads=["rowt", "ones32"], writes=["psb2"], tag=f"dbc2{fc}")
                s.op("V", (lambda fcp: lambda v: v.tensor_copy(sbcd[fcp][:, :], psb2[:, :]))(fcp),
                     reads=["psb2"], writes=[f"sbcd{fcp}"], tag=f"dbe2{fc}")
                for g4 in range(4):
                    gp = 0
                    off = GRAN_OFF[("wd", fc * 4 + g4)]
                    s.op("G", (lambda off: lambda g: pay_dma(
                            g, wdi2[0][:, :, :], off, [[512, 128], [128 * 512, 11], [1, 512]]))(off),
                         reads=["pay"], writes=["wdi2_0"], tag=f"dl{fc}_{g4}")

                    def fqd(g4, gp, fcp):
                        def f(v):
                            r = None
                            for j in range(11):
                                r = v.tensor_tensor(wdq[fcp][:, g4 * 11 + j, :], wdi2[gp][:, j, :],
                                                    ibcd[fcp][:, :], op=ALU.mult)
                            return r
                        return f
                    s.op("V", fqd(g4, gp, fcp), reads=[f"wdi2_{gp}", f"ibcd{fcp}"],
                         writes=[f"wdq{fcp}"], tag=f"dq{fc}_{g4}")
                for tc in range(16):
                    pi = tc % 4
                    i2 = tc % 2

                    def fdm(fc, tc, pi, fcp):
                        def f(t):
                            r = None
                            for icc in range(22):
                                r = t.matmul(psd[pi][:, :],
                                             qt[:, 2 * icc:2 * icc + 2, tc * 128:(tc + 1) * 128],
                                             wdq[fcp][:, 2 * icc:2 * icc + 2, :],
                                             start=(icc == 0), stop=(icc == 21),
                                             perf_mode=PM.DoubleRow)
                            return r
                        return f
                    s.op("T", fdm(fc, tc, pi, fcp), reads=["qt", f"wdq{fcp}"], writes=[f"psd{pi}"],
                         tag=f"dm{fc}_{tc}")
                    s.op("Q", (lambda fc, tc, i2: lambda g: g.dma_start(
                            out=x2r[i2][:, :],
                            in_=x2_sc.ap()[tc, :, fc * 512:(fc + 1) * 512]))(fc, tc, i2),
                         reads=["x2_sc"], writes=[f"x2r{i2}"], tag=f"dx{fc}_{tc}")

                    s.op("V", (lambda tc, pi, i2, fcp: lambda v: v.scalar_tensor_tensor(
                            outt[i2][:, :], psd[pi][:, :], m2f[:, tc:tc + 1],
                            sbcd[fcp][:, :], op0=ALU.mult, op1=ALU.mult))(tc, pi, i2, fcp),
                         reads=[f"psd{pi}", "m2f", f"sbcd{fcp}"], writes=[f"outt{i2}"], tag=f"deA{fc}_{tc}")
                    s.op("V", (lambda pi, i2: lambda v: v.tensor_tensor(
                            outs[i2][:, :], outt[i2][:, :], x2r[i2][:, :], op=ALU.add))(pi, i2),
                         reads=[f"outt{i2}", f"x2r{i2}"], writes=[f"outs{i2}"], tag=f"deB{fc}_{tc}")
                    s.op("G", (lambda fc, tc, i2: lambda g: g.dma_start(
                            out=out_d.ap()[tc * 128:(tc + 1) * 128, fc * 512:(fc + 1) * 512],
                            in_=outs[i2][:, :]))(fc, tc, i2),
                         reads=[f"outs{i2}"], writes=["outdone"], tag=f"do{fc}_{tc}")
            s.op("G", lambda g: g.nop(), reads=["outdone"], tag="final", barrier=True)
        s.emit_phase(nc, ctx)
        ph5c.close()
        sc5bc.close()
        s.note_free()

        s.emit_phase(nc, ctx)
    return nc


def _pack_payload(x, wqkvu, wout, wg_p, wu_p, wd):
    pay = np.zeros((8, S_SH), np.float32)
    flat = pay.reshape(-1)

    def put(key, arr):
        off = GRAN_OFF[key]
        flat[off:off + arr.size] = arr.ravel()

    wq = np.ascontiguousarray(wqkvu.T).reshape(16, 128, 16, 512).transpose(2, 1, 0, 3)
    wqk = np.ascontiguousarray(wq).reshape(1, N_WQK * GR_E)
    wo = np.ascontiguousarray(wout.T).reshape(16, 128, D).transpose(1, 0, 2)
    wo4 = wo.reshape(128, 16, 4, 512).transpose(2, 0, 1, 3)
    for i in range(4):
        put(("wout", i), wo4[i])
    for base, w in (("wg", wg_p), ("wu", wu_p)):
        wt = np.ascontiguousarray(w.T).reshape(16, 128, 11, 512).transpose(2, 1, 0, 3)
        for i in range(11):
            put((base, i), wt[i])
    wdt = np.ascontiguousarray(wd.T).reshape(44, 128, 4, 512).transpose(2, 0, 1, 3)
    for fc in range(4):
        for g4 in range(4):
            put(("wd", fc * 4 + g4), wdt[fc, 11 * g4:11 * (g4 + 1)])
    return pay, wqk


def _consts():
    arrq = np.broadcast_to(np.arange(1, 257, dtype=np.float32), (128, 256)).copy()
    p = np.arange(128, dtype=np.float32)
    kdc = np.stack([255.0 - p, 127.0 - p], axis=1).astype(np.float32)
    cm = np.empty((2, 128, 256), np.float32)
    for st in range(2):
        sidx = st * 128 + np.arange(128)[:, None]
        tidx = np.arange(256)[None, :]
        d = tidx - sidx
        cm[st] = np.where(d >= 0, d, 1.0e9).astype(np.float32)
    eye32 = np.eye(128, dtype=np.float32)
    eyebf = np.eye(128, dtype=ml_dtypes.bfloat16)
    eye8 = np.eye(128, dtype=ml_dtypes.float8_e4m3)
    onesbf = np.ones((128, 128), ml_dtypes.bfloat16)
    ones32 = np.ones((1, 128), np.float32)
    return arrq, kdc, cm, eye32, eyebf, eye8, onesbf, ones32


def kernel(L0_attn_x, L0_attn_slope_rate, L0_attn_W_qkvu, L0_attn_W_out,
           L0_ffn_w_gate, L0_ffn_w_up, L0_ffn_w_down,
           L0_ffn_up_reorder_index, L0_ffn_down_reorder_index):
    x = np.asarray(L0_attn_x, np.float32)
    wqkvu = np.asarray(L0_attn_W_qkvu, np.float32)
    wout = np.asarray(L0_attn_W_out, np.float32)
    didx = np.asarray(L0_ffn_down_reorder_index).astype(np.int64)
    wg_p = np.take(np.asarray(L0_ffn_w_gate, np.float32), didx, axis=0)
    wu_p = np.take(np.asarray(L0_ffn_w_up, np.float32), didx, axis=0)
    wd = np.asarray(L0_ffn_w_down, np.float32)
    slope = np.asarray(L0_attn_slope_rate, np.float32).reshape(1, 16)
    slope_b = np.broadcast_to(slope, (128, 16)).copy()

    pay, wqk = _pack_payload(x, wqkvu, wout, wg_p, wu_p, wd)
    arrq, kdc, cm, eye32, eyebf, eye8, onesbf, ones32 = _consts()

    nc = build_program()
    in_maps = []
    for c in range(8):
        seq, half = c // 2, c % 2
        x_own = np.ascontiguousarray(x[seq, half * T:(half + 1) * T], np.float32)
        x_prev = np.ascontiguousarray(x[seq, 0:T], np.float32)
        cpkv = np.zeros((128, 24), np.float32)
        cpkv[:, 0:2] = kdc
        cpkv[:, 2:18] = slope_b
        cpkv[:, 18] = 1.0 if half == 1 else 0.0
        cpkv[:, 19] = EPS1
        cpkv[:, 20] = EPS2
        in_maps.append({
            "shard": pay[c:c + 1], "wqk": wqk, "xin": x_own, "xpr": x_prev,
            "cpk": cpkv,
            "arrq": arrq, "cmask": cm,
            "eye32": eye32, "eyebf": eyebf, "eye8": eye8,
            "onesbf": onesbf, "ones32": ones32,
        })
    res = run_bass_kernel_spmd(nc, in_maps, core_ids=list(range(8)))
    out = np.empty((B, N, D), np.float32)
    for c in range(8):
        seq, half = c // 2, c % 2
        out[seq, half * T:(half + 1) * T] = res.results[c]["out"]
    return out


# version 12
# speedup vs baseline: 1.0099x; 1.0099x over previous
import os
os.environ["NEURON_SCRATCHPAD_PAGE_SIZE"] = "512"
import numpy as np
import ml_dtypes
from contextlib import ExitStack

import concourse.bass as bass
import concourse.mybir as mybir
from concourse.bass_utils import run_bass_kernel_spmd

F32 = mybir.dt.float32
BF16 = mybir.dt.bfloat16
FP8 = mybir.dt.float8e4
I32 = mybir.dt.int32
AF = mybir.ActivationFunctionType
ALU = mybir.AluOpType
AX = mybir.AxisListType.X
PM = mybir.MatmulPerfMode

B, N, D = 4, 4096, 2048
H, DH = 16, 128
BLK = 256
T = 2048
NB = T // BLK
IC = 5632
EPS1 = 1e-5
EPS2 = 1e-6

_x_own_elems = 0
GR_E = 128 * 16 * 512
N_WQK = 16
_GRAN = []
for _i in range(4):
    _GRAN.append(("wout", _i, GR_E))
for _i in range(11):
    _GRAN.append(("wg", _i, GR_E))
for _i in range(11):
    _GRAN.append(("wu", _i, GR_E))
for _i in range(16):
    _GRAN.append(("wd", _i, 44 * 128 * 128))
S_SH = 5_636_096
_big = [g for g in _GRAN if g[0] != "wd"]
_wdg = [g for g in _GRAN if g[0] == "wd"]
assert len(_big) == 26 and len(_wdg) == 16
_assign = []
_bi = 0
for _s in range(8):
    _nb = 4 if _s < 2 else 3
    _assign.append(_big[_bi:_bi + _nb] + _wdg[2 * _s:2 * _s + 2])
    _bi += _nb
GRAN_OFF = {}
for _s, _gl in enumerate(_assign):
    _off = _x_own_elems
    for _g in _gl:
        GRAN_OFF[(_g[0], _g[1])] = _s * S_SH + _off
        _off += _g[2]
    assert _off <= S_SH
PAY_TOT = 8 * S_SH


class _Buf:
    __slots__ = ("name", "last_write", "readers", "xread", "fence")

    def __init__(self, name, xread, fence=()):
        self.name = name
        self.last_write = None
        self.readers = []
        self.xread = xread
        self.fence = list(fence)


class _Op:
    __slots__ = ("engine", "emit", "inc_amount", "needs_inc", "cum", "deps",
                 "tag", "barrier", "phase")

    def __init__(self, engine, emit, inc_amount, tag, phase):
        self.engine = engine
        self.emit = emit
        self.inc_amount = inc_amount
        self.needs_inc = False
        self.cum = -1
        self.deps = []
        self.tag = tag
        self.barrier = False
        self.phase = phase


class Sched:
    SECTION = {"T": "tensor", "V": "vector", "S": "scalar", "Q": "sync",
               "G": "gpsimd", "C": "gpsimd", "P": "gpsimd", "R": "scalar"}

    def __init__(self):
        self.pending = {k: [] for k in self.SECTION.values()}
        self.bufs = {}
        self.phase = 0
        self.tails = {}
        self.free_fence = {}
        self.sems = None
        self.cum = {k: 0 for k in ("T", "V", "S", "Q", "G", "C", "P", "R")}
        self.waited = {sec: {k: 0 for k in ("T", "V", "S", "Q", "G", "C", "P", "R")}
                       for sec in set(self.SECTION.values())}
        self.prev_dma = {"Q": 0, "G": 0, "R": 0}

    def buf(self, name, xread=False):
        b = self.bufs.get(name)
        if b is None:
            b = self.bufs[name] = _Buf(name, xread, self.free_fence.values())
        if xread:
            b.xread = True
        return b

    def op(self, engine, emit, reads=(), writes=(), tag="", barrier=False):
        rd = [self.buf(b) if isinstance(b, str) else b for b in reads]
        wr = [self.buf(b) if isinstance(b, str) else b for b in writes]
        o = _Op(engine, emit, 16 if engine in "QGR" else 1, tag, self.phase)
        o.barrier = barrier
        if engine in "QGCR" and not barrier:
            o.needs_inc = True
        deps = []
        for b in rd:
            if b.last_write is not None:
                deps.append(b.last_write)
            if b.xread:
                for r in b.readers:
                    if r.engine != engine:
                        deps.append(r)
        for b in wr:
            deps.extend(b.readers)
            if b.last_write is not None:
                deps.append(b.last_write)
            elif not b.readers:
                deps.extend(b.fence)
        res = []
        for d in deps:
            if d is o:
                continue
            if d.phase == self.phase:
                if d.barrier:
                    raise RuntimeError(f"dep on barrier op {d.tag} <- {tag}")
                d.needs_inc = True
                res.append(d)
            elif d.needs_inc:
                res.append(d)
            else:
                res.extend(self.tails.values())
        seen = set()
        o.deps = [d for d in res if not (id(d) in seen or seen.add(id(d)))]
        for b in rd:
            b.readers.append(o)
        for b in wr:
            b.last_write = o
            b.readers = []
        self.pending[self.SECTION[engine]].append(o)
        return o

    def emit_phase(self, nc, ctx):
        if self.sems is None:
            self.sems = {k: ctx.enter_context(nc.semaphore(f"sem_{k}"))
                         for k in ("T", "V", "S", "Q", "G", "C", "P", "R")}
        for sec, ops in self.pending.items():
            if ops and not ops[-1].barrier:
                ops[-1].needs_inc = True
        for sec, ops in self.pending.items():
            for o in ops:
                if o.needs_inc:
                    self.cum[o.engine] += o.inc_amount
                    o.cum = self.cum[o.engine]
        sched = self

        def make_body(section):
            def body(eng):
                waited = sched.waited[section]
                for o in sched.pending[section]:
                    if o.engine in sched.prev_dma:
                        pd = sched.prev_dma[o.engine]
                        if pd > waited[o.engine]:
                            eng.wait_ge(sched.sems[o.engine], pd)
                            waited[o.engine] = pd
                    for d in o.deps:
                        if d.cum <= 0:
                            raise RuntimeError(f"dep no inc: {d.tag}<-{o.tag}")
                        if d.cum > waited[d.engine]:
                            eng.wait_ge(sched.sems[d.engine], d.cum)
                            waited[d.engine] = d.cum
                    inst = o.emit(eng)
                    if o.needs_inc:
                        if inst is None:
                            raise RuntimeError(f"{o.tag}: inc w/o inst")
                        inst.then_inc(sched.sems[o.engine], o.inc_amount)
                        if o.engine in sched.prev_dma:
                            sched.prev_dma[o.engine] = o.cum
            return body

        active = [sec for sec in ("sync", "gpsimd", "vector", "scalar", "tensor")
                  if self.pending[sec]]
        if active:
            with nc.Block() as blk:
                for sec in active:
                    getattr(blk, sec)(make_body(sec))
        for sec, ops in self.pending.items():
            if ops:
                for o in reversed(ops):
                    if o.needs_inc:
                        self.tails[o.engine] = o
                        break
        self.pending = {k: [] for k in self.SECTION.values()}
        self.phase += 1

    def note_free(self):
        self.free_fence = dict(self.tails)


def build_program(phases=99, probes=()):
    nc = bass.Bass("TRN2", target_bir_lowering=False)

    shard = nc.dram_tensor("shard", [1, S_SH], F32, kind="ExternalInput")
    wqk_d = nc.dram_tensor("wqk", [1, N_WQK * GR_E], F32, kind="ExternalInput")
    xin_d = nc.dram_tensor("xin", [T, D], F32, kind="ExternalInput")
    xpr_d = nc.dram_tensor("xpr", [T, D], F32, kind="ExternalInput")
    cpk_d = nc.dram_tensor("cpk", [128, 24], F32, kind="ExternalInput")
    arrq_d = nc.dram_tensor("arrq", [128, 256], F32, kind="ExternalInput")
    cmask_d = nc.dram_tensor("cmask", [2, 128, 256], F32, kind="ExternalInput")
    eye32_d = nc.dram_tensor("eye32", [128, 128], F32, kind="ExternalInput")
    eyebf_d = nc.dram_tensor("eyebf", [128, 128], BF16, kind="ExternalInput")
    eye8_d = nc.dram_tensor("eye8", [128, 128], FP8, kind="ExternalInput")
    onesbf_d = nc.dram_tensor("onesbf", [128, 128], BF16, kind="ExternalInput")
    ones32_d = nc.dram_tensor("ones32", [1, 128], F32, kind="ExternalInput")
    out_d = nc.dram_tensor("out", [T, D], F32, kind="ExternalOutput")

    shard_b = nc.dram_tensor("shard_b", [1, S_SH], F32)
    pay = nc.dram_tensor("pay", [8, S_SH], F32, addr_space="Shared")

    q_sc = nc.dram_tensor("q_sc", [H, 128, T], BF16)
    k_sc = nc.dram_tensor("k_sc", [H, 128, T], BF16)
    u_sc = nc.dram_tensor("u_sc", [H, 128, T], BF16)
    kp_sc = nc.dram_tensor("kp_sc", [H, 128, T], BF16)
    v_sc = nc.dram_tensor("v_sc", [4, 16, 128, 512], BF16)
    vp_sc = nc.dram_tensor("vp_sc", [4, 16, 128, 512], BF16)
    x2_sc = nc.dram_tensor("x2_sc", [16, 128, D], F32)
    t_sc = nc.dram_tensor("t_sc", [11, 128, 16, 512], BF16)

    s = Sched()

    with ExitStack() as ctx:
        e = ctx.enter_context

        cpk = e(nc.sbuf_tensor("cpk_s", [128, 24], F32))
        eye32 = e(nc.sbuf_tensor("eye32_s", [128, 128], F32))
        eyebf = e(nc.sbuf_tensor("eyebf_s", [128, 128], BF16))
        eye8 = e(nc.sbuf_tensor("eye8_s", [128, 128], FP8))
        onesbf = e(nc.sbuf_tensor("onesbf_s", [128, 128], BF16))
        ones32 = e(nc.sbuf_tensor("ones32_s", [1, 128], F32))
        mxs = e(nc.sbuf_tensor("mxs", [128, 16], F32))
        m2s = e(nc.sbuf_tensor("m2s", [128, 16], F32))
        m2f = e(nc.sbuf_tensor("m2f", [128, 16], F32))
        tqs = e(nc.sbuf_tensor("tqs", [128, 16], F32))
        rxs = e(nc.sbuf_tensor("rxs", [128, 16], F32))

        for nm, dst, src in (("cpk", cpk, cpk_d),
                             ("eye32", eye32, eye32_d),
                             ("eyebf", eyebf, eyebf_d), ("eye8", eye8, eye8_d),
                             ("onesbf", onesbf, onesbf_d), ("ones32", ones32, ones32_d)):
            s.op("Q", (lambda dst, src: lambda g: g.dma_start(out=dst[...], in_=src[...]))(dst, src),
                 writes=[nm], tag=f"ld_{nm}")

        s.op("G", lambda g: g.dma_start(out=shard_b[:, :], in_=shard[:, :]),
             writes=["shard_b"], tag="bounce")
        s.op("C", lambda g: g.collective_compute(
                "AllGather", ALU.bypass, replica_groups=[list(range(8))],
                ins=[shard_b.ap().opt()], outs=[pay.ap().opt()]),
             reads=["shard_b"], writes=["pay"], tag="ag")


        def pay_dma(eng, dst_ap, elem_off, dims, base=None):
            return eng.dma_start(out=dst_ap, in_=bass.AP(base if base is not None else pay,
                                                         elem_off, dims))

        def s2ap(guard, n=128):
            h = guard[0:1, 0:1].tensor
            return bass.AP(h, 0, [[2 * n, 128], [2, n]])

        sc12 = ExitStack()
        xl = sc12.enter_context(nc.sbuf_tensor("xl", [128, 16, T], BF16))
        xlp = sc12.enter_context(nc.sbuf_tensor("xlp", [128, 16, T], BF16))
        ph2 = ExitStack()
        wb2 = [ph2.enter_context(nc.sbuf_tensor(f"wb2_{i}", [128, 16, 512], BF16)) for i in range(2)]
        stg = [ph2.enter_context(nc.sbuf_tensor(f"stg{i}", [128, T], BF16)) for i in range(2)]
        vst = [ph2.enter_context(nc.sbuf_tensor(f"vst{i}", [128, 512], BF16)) for i in range(4)]
        wstg = ph2.enter_context(nc.sbuf_tensor("wstg", [128, 4, 512], F32))
        psp = [ph2.enter_context(nc.psum_tensor(f"psp{i}", [128, 512], F32)) for i in range(4)]
        s.buf("wstg")
        for i in range(2):
            s.buf(f"wb2_{i}")
            s.buf(f"stg{i}")
        for i in range(4):
            s.buf(f"vst{i}")
            s.buf(f"psp{i}", xread=True)
        ph1 = ExitStack()
        xin = [ph1.enter_context(nc.sbuf_tensor(f"xin{i}", [128, D], F32)) for i in range(2)]
        xlt = [ph1.enter_context(nc.sbuf_tensor(f"xlt{i}", [128, D], BF16)) for i in range(2)]
        st1 = [ph1.enter_context(nc.sbuf_tensor(f"st1_{i}", [128, 8], F32)) for i in range(2)]
        ps1 = [ph1.enter_context(nc.psum_tensor(f"ps1t_{i}", [128, 128], BF16)) for i in range(2)]
        s.buf("ps1t_0", xread=True)
        s.buf("ps1t_1", xread=True)

        def ln_chunk(which, tc, dest, destn):
            i2 = (which * 16 + tc) % 2
            xi, xt = xin[i2], xlt[i2]
            xn, tn = f"xin{i2}", f"xlt{i2}"
            st1_ = st1[i2]
            sn1 = f"st1_{i2}"
            tg = f"{which}_{tc}"
            xsrc = xin_d if which == 0 else xpr_d
            xsn = "xin_d" if which == 0 else "xpr_d"
            ldq = "R" if (which == 0 and tc % 2 == 1) or which == 1 else "Q"
            s.op(ldq, (lambda xi, tc, xsrc: lambda g: g.dma_start(
                    out=xi[:, :], in_=xsrc.ap()[tc * 128:(tc + 1) * 128, :]))(xi, tc, xsrc),
                 reads=[xsn], writes=[xn], tag=f"ldx{tg}")
            s.op("V", (lambda xi, st1_: lambda v: v.reduce_sum(st1_[:, 0:1], xi[:, :], axis=AX))(xi, st1_),
                 reads=[xn], writes=[sn1], tag=f"sum{tg}")
            s.op("S", (lambda xi, st1_, xt: lambda sc: sc.activation(
                    xt[:, :], xi[:, :], AF.Square, accum_out=st1_[:, 1:2]))(xi, st1_, xt),
                 reads=[xn], writes=[tn, sn1], tag=f"sq{tg}")

            def fstat(st1_):
                def f(v):
                    v.tensor_scalar(st1_[:, 2:3], st1_[:, 0:1], 1.0 / D, None, op0=ALU.mult)
                    return v.tensor_scalar(st1_[:, 3:4], st1_[:, 1:2], 1.0 / D, None, op0=ALU.mult)
                return f
            s.op("V", fstat(st1_), reads=[sn1], writes=[sn1], tag=f"st{tg}")
            s.op("V", (lambda st1_: lambda v: v.tensor_tensor(
                    st1_[:, 4:5], st1_[:, 2:3], st1_[:, 2:3], op=ALU.mult))(st1_),
                 reads=[sn1], writes=[sn1], tag=f"st2{tg}")
            s.op("V", (lambda st1_: lambda v: v.tensor_tensor(
                    st1_[:, 3:4], st1_[:, 3:4], st1_[:, 4:5], op=ALU.subtract))(st1_),
                 reads=[sn1], writes=[sn1], tag=f"st3{tg}")
            s.op("S", (lambda st1_: lambda sc: sc.activation(
                    st1_[:, 5:6], st1_[:, 3:4], AF.Sqrt, bias=cpk[:, 19:20]))(st1_),
                 reads=[sn1, "cpk"], writes=[sn1], tag=f"sqr{tg}")
            s.op("V", (lambda st1_: lambda v: v.reciprocal(st1_[:, 6:7], st1_[:, 5:6]))(st1_),
                 reads=[sn1], writes=[sn1], tag=f"rc{tg}")
            s.op("V", (lambda xi, xt, st1_: lambda v: v.tensor_scalar(
                    xt[:, :], xi[:, :], st1_[:, 2:3], st1_[:, 6:7],
                    op0=ALU.subtract, op1=ALU.mult))(xi, xt, st1_),
                 reads=[xn, sn1], writes=[tn], tag=f"nm{tg}")
            for a in range(16):
                ai = a % 2
                s.op("T", (lambda a, xt, ai: lambda t: t.transpose(
                        ps1[ai][:, :], xt[:, a * 128:(a + 1) * 128], eyebf[:, :]))(a, xt, ai),
                     reads=[tn, "eyebf"], writes=[f"ps1t_{ai}"], tag=f"tr{tg}_{a}")
                if ai == 0:
                    s.op("S", (lambda a, tc, dest, ai: lambda sc: sc.activation(
                            dest[:, a, tc * 128:(tc + 1) * 128], ps1[ai][:, :], AF.Identity))(a, tc, dest, ai),
                         reads=[f"ps1t_{ai}"], writes=[f"{destn}0"], tag=f"te{tg}_{a}")
                else:
                    s.op("V", (lambda a, tc, dest, ai: lambda v: v.tensor_copy(
                            dest[:, a, tc * 128:(tc + 1) * 128], ps1[ai][:, :]))(a, tc, dest, ai),
                         reads=[f"ps1t_{ai}"], writes=[f"{destn}1"], tag=f"te{tg}_{a}")

        CAST_SET = (0, 1, 2, 3, 4, 5, 6, 7, 12, 13, 14, 15)

        def emit_oc(oc):
                wb, wn = wb2[oc % 2], f"wb2_{oc%2}"
                if oc in CAST_SET:
                    for h4 in range(4):
                        s.op("Q", (lambda oc, h4: lambda g: g.dma_start(
                                out=wstg[:, :, :],
                                in_=bass.AP(wqk_d, oc * GR_E + h4 * 4 * 512,
                                            [[16 * 512, 128], [512, 4], [1, 512]])))(oc, h4),
                             reads=["wqk"], writes=["wstg"], tag=f"lws{oc}_{h4}")
                        s.op("V", (lambda wb, h4: lambda v: v.tensor_copy(
                                wb[:, 4 * h4:4 * h4 + 4, :], wstg[:, :, :]))(wb, h4),
                             reads=["wstg"], writes=[wn], tag=f"cst{oc}_{h4}")
                else:
                    s.op("G", (lambda oc, wb: lambda g: pay_dma(
                            g, wb[:, :, :], oc * GR_E, [[16 * 512, 128], [512, 16], [1, 512]],
                            base=wqk_d))(oc, wb),
                         reads=["wqk"], writes=[wn], tag=f"ldw{oc}")
                fam = ("q", "k", "v", "u")[oc // 4]
                if fam != "v":
                    srcs = [("o", xl, "xl")]
                    if fam == "k":
                        srcs.append(("p", xlp, "xlp"))
                    for tagp, xsrc, xsn in srcs:
                        for sub in range(4):
                            head = (oc % 4) * 4 + sub
                            sg, sn = stg[sub % 2], f"stg{sub%2}"
                            for tch in range(4):
                                pi = tch % 4

                                def fmm(wb, sub, xsrc, tch, pi):
                                    def f(t):
                                        r = None
                                        for a in range(16):
                                            r = t.matmul(psp[pi][:, :],
                                                         wb[:, a, sub * 128:(sub + 1) * 128],
                                                         xsrc[:, a, tch * 512:(tch + 1) * 512],
                                                         start=(a == 0), stop=(a == 15))
                                        return r
                                    return f
                                s.op("T", fmm(wb, sub, xsrc, tch, pi),
                                     reads=[wn, f"{xsn}0", f"{xsn}1"],
                                     writes=[f"psp{pi}"], tag=f"mm{oc}{tagp}_{sub}_{tch}")
                                func = AF.Silu if fam in ("q", "k") else AF.Identity
                                s.op("S", (lambda sg, tch, pi, func: lambda sc: sc.activation(
                                        sg[:, tch * 512:(tch + 1) * 512], psp[pi][:, :], func))(sg, tch, pi, func),
                                     reads=[f"psp{pi}"], writes=[sn], tag=f"ev{oc}{tagp}_{sub}_{tch}")
                            dst = {"q": q_sc, "k": k_sc, "u": u_sc}[fam]
                            dnm = fam + "_sc"
                            if tagp == "p":
                                dst, dnm = kp_sc, "kp_sc"
                            s.op("G", (lambda dst, head, sg: lambda g: g.dma_start(
                                    out=dst[head], in_=sg[:, :]))(dst, head, sg),
                                 reads=[sn], writes=[dnm], tag=f"sp{fam}{tagp}{head}")
                else:
                    vcc = oc - 8
                    for tagp, xsrc, xsn, dst, dnm in (("o", xl, "xl", v_sc, "v_sc"),
                                                      ("p", xlp, "xlp", vp_sc, "vp_sc")):
                        for tc in range(16):
                            pi = tc % 4

                            def fmmv(wb, tc, xsrc, pi):
                                def f(t):
                                    r = None
                                    for a in range(16):
                                        r = t.matmul(psp[pi][:, :],
                                                     xsrc[:, a, tc * 128:(tc + 1) * 128],
                                                     wb[:, a, :], start=(a == 0), stop=(a == 15))
                                    return r
                                return f
                            s.op("T", fmmv(wb, tc, xsrc, pi),
                                 reads=[wn, f"{xsn}0", f"{xsn}1"],
                                 writes=[f"psp{pi}"], tag=f"vm{oc}{tagp}_{tc}")
                            vs, vn = vst[tc % 4], f"vst{tc%4}"
                            s.op("S", (lambda vs, pi: lambda sc: sc.activation(
                                    vs[:, :], psp[pi][:, :], AF.Identity))(vs, pi),
                                 reads=[f"psp{pi}"], writes=[vn], tag=f"vev{oc}{tagp}_{tc}")
                            s.op("G", (lambda dst, vcc, tc, vs: lambda g: g.dma_start(
                                    out=dst[vcc, tc], in_=vs[:, :]))(dst, vcc, tc, vs),
                                 reads=[vn], writes=[dnm], tag=f"vsp{oc}{tagp}_{tc}")
        if phases >= 1:
            for tc in range(16):
                ln_chunk(0, tc, xl, "xl")
        if phases >= 2:
            for k, oc in enumerate((0, 1, 2, 3)):
                emit_oc(oc)
                if phases >= 1:
                    for j in range(4):
                        ln_chunk(1, 4 * k + j, xlp, "xlp")
            for oc in (4, 5, 6, 7, 12, 13, 14, 15, 8, 9, 10, 11):
                emit_oc(oc)
        elif phases >= 1:
            for tc in range(16):
                ln_chunk(1, tc, xlp, "xlp")
        s.emit_phase(nc, ctx)
        ph1.close()
        ph2.close()
        sc12.close()
        s.note_free()

        sc5bc = ExitStack()
        wsd_row = sc5bc.enter_context(nc.sbuf_tensor("wsd_row", [1, D], F32))
        sc34 = ExitStack()
        og = sc34.enter_context(nc.sbuf_tensor("og", [128, 16, T], BF16))
        ph3 = ExitStack()
        if phases >= 3:
            arrq = ph3.enter_context(nc.sbuf_tensor("arrq_s", [128, 256], F32))
            cmask = ph3.enter_context(nc.sbuf_tensor("cmask_s", [128, 2, 256], F32))
            s.op("Q", lambda g: g.dma_start(out=arrq[:, :], in_=arrq_d[:, :]),
                 writes=["arrq"], tag="ld_arrq")
            s.op("Q", lambda g: g.dma_start(out=cmask[:, :, :],
                                            in_=cmask_d.ap().rearrange("a p t -> p a t")),
                 writes=["cmask"], tag="ld_cmask")
            qdec = ph3.enter_context(nc.sbuf_tensor("qdec", [128, 16, 256], BF16))
            mask = ph3.enter_context(nc.sbuf_tensor("mask", [128, 16, 2, 256], BF16))
            kdec = ph3.enter_context(nc.sbuf_tensor("kdec", [128, 16, 2], F32))
            bdc = ph3.enter_context(nc.sbuf_tensor("bdc", [128, 16], F32))
            tmpd = ph3.enter_context(nc.sbuf_tensor("tmpd", [128, 2, 256], F32))
            kv = ph3.enter_context(nc.sbuf_tensor("kv", [128, 16, 128], F32))
            kvb = [ph3.enter_context(nc.sbuf_tensor(f"kvb{i}", [128, 128], BF16)) for i in range(2)]
            qb = [ph3.enter_context(nc.sbuf_tensor(f"qb{i}", [128, 16, 256], BF16)) for i in range(2)]
            kb = [ph3.enter_context(nc.sbuf_tensor(f"kb{i}", [128, 16, 256], BF16)) for i in range(2)]
            vb = [ph3.enter_context(nc.sbuf_tensor(f"vb{i}", [128, 2, 2048], BF16)) for i in range(2)]
            ub = [ph3.enter_context(nc.sbuf_tensor("ub0", [128, 16, 256], BF16))]
            kqm = [ph3.enter_context(nc.sbuf_tensor(f"kqm{i}", [128, 2, 256], BF16)) for i in range(2)]
            ktda = [ph3.enter_context(nc.sbuf_tensor(f"ktda{i}", [128, 2, 16, 128], BF16)) for i in range(2)]
            qdq = [ph3.enter_context(nc.sbuf_tensor(f"qdq{i}", [128, 256], BF16)) for i in range(2)]
            obf = ph3.enter_context(nc.sbuf_tensor("obf", [128, 16, 256], BF16))
            osq = ph3.enter_context(nc.sbuf_tensor("osq", [128, 16, 256], BF16))
            rsq = ph3.enter_context(nc.sbuf_tensor("rsq", [128, 256], F32))
            ogt = ph3.enter_context(nc.sbuf_tensor("ogt", [128, 2, 256], F32))
            ps_kq = [ph3.enter_context(nc.psum_tensor(f"ps_kq{i}", [128, 512], F32)) for i in range(2)]
            ps_o = [ph3.enter_context(nc.psum_tensor(f"ps_o{i}", [128, 256], F32)) for i in range(2)]
            ps_kv = ph3.enter_context(nc.psum_tensor("ps_kv", [128, 128], F32))
            ps_kt = [ph3.enter_context(nc.psum_tensor(f"ps_kt{i}", [128, 256], BF16)) for i in range(2)]
            ps_rms = ph3.enter_context(nc.psum_tensor("ps_rms", [128, 256], F32))
            for n in ("ps_kq0", "ps_kq1", "ps_o0", "ps_o1", "ps_kv", "ps_kt0", "ps_kt1", "ps_rms"):
                s.buf(n, xread=True)

            for h in range(16):
                s.op("V", (lambda h: lambda v: v.tensor_scalar(
                        tmpd[:, 0, :], arrq[:, :], cpk[:, 2 + h:3 + h], None, op0=ALU.mult))(h),
                     reads=["arrq", "cpk"], writes=["tmpd"], tag=f"qd0_{h}")
                s.op("S", (lambda h: lambda sc: sc.activation(
                        qdec[:, h, :], tmpd[:, 0, :], AF.Exp, scale=-1.0))(h),
                     reads=["tmpd"], writes=["qdec"], tag=f"qd1_{h}")

                def fm(h):
                    def g(v):
                        v.tensor_scalar(tmpd[:, 0, :], cmask[:, 0, :], cpk[:, 2 + h:3 + h], None, op0=ALU.mult)
                        return v.tensor_scalar(tmpd[:, 1, :], cmask[:, 1, :], cpk[:, 2 + h:3 + h], None, op0=ALU.mult)
                    return g
                s.op("V", fm(h), reads=["cmask", "cpk"], writes=["tmpd"], tag=f"mk0_{h}")

                def fe(h):
                    def g(sc):
                        sc.activation(mask[:, h, 0, :], tmpd[:, 0, :], AF.Exp, scale=-1.0)
                        return sc.activation(mask[:, h, 1, :], tmpd[:, 1, :], AF.Exp, scale=-1.0)
                    return g
                s.op("S", fe(h), reads=["tmpd"], writes=["mask"], tag=f"mk1_{h}")
                s.op("V", (lambda h: lambda v: v.tensor_scalar(
                        tmpd[:, 0, 0:2], cpk[:, 0:2], cpk[:, 2 + h:3 + h], None, op0=ALU.mult))(h),
                     reads=["cpk"], writes=["tmpd"], tag=f"kd0_{h}")
                s.op("S", (lambda h: lambda sc: sc.activation(
                        kdec[:, h, :], tmpd[:, 0, 0:2], AF.Exp, scale=-1.0))(h),
                     reads=["tmpd"], writes=["kdec"], tag=f"kd1_{h}")
            s.op("S", lambda sc: sc.activation(bdc[:, :], cpk[:, 2:18], AF.Exp, scale=-256.0),
                 reads=["cpk"], writes=["bdc"], tag="bd")
            s.op("V", lambda v: v.memset(kv[:, :, :], 0.0),
                 writes=[f"kv_h{h}" for h in range(16)], tag="kv0")

            def load_blk(b, par, pre):
                ksrc, kn = (kp_sc, "kp_sc") if pre else (k_sc, "k_sc")
                vsrc, vn = (vp_sc, "vp_sc") if pre else (v_sc, "v_sc")
                if not pre:
                    s.op("Q", (lambda b, par: lambda g: g.dma_start(
                            out=qb[par][:, :, :],
                            in_=q_sc.ap()[:, :, b * 256:(b + 1) * 256].rearrange("h p t -> p h t")))(b, par),
                         reads=["q_sc"], writes=[f"qb{par}"], tag=f"lq{b}")
                    s.op("Q", (lambda b: lambda g: g.dma_start(
                            out=ub[0][:, :, :],
                            in_=u_sc.ap()[:, :, b * 256:(b + 1) * 256].rearrange("h p t -> p h t")))(b),
                         reads=["u_sc"], writes=["ub0"], tag=f"lu{b}")
                s.op("Q", (lambda b, par, ksrc: lambda g: g.dma_start(
                        out=kb[par][:, :, :],
                        in_=ksrc.ap()[:, :, b * 256:(b + 1) * 256].rearrange("h p t -> p h t")))(b, par, ksrc),
                     reads=[kn], writes=[f"kb{par}"], tag=f"lk{pre}{b}")
                for vcc in range(4):
                    s.op("Q", (lambda b, par, vcc, vsrc: lambda g: g.dma_start(
                            out=vb[par][:, :, vcc * 512:(vcc + 1) * 512],
                            in_=vsrc.ap()[vcc, 2 * b:2 * b + 2, :, :].rearrange("c p n -> p c n")))(b, par, vcc, vsrc),
                         reads=[vn], writes=[f"vb{par}"], tag=f"lv{pre}{b}_{vcc}")

            def ktrans_blk(b, par, pre):
                kn = "kp_sc" if pre else "k_sc"
                for h in range(16):
                    pk = ps_kt[(2 * b + h) % 2]
                    pkn = f"ps_kt{(2 * b + h) % 2}"

                    def ftr(h, par, pk):
                        def f(t):
                            t.transpose(pk[:, 0:128], kb[par][:, h, 0:128], eyebf[:, :])
                            return t.transpose(pk[:, 128:256], kb[par][:, h, 128:256], eyebf[:, :])
                        return f
                    s.op("T", ftr(h, par, pk), reads=[f"kb{par}", "eyebf"], writes=[pkn],
                         tag=f"kt{pre}{b}_{h}")

                    def fkd(h, par, pk):
                        def f(sc):
                            sc.activation(ktda[par][:, 0, h, :], pk[:, 0:128], AF.Identity,
                                          scale=kdec[:, h, 0:1])
                            return sc.activation(ktda[par][:, 1, h, :], pk[:, 128:256], AF.Identity,
                                                 scale=kdec[:, h, 1:2])
                        return f
                    s.op("S", fkd(h, par, pk), reads=[pkn, "kdec"], writes=[f"ktda{par}_h{h}"],
                         tag=f"kd{pre}{b}_{h}")

            def km_op(h, b, par, pre, pk, pkn):
                def fkv(h, par, pk):
                    def f(t):
                        t.matmul(pk[:, :], ktda[par][:, 0, h, :], vb[par][:, 0, h * 128:(h + 1) * 128],
                                 start=True, stop=False)
                        return t.matmul(pk[:, :], ktda[par][:, 1, h, :], vb[par][:, 1, h * 128:(h + 1) * 128],
                                        start=False, stop=True)
                    return f
                s.op("T", fkv(h, par, pk), reads=[f"ktda{par}_h{h}", f"vb{par}"], writes=[pkn],
                     tag=f"km{pre}{b}_{h}")

            def ku_op(h, b, pre, pk, pkn):
                s.op("V", (lambda h, pk: lambda v: v.scalar_tensor_tensor(
                        kv[:, h, :], kv[:, h, :], bdc[:, h:h + 1], pk[:, :],
                        op0=ALU.mult, op1=ALU.add))(h, pk),
                     reads=[pkn, "bdc", f"kv_h{h}"], writes=[f"kv_h{h}"], tag=f"ku{pre}{b}_{h}")

            pre_pk = lambda h: (ps_kq[h % 2][:, 0:128], f"ps_kq{h % 2}")
            for b in range(NB):
                load_blk(b, b % 2, True)
                ktrans_blk(b, b % 2, True)
                for h in range(16):
                    if h >= 1:
                        ku_op(h - 1, b, True, *pre_pk(h - 1))
                    km_op(h, b, b % 2, True, *pre_pk(h))
                ku_op(15, b, True, *pre_pk(15))
            s.op("V", lambda v: v.tensor_scalar(kv[:, :, :], kv[:, :, :], cpk[:, 18:19], None, op0=ALU.mult),
                 reads=["cpk"] + [f"kv_h{h}" for h in range(16)],
                 writes=[f"kv_h{h}" for h in range(16)], tag="kvflag")

            def kq_mm(h, b, par):
                i = h % 2
                def fkq(h, par, i):
                    def f(t):
                        t.matmul(ps_kq[i][:, 0:256], kb[par][:, h, 0:128], qb[par][:, h, :],
                                 start=True, stop=True)
                        return t.matmul(ps_kq[i][:, 256:512], kb[par][:, h, 128:256], qb[par][:, h, :],
                                        start=True, stop=True)
                    return f
                s.op("T", fkq(h, par, i), reads=[f"kb{par}", f"qb{par}"],
                     writes=[f"ps_kq{i}"], tag=f"kq{b}_{h}")

            def mkq_qdq(h, b, par):
                i = h % 2
                def fmk(h, i):
                    def f(v):
                        v.tensor_tensor(kqm[i][:, 0, :], ps_kq[i][:, 0:256], mask[:, h, 0, :], op=ALU.mult)
                        return v.tensor_tensor(kqm[i][:, 1, :], ps_kq[i][:, 256:512], mask[:, h, 1, :], op=ALU.mult)
                    return f
                s.op("V", fmk(h, i), reads=[f"ps_kq{i}", "mask"], writes=[f"kqm{i}"], tag=f"mq{b}_{h}")
                s.op("P", (lambda h, par, i: lambda g: g.tensor_tensor(
                        qdq[i][:, :], qb[par][:, h, :], qdec[:, h, :], op=ALU.mult))(h, par, i),
                     reads=[f"qb{par}", "qdec"], writes=[f"qdq{i}"], tag=f"qq{b}_{h}")

            def kc_op(h, b):
                i = h % 2
                s.op("S", (lambda h, i: lambda sc: sc.activation(
                        kvb[i][:, :], kv[:, h, :], AF.Identity))(h, i),
                     reads=[f"kv_h{h}"], writes=[f"kvb{i}"], tag=f"kc{b}_{h}")

            def o_mm(h, b, par):
                i = h % 2
                def fo(h, par, i):
                    def f(t):
                        t.matmul(ps_o[i][:, 0:256], kvb[i][:, :], qdq[i][:, :], start=True, stop=False)
                        t.matmul(ps_o[i][:, 0:256], vb[par][:, 0, h * 128:(h + 1) * 128],
                                 kqm[i][:, 0, :], start=False, stop=False)
                        return t.matmul(ps_o[i][:, 0:256], vb[par][:, 1, h * 128:(h + 1) * 128],
                                        kqm[i][:, 1, :], start=False, stop=True)
                    return f
                s.op("T", fo(h, par, i), reads=[f"kvb{i}", f"qdq{i}", f"vb{par}", f"kqm{i}"],
                     writes=[f"ps_o{i}"], tag=f"om{b}_{h}")

            def o_evac(h, b):
                i = h % 2
                s.op("S", (lambda h, i: lambda sc: sc.activation(obf[:, h, :], ps_o[i][:, 0:256],
                                                                 AF.Identity))(h, i),
                     reads=[f"ps_o{i}"], writes=[f"obf_h{h}"], tag=f"oe{b}_{h}")
                s.op("S", (lambda h, i: lambda sc: sc.activation(osq[:, h, :], ps_o[i][:, 0:256],
                                                                 AF.Square))(h, i),
                     reads=[f"ps_o{i}"], writes=[f"osq_h{h}"], tag=f"os{b}_{h}")

            for b in range(NB):
                par = b % 2
                load_blk(b, par, False)
                ktrans_blk(b, par, False)
                kq_mm(0, b, par)
                kc_op(0, b)
                for h in range(16):
                    mkq_qdq(h, b, par)
                    if h + 1 < 16:
                        kq_mm(h + 1, b, par)
                        kc_op(h + 1, b)
                    o_mm(h, b, par)
                    o_evac(h, b)
                    if h >= 1:
                        ku_op(h - 1, b, False, ps_kv, "ps_kv")
                    km_op(h, b, par, False, ps_kv, "ps_kv")
                ku_op(15, b, False, ps_kv, "ps_kv")

                def frms(t):
                    r = None
                    for ct in range(16):
                        r = t.matmul(ps_rms[:, :], onesbf[:, :], osq[:, ct, :],
                                     start=(ct == 0), stop=(ct == 15))
                    return r
                s.op("T", frms, reads=["onesbf"] + [f"osq_h{h}" for h in range(16)],
                     writes=["ps_rms"], tag=f"rm{b}")
                s.op("S", lambda sc: sc.activation(rsq[:, :], ps_rms[:, :], AF.Sqrt,
                                                   scale=1.0 / D, bias=cpk[:, 20:21]),
                     reads=["ps_rms", "cpk"], writes=["rsq"], tag=f"rs{b}")
                s.op("V", lambda v: v.reciprocal(rsq[:, :], rsq[:, :]),
                     reads=["rsq"], writes=["rsq"], tag=f"rr{b}")
                for ct in range(16):
                    s.op("V", (lambda ct, b: lambda v: v.tensor_tensor(
                            ogt[:, ct % 2, :], obf[:, ct, :], rsq[:, :], op=ALU.mult))(ct, b),
                         reads=[f"obf_h{ct}", "rsq"], writes=["ogt"], tag=f"ogA{b}_{ct}")
                    s.op("V", (lambda ct, b: lambda v: v.tensor_tensor(
                            og[:, ct, b * 256:(b + 1) * 256], ogt[:, ct % 2, :],
                            ub[0][:, ct, :], op=ALU.mult))(ct, b),
                         reads=["ogt", "ub0"], writes=["og"], tag=f"ogB{b}_{ct}")
        s.emit_phase(nc, ctx)
        ph3.close()
        s.note_free()

        ph4 = ExitStack()
        if phases >= 4:
            wo = ph4.enter_context(nc.sbuf_tensor("wo", [128, 16, D], BF16))
            x2s = [ph4.enter_context(nc.sbuf_tensor(f"x2s{i}", [128, D], F32)) for i in range(2)]
            ps4 = [ph4.enter_context(nc.psum_tensor(f"ps4_{i}", [128, 512], F32)) for i in range(4)]
            for i in range(4):
                s.buf(f"ps4_{i}", xread=True)
            for wi_ in range(4):
                s.op("G", (lambda wi_: lambda g: pay_dma(
                        g, wo[:, :, wi_ * 512:(wi_ + 1) * 512], GRAN_OFF[("wout", wi_)],
                        [[16 * 512, 128], [512, 16], [1, 512]]))(wi_),
                     reads=["pay"], writes=["wo"], tag=f"ldwo{wi_}")
            for tc in range(16):
                i2 = tc % 2
                s.op("Q", (lambda tc, i2: lambda g: g.dma_start(
                        out=x2s[i2][:, :], in_=xin_d.ap()[tc * 128:(tc + 1) * 128, :]))(tc, i2),
                     reads=["xin_d"], writes=[f"x2s{i2}"], tag=f"lr{tc}")
                for fc in range(4):
                    pi = fc

                    def fmm(tc, fc, pi):
                        def f(t):
                            r = None
                            for cc in range(16):
                                r = t.matmul(ps4[pi][:, :], og[:, cc, tc * 128:(tc + 1) * 128],
                                             wo[:, cc, fc * 512:(fc + 1) * 512],
                                             start=(cc == 0), stop=(cc == 15))
                            return r
                        return f
                    s.op("T", fmm(tc, fc, pi), reads=["og", "wo"], writes=[f"ps4_{pi}"],
                         tag=f"o2_{tc}_{fc}")
                    s.op("V", (lambda tc, fc, pi, i2: lambda v: v.tensor_tensor(
                            x2s[i2][:, fc * 512:(fc + 1) * 512], ps4[pi][:, :],
                            x2s[i2][:, fc * 512:(fc + 1) * 512], op=ALU.add))(tc, fc, pi, i2),
                         reads=[f"ps4_{pi}", f"x2s{i2}"], writes=[f"x2s{i2}"], tag=f"x2_{tc}_{fc}")
                s.op("Q", (lambda tc, i2: lambda g: g.dma_start(
                        out=x2_sc[tc], in_=x2s[i2][:, :]))(tc, i2),
                     reads=[f"x2s{i2}"], writes=["x2_sc"], tag=f"sx2{tc}")
            wdm = ph4.enter_context(nc.sbuf_tensor("wdm", [128, 512], F32))
            wdm2 = ph4.enter_context(nc.sbuf_tensor("wdm2", [128, 512], F32))
            wdi = [ph4.enter_context(nc.sbuf_tensor("wdi0", [128, 11, 512], F32))]
            ps4w = ph4.enter_context(nc.psum_tensor("ps4w", [128, 512], F32))
            s.buf("ps4w", xread=True)
            for fc in range(4):
                for g4 in range(4):
                    off = GRAN_OFF[("wd", fc * 4 + g4)]
                    s.op("G", (lambda off: lambda g: pay_dma(
                            g, wdi[0][:, :, :], off, [[512, 128], [128 * 512, 11], [1, 512]]))(off),
                         reads=["pay"], writes=["wdi0"], tag=f"wd{fc}_{g4}")

                    for j in range(11):
                        if g4 == 0 and j == 0:
                            s.op("V", lambda v: v.scalar_tensor_tensor(
                                    wdm[:, :], wdi[0][:, 0, :], -1.0, wdi[0][:, 0, :],
                                    op0=ALU.mult, op1=ALU.max),
                                 reads=["wdi0"], writes=["wdm"], tag=f"wm{fc}_{g4}_0")
                        else:
                            s.op("V", (lambda j: lambda v: v.scalar_tensor_tensor(
                                    wdm2[:, :], wdi[0][:, j, :], -1.0, wdi[0][:, j, :],
                                    op0=ALU.mult, op1=ALU.max))(j),
                                 reads=["wdi0"], writes=["wdm2"], tag=f"wmA{fc}_{g4}_{j}")
                            s.op("V", lambda v: v.tensor_tensor(
                                    wdm[:, :], wdm[:, :], wdm2[:, :], op=ALU.max),
                                 reads=["wdm", "wdm2"], writes=["wdm"], tag=f"wmB{fc}_{g4}_{j}")

                def ftr5(t):
                    r = None
                    for j in range(4):
                        r = t.transpose(ps4w[:, j * 128:(j + 1) * 128],
                                        wdm[:, j * 128:(j + 1) * 128], eye32[:, :])
                    return r
                s.op("T", ftr5, reads=["wdm", "eye32"], writes=["ps4w"], tag=f"wt{fc}")

                def fred(v):
                    r = None
                    for j in range(4):
                        r = v.reduce_max(wdm[:, j:j + 1], ps4w[:, j * 128:(j + 1) * 128], axis=AX)
                    return r
                s.op("V", fred, reads=["ps4w"], writes=["wdm"], tag=f"wr{fc}")
                s.op("V", lambda v: v.tensor_scalar(wdm[:, 0:4], wdm[:, 0:4], 1.0 / 448.0, None,
                                                    op0=ALU.mult),
                     reads=["wdm"], writes=["wdm"], tag=f"ws{fc}")
                for jj in range(4):
                    s.op("T", (lambda jj: lambda t: t.transpose(
                            ps4w[0:1, 256:384], wdm[:, jj:jj + 1], eye32[:, :]))(jj),
                         reads=["wdm", "eye32"], writes=["ps4w"], tag=f"wfT{fc}_{jj}")
                    s.op("V", (lambda fc, jj: lambda v: v.tensor_copy(
                            wsd_row[0:1, fc * 512 + jj * 128:fc * 512 + (jj + 1) * 128],
                            ps4w[0:1, 256:384]))(fc, jj),
                         reads=["ps4w"], writes=["wsd_row"], tag=f"wfC{fc}_{jj}")


        s.emit_phase(nc, ctx)
        ph4.close()
        sc34.close()
        s.note_free()

        qt = sc5bc.enter_context(nc.sbuf_tensor("qt", [128, 44, T], FP8))
        sc5 = ExitStack()
        qx = sc5.enter_context(nc.sbuf_tensor("qx", [128, 16, T], FP8))
        ph5a = ExitStack()
        if phases >= 5:
            x2i = [ph5a.enter_context(nc.sbuf_tensor(f"x2i{i}", [128, D], F32)) for i in range(3)]
            xl2 = [ph5a.enter_context(nc.sbuf_tensor(f"xl2_{i}", [128, D], F32)) for i in range(3)]
            q8t = [ph5a.enter_context(nc.sbuf_tensor(f"q8t_{i}", [128, D], FP8)) for i in range(3)]
            st5 = [ph5a.enter_context(nc.sbuf_tensor(f"st5_{i}", [128, 8], F32)) for i in range(3)]
            ps5 = ph5a.enter_context(nc.psum_tensor("ps5", [128, 512], F32))
            ps5q = [ph5a.enter_context(nc.psum_tensor(f"ps5q_{i}", [128, 256], FP8)) for i in range(2)]
            s.buf("ps5", xread=True)
            s.buf("ps5q_0", xread=True)
            s.buf("ps5q_1", xread=True)

            for tc in range(16):
                i2 = tc % 3
                xi, xn = x2i[i2], f"x2i{i2}"
                sv, svn = st5[i2], f"st5_{i2}"
                xlv, xlvn = xl2[i2], f"xl2_{i2}"
                qv, qvn = q8t[i2], f"q8t_{i2}"
                s.op("Q", (lambda tc, xi: lambda g: g.dma_start(
                        out=xi[:, :], in_=x2_sc[tc]))(tc, xi),
                     reads=["x2_sc"], writes=[xn], tag=f"l2{tc}")
                s.op("V", (lambda xi, sv: lambda v: v.reduce_sum(sv[:, 0:1], xi[:, :], axis=AX))(xi, sv),
                     reads=[xn], writes=[svn], tag=f"a{tc}")
                s.op("S", (lambda xi, sv, xlv: lambda sc: sc.activation(
                        xlv[:, :], xi[:, :], AF.Square, accum_out=sv[:, 1:2]))(xi, sv, xlv),
                     reads=[xn], writes=[xlvn, svn], tag=f"b{tc}")

                def fst(sv):
                    def f(v):
                        v.tensor_scalar(sv[:, 2:3], sv[:, 0:1], 1.0 / D, None, op0=ALU.mult)
                        return v.tensor_scalar(sv[:, 3:4], sv[:, 1:2], 1.0 / D, None, op0=ALU.mult)
                    return f
                s.op("V", fst(sv), reads=[svn], writes=[svn], tag=f"c{tc}")
                s.op("V", (lambda sv: lambda v: v.tensor_tensor(
                        sv[:, 4:5], sv[:, 2:3], sv[:, 2:3], op=ALU.mult))(sv),
                     reads=[svn], writes=[svn], tag=f"c2{tc}")
                s.op("V", (lambda sv: lambda v: v.tensor_tensor(
                        sv[:, 3:4], sv[:, 3:4], sv[:, 4:5], op=ALU.subtract))(sv),
                     reads=[svn], writes=[svn], tag=f"c3{tc}")
                s.op("S", (lambda sv: lambda sc: sc.activation(
                        sv[:, 5:6], sv[:, 3:4], AF.Sqrt, bias=cpk[:, 19:20]))(sv),
                     reads=[svn, "cpk"], writes=[svn], tag=f"d{tc}")
                s.op("V", (lambda sv: lambda v: v.reciprocal(sv[:, 6:7], sv[:, 5:6]))(sv),
                     reads=[svn], writes=[svn], tag=f"e{tc}")
                s.op("V", (lambda xi, sv, xlv: lambda v: v.tensor_scalar(
                        xlv[:, :], xi[:, :], sv[:, 2:3], sv[:, 6:7],
                        op0=ALU.subtract, op1=ALU.mult))(xi, sv, xlv),
                     reads=[xn, svn], writes=[xlvn], tag=f"f{tc}")
                s.op("V", (lambda sv, xlv: lambda v: v.reduce_max(
                        sv[:, 0:1], xlv[:, :], axis=AX, apply_absolute_value=True))(sv, xlv),
                     reads=[xlvn], writes=[svn], tag=f"g{tc}")

                s.op("V", (lambda sv: lambda v: v.reciprocal(sv[:, 1:2], sv[:, 0:1]))(sv),
                     reads=[svn], writes=[svn], tag=f"h{tc}")
                s.op("V", (lambda sv: lambda v: v.tensor_scalar(
                        sv[:, 1:2], sv[:, 1:2], 224.0, None, op0=ALU.mult))(sv),
                     reads=[svn], writes=[svn], tag=f"h2{tc}")
                s.op("S", (lambda sv, xlv, qv: lambda sc: sc.activation(
                        qv[:, :], xlv[:, :], AF.Identity, scale=sv[:, 1:2]))(sv, xlv, qv),
                     reads=[xlvn, svn], writes=[qvn], tag=f"i{tc}")
                s.op("V", (lambda tc, sv: lambda v: v.tensor_copy(mxs[:, tc:tc + 1], sv[:, 0:1]))(tc, sv),
                     reads=[svn], writes=["mxs"], tag=f"j{tc}")
                for a in range(16):
                    ai = a % 2
                    s.op("T", (lambda a, qv, ai: lambda t: t.transpose(
                            s2ap(ps5q[ai]), qv[:, a * 128:(a + 1) * 128], eye8[:, :]))(a, qv, ai),
                         reads=[qvn, "eye8"], writes=[f"ps5q_{ai}"], tag=f"k{tc}_{a}")
                    if ai == 0:
                        s.op("S", (lambda a, tc, ai: lambda sc: sc.activation(
                                qx[:, a, tc * 128:(tc + 1) * 128], s2ap(ps5q[ai]), AF.Identity))(a, tc, ai),
                             reads=[f"ps5q_{ai}"], writes=["qx0"], tag=f"l{tc}_{a}")
                    else:
                        s.op("V", (lambda a, tc, ai: lambda v: v.tensor_copy(
                                qx[:, a, tc * 128:(tc + 1) * 128], s2ap(ps5q[ai])))(a, tc, ai),
                             reads=[f"ps5q_{ai}"], writes=["qx1"], tag=f"l{tc}_{a}")
        s.emit_phase(nc, ctx)
        ph5a.close()
        s.note_free()

        ph5b = ExitStack()
        if phases >= 6:
            wmac = [ph5b.enter_context(nc.sbuf_tensor(f"wmac{i}", [128, 128], F32)) for i in range(2)]
            wabs = ph5b.enter_context(nc.sbuf_tensor("wabs", [128, 8, 128], F32))
            wred = ph5b.enter_context(nc.sbuf_tensor("wred", [128, 2], F32))
            wrow = ph5b.enter_context(nc.sbuf_tensor("wrow", [1, 512], F32))
            wibc = ph5b.enter_context(nc.sbuf_tensor("wibc", [128, 2, 512], F32))
            wsbc = ph5b.enter_context(nc.sbuf_tensor("wsbc", [128, 4, 512], F32))
            gtmp = [ph5b.enter_context(nc.sbuf_tensor(f"gtmp{i}", [128, 512], F32)) for i in range(2)]
            tstg = [ph5b.enter_context(nc.sbuf_tensor(f"tstg{i}", [128, 512], F32)) for i in range(2)]
            tq8 = [ph5b.enter_context(nc.sbuf_tensor(f"tq8_{i}", [128, 512], FP8))
                   for i in range(2)]
            mtmp = ph5b.enter_context(nc.sbuf_tensor("mtmp", [128, 2], F32))
            gup = ExitStack()
            wgi = gup.enter_context(nc.sbuf_tensor("wgi", [128, 16, 128], F32))
            wui = gup.enter_context(nc.sbuf_tensor("wui", [128, 16, 128], F32))
            qwg = [gup.enter_context(nc.sbuf_tensor(f"qwg{i}", [128, 16, 512], FP8)) for i in range(2)]
            qwu = [gup.enter_context(nc.sbuf_tensor(f"qwu{i}", [128, 16, 512], FP8)) for i in range(2)]
            wrowt = gup.enter_context(nc.sbuf_tensor("wrowt", [1, 256], F32))
            psg = [gup.enter_context(nc.psum_tensor(f"psg{i}", [128, 512], F32)) for i in range(3)]
            psu = [gup.enter_context(nc.psum_tensor(f"psu{i}", [128, 512], F32)) for i in range(3)]
            psb = gup.enter_context(nc.psum_tensor("psb", [128, 512], F32))
            psw = gup.enter_context(nc.psum_tensor("psw", [128, 512], F32))
            for n in ("psg0", "psg1", "psg2", "psu0", "psu1", "psu2", "psb", "psw"):
                s.buf(n, xread=True)

            s.op("V", lambda v: v.memset(m2s[:, :], 0.0), writes=["m2s"], tag="m2z")
            s.op("V", lambda v: v.tensor_scalar(rxs[:, :], mxs[:, :], 1.0 / 112.0, None, op0=ALU.mult),
                 reads=["mxs"], writes=["rxs"], tag="rxs")

            def wprep_sub(icc, sub, mat, wi, win, qw, qwn, slot, off, stage, wm, wmn):
                ib = icc % 2
                """Load one [2048f x 128ic] fp32 sub-chunk, compute its ws,
                bcast scale slices, quantize into qw[:, :, sub*128...]."""
                if stage == 2:
                    pass
                else:
                    s.op("G", (lambda off, wi: lambda g: pay_dma(
                            g, wi[:, :, :], off, [[16 * 512, 128], [512, 16], [1, 128]]))(off, wi),
                         reads=["pay"], writes=[win], tag=f"lw{mat}{icc}_{sub}")

                for rnd in ([] if stage == 2 else range(2)):
                    s.op("S", (lambda wi, rnd: lambda sc: sc.activation(
                            wabs[:, :, :], wi[:, 8 * rnd:8 * rnd + 8, :], AF.Abs))(wi, rnd),
                         reads=[win, wmn] if rnd else [win], writes=["wabs"],
                         tag=f"wx0{mat}{icc}_{sub}_{rnd}")
                    for lvl, st in ((1, 1), (2, 2)):
                        def ftree(st):
                            def f(v):
                                r = None
                                for a in range(0, 8, 2 * st):
                                    r = v.tensor_tensor(wabs[:, a, :], wabs[:, a, :],
                                                        wabs[:, a + st, :], op=ALU.max)
                                return r
                            return f
                        s.op("V", ftree(st), reads=["wabs"], writes=["wabs"],
                             tag=f"wx{lvl}{mat}{icc}_{sub}_{rnd}")
                    if rnd == 0:
                        s.op("V", (lambda wm: lambda v: v.tensor_tensor(wm[:, :], wabs[:, 0, :],
                                                                        wabs[:, 4, :], op=ALU.max))(wm),
                             reads=["wabs"], writes=[wmn], tag=f"wx3{mat}{icc}_{sub}_0")
                    else:
                        def flast(v):
                            v.tensor_tensor(wabs[:, 0, :], wabs[:, 0, :], wabs[:, 4, :], op=ALU.max)
                            return None
                        s.op("V", lambda v: v.tensor_tensor(wabs[:, 0, :], wabs[:, 0, :],
                                                            wabs[:, 4, :], op=ALU.max),
                             reads=["wabs"], writes=["wabs"], tag=f"wx3{mat}{icc}_{sub}_1")
                        s.op("V", (lambda wm: lambda v: v.tensor_tensor(wm[:, :], wm[:, :],
                                                                          wabs[:, 0, :], op=ALU.max))(wm),
                             reads=["wabs", wmn], writes=[wmn], tag=f"wx4{mat}{icc}_{sub}")
                if stage == 1:
                    return
                s.op("T", (lambda wm: lambda t: t.transpose(psw[:, 0:128], wm[:, :], eye32[:, :]))(wm),
                     reads=[wmn, "eye32"], writes=["psw"], tag=f"wt{mat}{icc}_{sub}")

                s.op("V", lambda v: v.reduce_max(wred[:, 0:1], psw[:, 0:128], axis=AX),
                     reads=["psw"], writes=["wred"], tag=f"wr{mat}{icc}_{sub}")
                s.op("V", lambda v: v.tensor_scalar(wred[:, 1:2], wred[:, 0:1], 1.0 / 448.0, None,
                                                    op0=ALU.mult),
                     reads=["wred"], writes=["wred"], tag=f"wr2{mat}{icc}_{sub}")
                s.op("T", lambda t: t.transpose(psw[0:1, 128:256], wred[:, 1:2], eye32[:, :]),
                     reads=["wred", "eye32"], writes=["psw"], tag=f"wfT{mat}{icc}_{sub}")
                s.op("V", (lambda sub: lambda v: v.tensor_copy(
                        wrow[0:1, sub * 128:(sub + 1) * 128], psw[0:1, 128:256]))(sub),
                     reads=["psw"], writes=["wrow"], tag=f"wfC{mat}{icc}_{sub}")

                s.op("V", (lambda sub: lambda v: v.reciprocal(
                        wrowt[0:1, 0:128], wrow[0:1, sub * 128:(sub + 1) * 128]))(sub),
                     reads=["wrow"], writes=["wrowt"], tag=f"wi{mat}{icc}_{sub}")
                s.op("V", lambda v: v.tensor_scalar(wrowt[0:1, 128:256], wrowt[0:1, 0:128], 0.5, None,
                                                    op0=ALU.mult),
                     reads=["wrowt"], writes=["wrowt"], tag=f"wi2{mat}{icc}_{sub}")

                def fbc(t):
                    t.matmul(psb[:, 0:128], ones32[0:1, 0:128], wrowt[0:1, 128:256], start=True, stop=True)
                    return t.matmul(psb[:, 128:256], ones32[0:1, 0:128],
                                    wrow[0:1, sub * 128:(sub + 1) * 128], start=True, stop=True)
                s.op("T", fbc, reads=["wrowt", "wrow", "ones32"], writes=["psb"], tag=f"wb{mat}{icc}_{sub}")

                sl = slot * 2 + ib

                def fev(v):
                    v.tensor_copy(wibc[:, slot, sub * 128:(sub + 1) * 128], psb[:, 0:128])
                    return v.tensor_copy(wsbc[:, sl, sub * 128:(sub + 1) * 128], psb[:, 128:256])
                s.op("V", fev, reads=["psb"], writes=["wibc", "wsbc"], tag=f"we{mat}{icc}_{sub}")

                def fq(wi, qw, sub, slot):
                    def f(v):
                        r = None
                        for a in range(16):
                            r = v.tensor_tensor(qw[:, a, sub * 128:(sub + 1) * 128], wi[:, a, :],
                                                wibc[:, slot, sub * 128:(sub + 1) * 128], op=ALU.mult)
                        return r
                    return f
                s.op("P", fq(wi, qw, sub, slot), reads=[win, "wibc"], writes=[qwn],
                     tag=f"wq{mat}{icc}_{sub}")

            def wprep_piece(icc2, j, stage=0):
                ib2 = icc2 % 2
                sub, mat = j // 2, ("g", "u")[j % 2]
                wm, wmn = wmac[j % 2], f"wmac{j % 2}"
                if mat == "g":
                    off = GRAN_OFF[("wg", icc2)] + sub * 128
                    wprep_sub(icc2, sub, "g", wgi, "wgi", qwg[ib2], f"qwg{ib2}", 0, off, stage, wm, wmn)
                else:
                    off = GRAN_OFF[("wu", icc2)] + sub * 128
                    wprep_sub(icc2, sub, "u", wui, "wui", qwu[ib2], f"qwu{ib2}", 1, off, stage, wm, wmn)

            for j in range(8):
                wprep_piece(0, j)
            for icc in range(11):
                ib = icc % 2
                def tail(tc):
                    pt = tc % 3
                    i2 = tc % 2
                    s.op("V", (lambda ib, i2: lambda v: v.tensor_tensor(
                            gtmp[i2][:, :], gtmp[i2][:, :], wsbc[:, 2 + ib, :], op=ALU.mult))(ib, i2),
                         reads=[f"gtmp{i2}", "wsbc"], writes=[f"gtmp{i2}"], tag=f"ue{icc}_{tc}")
                    s.op("V", (lambda tc, pt, i2: lambda v: v.scalar_tensor_tensor(
                            tstg[i2][:, :], psu[pt][:, :], rxs[:, tc:tc + 1], gtmp[i2][:, :],
                            op0=ALU.mult, op1=ALU.mult))(tc, pt, i2),
                         reads=[f"psu{pt}", "rxs", f"gtmp{i2}"], writes=[f"tstg{i2}"],
                         tag=f"tm{icc}_{tc}")
                    s.op("V", (lambda tc, i2: lambda v: v.reduce_max(
                            mtmp[:, i2:i2 + 1], tstg[i2][:, :], axis=AX,
                            apply_absolute_value=True))(tc, i2),
                         reads=[f"tstg{i2}"], writes=["mtmp"], tag=f"mxa{icc}_{tc}")
                    s.op("V", (lambda tc, i2: lambda v: v.tensor_tensor(
                            m2s[:, tc:tc + 1], m2s[:, tc:tc + 1], mtmp[:, i2:i2 + 1], op=ALU.max))(tc, i2),
                         reads=["mtmp", "m2s"], writes=["m2s"], tag=f"mxb{icc}_{tc}")
                    s.op("G", (lambda icc, tc, i2: lambda g: g.dma_start(
                            out=t_sc.ap()[icc, :, tc, :], in_=tstg[i2][:, :]))(icc, tc, i2),
                         reads=[f"tstg{i2}"], writes=["t_sc"], tag=f"ts{icc}_{tc}")

                for tc in range(16):
                    pt = tc % 3
                    i2 = tc % 2
                    def fg(tc, ib, pt):
                        def f(t):
                            r = None
                            for a in range(8):
                                r = t.matmul(psg[pt][:, :], qx[:, 2 * a:2 * a + 2, tc * 128:(tc + 1) * 128],
                                             qwg[ib][:, 2 * a:2 * a + 2, :], start=(a == 0), stop=(a == 7),
                                             perf_mode=PM.DoubleRow)
                            return r
                        return f
                    s.op("T", fg(tc, ib, pt), reads=["qx0", "qx1", f"qwg{ib}"], writes=[f"psg{pt}"],
                         tag=f"gm{icc}_{tc}")

                    def fu(tc, ib, pt):
                        def f(t):
                            r = None
                            for a in range(8):
                                r = t.matmul(psu[pt][:, :], qx[:, 2 * a:2 * a + 2, tc * 128:(tc + 1) * 128],
                                             qwu[ib][:, 2 * a:2 * a + 2, :], start=(a == 0), stop=(a == 7),
                                             perf_mode=PM.DoubleRow)
                            return r
                        return f
                    s.op("T", fu(tc, ib, pt), reads=["qx0", "qx1", f"qwu{ib}"], writes=[f"psu{pt}"],
                         tag=f"um{icc}_{tc}")
                    s.op("V", (lambda tc, ib, pt, i2: lambda v: v.scalar_tensor_tensor(
                            gtmp[i2][:, :], psg[pt][:, :], rxs[:, tc:tc + 1], wsbc[:, ib, :],
                            op0=ALU.mult, op1=ALU.mult))(tc, ib, pt, i2),
                         reads=[f"psg{pt}", "rxs", "wsbc"], writes=[f"gtmp{i2}"], tag=f"ge{icc}_{tc}")
                    s.op("S", (lambda i2: lambda sc: sc.activation(
                            gtmp[i2][:, :], gtmp[i2][:, :], AF.Silu))(i2),
                         reads=[f"gtmp{i2}"], writes=[f"gtmp{i2}"], tag=f"gs{icc}_{tc}")
                    if tc >= 1:
                        tail(tc - 1)
                    if icc + 1 < 11 and tc % 2 == 1:
                        j = tc // 2
                        wprep_piece(icc + 1, j, stage=1)
                        if j >= 1:
                            wprep_piece(icc + 1, j - 1, stage=2)
                tail(15)
                if icc + 1 < 11:
                    wprep_piece(icc + 1, 7, stage=2)
            s.emit_phase(nc, ctx)
            gup.close()
            s.note_free()
            psw8 = [ph5b.enter_context(nc.psum_tensor(f"psw8_{i}", [128, 256], FP8)) for i in range(8)]
            for i in range(8):
                s.buf(f"psw8_{i}", xread=True)

            s.op("V", lambda v: v.reciprocal(tqs[:, :], m2s[:, :]),
                 reads=["m2s"], writes=["tqs"], tag="m2inv0")
            s.op("V", lambda v: v.tensor_scalar(tqs[:, :], tqs[:, :], 224.0, None, op0=ALU.mult),
                 reads=["tqs"], writes=["tqs"], tag="m2inv1")
            s.op("V", lambda v: v.tensor_scalar(m2f[:, :], m2s[:, :], 1.0 / 112.0, None, op0=ALU.mult),
                 reads=["m2s"], writes=["m2f"], tag="m2fold")

            tri = [ph5b.enter_context(nc.sbuf_tensor(f"tri{i}", [128, 8, 512], BF16)) for i in range(2)]
            for icc in range(11):
                for half in range(2):
                    i2 = half
                    ti, tn = tri[i2], f"tri{i2}"
                    s.op("Q", (lambda icc, half, ti: lambda g: g.dma_start(
                            out=ti[:, :, :], in_=t_sc.ap()[icc, :, half * 8:(half + 1) * 8, :]))(icc, half, ti),
                         reads=["t_sc"], writes=[tn], tag=f"tl{icc}_{half}")
                    for tcl in range(8):
                        tc = half * 8 + tcl
                        kt8 = (icc * 16 + tc) % 2
                        bs = 4 * kt8
                        s.op("S", (lambda ti, tcl, tc, kt8: lambda sc: sc.activation(
                                tq8[kt8][:, :], ti[:, tcl, :], AF.Identity,
                                scale=tqs[:, tc:tc + 1]))(ti, tcl, tc, kt8),
                             reads=[tn, "tqs"], writes=[f"tq8_{kt8}"], tag=f"tq{icc}_{tc}")
                        for j in range(4):
                            s.op("T", (lambda j, kt8, bs: lambda t: t.transpose(
                                    s2ap(psw8[bs + j]), tq8[kt8][:, j * 128:(j + 1) * 128],
                                    eye8[:, :]))(j, kt8, bs),
                                 reads=[f"tq8_{kt8}", "eye8"], writes=[f"psw8_{bs + j}"],
                                 tag=f"tt{icc}_{tc}_{j}")

                        def fte(icc, tc, bs):
                            def f(v):
                                r = None
                                for j in range(4):
                                    r = v.tensor_copy(qt[:, 4 * icc + j, tc * 128:(tc + 1) * 128],
                                                      s2ap(psw8[bs + j]))
                                return r
                            return f
                        s.op("V", fte(icc, tc, bs),
                             reads=[f"psw8_{bs + j}" for j in range(4)],
                             writes=["qt"], tag=f"te{icc}_{tc}")
        s.emit_phase(nc, ctx)
        ph5b.close()
        sc5.close()
        s.note_free()

        ph5c = ExitStack()
        if phases >= 7:
            wdi2 = [ph5c.enter_context(nc.sbuf_tensor("wdi2_0", [128, 11, 512], F32))]
            wdq = [ph5c.enter_context(nc.sbuf_tensor(f"wdq{i}", [128, 44, 512], FP8)) for i in range(2)]
            ibcd = [ph5c.enter_context(nc.sbuf_tensor(f"ibcd{i}", [128, 512], F32)) for i in range(2)]
            sbcd = [ph5c.enter_context(nc.sbuf_tensor(f"sbcd{i}", [128, 512], F32)) for i in range(2)]
            rowt = ph5c.enter_context(nc.sbuf_tensor("rowt", [1, 512], F32))
            x2r = [ph5c.enter_context(nc.sbuf_tensor(f"x2r{i}", [128, 512], F32)) for i in range(2)]
            outs = [ph5c.enter_context(nc.sbuf_tensor(f"outs{i}", [128, 512], F32)) for i in range(2)]
            outt = [ph5c.enter_context(nc.sbuf_tensor(f"outt{i}", [128, 512], F32)) for i in range(2)]
            psd = [ph5c.enter_context(nc.psum_tensor(f"psd{i}", [128, 512], F32)) for i in range(4)]
            psb2 = ph5c.enter_context(nc.psum_tensor("psb2", [128, 512], F32))
            for i in range(4):
                s.buf(f"psd{i}", xread=True)
            s.buf("psb2", xread=True)

            for fc in range(4):
                fcp = fc % 2
                s.op("V", (lambda fc: lambda v: v.reciprocal(
                        rowt[0:1, :], wsd_row[0:1, fc * 512:(fc + 1) * 512]))(fc),
                     reads=["wsd_row"], writes=["rowt"], tag=f"div{fc}")
                s.op("V", lambda v: v.tensor_scalar(rowt[0:1, :], rowt[0:1, :], 0.5, None, op0=ALU.mult),
                     reads=["rowt"], writes=["rowt"], tag=f"div2{fc}")
                s.op("T", lambda t: t.matmul(psb2[:, :], ones32[0:1, 0:128], rowt[0:1, :],
                                             start=True, stop=True),
                     reads=["rowt", "ones32"], writes=["psb2"], tag=f"dbc{fc}")
                s.op("V", (lambda fcp: lambda v: v.tensor_copy(ibcd[fcp][:, :], psb2[:, :]))(fcp),
                     reads=["psb2"], writes=[f"ibcd{fcp}"], tag=f"dbe{fc}")
                s.op("V", (lambda fc: lambda v: v.tensor_copy(
                        rowt[0:1, :], wsd_row[0:1, fc * 512:(fc + 1) * 512]))(fc),
                     reads=["wsd_row"], writes=["rowt"], tag=f"dsv{fc}")
                s.op("T", lambda t: t.matmul(psb2[:, :], ones32[0:1, 0:128], rowt[0:1, :],
                                             start=True, stop=True),
                     reads=["rowt", "ones32"], writes=["psb2"], tag=f"dbc2{fc}")
                s.op("V", (lambda fcp: lambda v: v.tensor_copy(sbcd[fcp][:, :], psb2[:, :]))(fcp),
                     reads=["psb2"], writes=[f"sbcd{fcp}"], tag=f"dbe2{fc}")
                for g4 in range(4):
                    gp = 0
                    off = GRAN_OFF[("wd", fc * 4 + g4)]
                    s.op("G", (lambda off: lambda g: pay_dma(
                            g, wdi2[0][:, :, :], off, [[512, 128], [128 * 512, 11], [1, 512]]))(off),
                         reads=["pay"], writes=["wdi2_0"], tag=f"dl{fc}_{g4}")

                    def fqd(g4, gp, fcp):
                        def f(v):
                            r = None
                            for j in range(11):
                                r = v.tensor_tensor(wdq[fcp][:, g4 * 11 + j, :], wdi2[gp][:, j, :],
                                                    ibcd[fcp][:, :], op=ALU.mult)
                            return r
                        return f
                    s.op("V", fqd(g4, gp, fcp), reads=[f"wdi2_{gp}", f"ibcd{fcp}"],
                         writes=[f"wdq{fcp}"], tag=f"dq{fc}_{g4}")
                for tc in range(16):
                    pi = tc % 4
                    i2 = tc % 2

                    def fdm(fc, tc, pi, fcp):
                        def f(t):
                            r = None
                            for icc in range(22):
                                r = t.matmul(psd[pi][:, :],
                                             qt[:, 2 * icc:2 * icc + 2, tc * 128:(tc + 1) * 128],
                                             wdq[fcp][:, 2 * icc:2 * icc + 2, :],
                                             start=(icc == 0), stop=(icc == 21),
                                             perf_mode=PM.DoubleRow)
                            return r
                        return f
                    s.op("T", fdm(fc, tc, pi, fcp), reads=["qt", f"wdq{fcp}"], writes=[f"psd{pi}"],
                         tag=f"dm{fc}_{tc}")
                    s.op("Q", (lambda fc, tc, i2: lambda g: g.dma_start(
                            out=x2r[i2][:, :],
                            in_=x2_sc.ap()[tc, :, fc * 512:(fc + 1) * 512]))(fc, tc, i2),
                         reads=["x2_sc"], writes=[f"x2r{i2}"], tag=f"dx{fc}_{tc}")

                    s.op("V", (lambda tc, pi, i2, fcp: lambda v: v.scalar_tensor_tensor(
                            outt[i2][:, :], psd[pi][:, :], m2f[:, tc:tc + 1],
                            sbcd[fcp][:, :], op0=ALU.mult, op1=ALU.mult))(tc, pi, i2, fcp),
                         reads=[f"psd{pi}", "m2f", f"sbcd{fcp}"], writes=[f"outt{i2}"], tag=f"deA{fc}_{tc}")
                    s.op("V", (lambda pi, i2: lambda v: v.tensor_tensor(
                            outs[i2][:, :], outt[i2][:, :], x2r[i2][:, :], op=ALU.add))(pi, i2),
                         reads=[f"outt{i2}", f"x2r{i2}"], writes=[f"outs{i2}"], tag=f"deB{fc}_{tc}")
                    s.op("G", (lambda fc, tc, i2: lambda g: g.dma_start(
                            out=out_d.ap()[tc * 128:(tc + 1) * 128, fc * 512:(fc + 1) * 512],
                            in_=outs[i2][:, :]))(fc, tc, i2),
                         reads=[f"outs{i2}"], writes=["outdone"], tag=f"do{fc}_{tc}")
            s.op("G", lambda g: g.nop(), reads=["outdone"], tag="final", barrier=True)
        s.emit_phase(nc, ctx)
        ph5c.close()
        sc5bc.close()
        s.note_free()

        s.emit_phase(nc, ctx)
    return nc


def _pack_payload(x, wqkvu, wout, wg_p, wu_p, wd):
    pay = np.zeros((8, S_SH), np.float32)
    flat = pay.reshape(-1)

    def put(key, arr):
        off = GRAN_OFF[key]
        flat[off:off + arr.size] = arr.ravel()

    wq = np.ascontiguousarray(wqkvu.T).reshape(16, 128, 16, 512).transpose(2, 1, 0, 3)
    wqk = np.ascontiguousarray(wq).reshape(1, N_WQK * GR_E)
    wo = np.ascontiguousarray(wout.T).reshape(16, 128, D).transpose(1, 0, 2)
    wo4 = wo.reshape(128, 16, 4, 512).transpose(2, 0, 1, 3)
    for i in range(4):
        put(("wout", i), wo4[i])
    for base, w in (("wg", wg_p), ("wu", wu_p)):
        wt = np.ascontiguousarray(w.T).reshape(16, 128, 11, 512).transpose(2, 1, 0, 3)
        for i in range(11):
            put((base, i), wt[i])
    wdt = np.ascontiguousarray(wd.T).reshape(44, 128, 4, 512).transpose(2, 0, 1, 3)
    for fc in range(4):
        for g4 in range(4):
            put(("wd", fc * 4 + g4), wdt[fc, 11 * g4:11 * (g4 + 1)])
    return pay, wqk


def _consts():
    arrq = np.broadcast_to(np.arange(1, 257, dtype=np.float32), (128, 256)).copy()
    p = np.arange(128, dtype=np.float32)
    kdc = np.stack([255.0 - p, 127.0 - p], axis=1).astype(np.float32)
    cm = np.empty((2, 128, 256), np.float32)
    for st in range(2):
        sidx = st * 128 + np.arange(128)[:, None]
        tidx = np.arange(256)[None, :]
        d = tidx - sidx
        cm[st] = np.where(d >= 0, d, 1.0e9).astype(np.float32)
    eye32 = np.eye(128, dtype=np.float32)
    eyebf = np.eye(128, dtype=ml_dtypes.bfloat16)
    eye8 = np.eye(128, dtype=ml_dtypes.float8_e4m3)
    onesbf = np.ones((128, 128), ml_dtypes.bfloat16)
    ones32 = np.ones((1, 128), np.float32)
    return arrq, kdc, cm, eye32, eyebf, eye8, onesbf, ones32


def kernel(L0_attn_x, L0_attn_slope_rate, L0_attn_W_qkvu, L0_attn_W_out,
           L0_ffn_w_gate, L0_ffn_w_up, L0_ffn_w_down,
           L0_ffn_up_reorder_index, L0_ffn_down_reorder_index):
    x = np.asarray(L0_attn_x, np.float32)
    wqkvu = np.asarray(L0_attn_W_qkvu, np.float32)
    wout = np.asarray(L0_attn_W_out, np.float32)
    didx = np.asarray(L0_ffn_down_reorder_index).astype(np.int64)
    wg_p = np.take(np.asarray(L0_ffn_w_gate, np.float32), didx, axis=0)
    wu_p = np.take(np.asarray(L0_ffn_w_up, np.float32), didx, axis=0)
    wd = np.asarray(L0_ffn_w_down, np.float32)
    slope = np.asarray(L0_attn_slope_rate, np.float32).reshape(1, 16)
    slope_b = np.broadcast_to(slope, (128, 16)).copy()

    pay, wqk = _pack_payload(x, wqkvu, wout, wg_p, wu_p, wd)
    arrq, kdc, cm, eye32, eyebf, eye8, onesbf, ones32 = _consts()

    nc = build_program()
    in_maps = []
    for c in range(8):
        seq, half = c // 2, c % 2
        x_own = np.ascontiguousarray(x[seq, half * T:(half + 1) * T], np.float32)
        x_prev = np.ascontiguousarray(x[seq, 0:T], np.float32)
        cpkv = np.zeros((128, 24), np.float32)
        cpkv[:, 0:2] = kdc
        cpkv[:, 2:18] = slope_b
        cpkv[:, 18] = 1.0 if half == 1 else 0.0
        cpkv[:, 19] = EPS1
        cpkv[:, 20] = EPS2
        in_maps.append({
            "shard": pay[c:c + 1], "wqk": wqk, "xin": x_own, "xpr": x_prev,
            "cpk": cpkv,
            "arrq": arrq, "cmask": cm,
            "eye32": eye32, "eyebf": eyebf, "eye8": eye8,
            "onesbf": onesbf, "ones32": ones32,
        })
    res = run_bass_kernel_spmd(nc, in_maps, core_ids=list(range(8)))
    out = np.empty((B, N, D), np.float32)
    for c in range(8):
        seq, half = c // 2, c % 2
        out[seq, half * T:(half + 1) * T] = res.results[c]["out"]
    return out


# version 13
# speedup vs baseline: 1.0755x; 1.0650x over previous
import os
os.environ["NEURON_SCRATCHPAD_PAGE_SIZE"] = "512"
import numpy as np
import ml_dtypes
from contextlib import ExitStack

import concourse.bass as bass
import concourse.mybir as mybir
from concourse.bass_utils import run_bass_kernel_spmd

F32 = mybir.dt.float32
BF16 = mybir.dt.bfloat16
FP8 = mybir.dt.float8e4
I32 = mybir.dt.int32
AF = mybir.ActivationFunctionType
ALU = mybir.AluOpType
AX = mybir.AxisListType.X
PM = mybir.MatmulPerfMode

B, N, D = 4, 4096, 2048
H, DH = 16, 128
BLK = 256
T = 2048
NB = T // BLK
IC = 5632
EPS1 = 1e-5
EPS2 = 1e-6

_x_own_elems = 0
GR_E = 128 * 16 * 512
N_WQK = 16
_GRAN = []
for _i in range(4):
    _GRAN.append(("wout", _i, GR_E))
for _i in range(11):
    _GRAN.append(("wg", _i, GR_E))
for _i in range(11):
    _GRAN.append(("wu", _i, GR_E))
for _i in range(16):
    _GRAN.append(("wd", _i, 44 * 128 * 128))
S_SH = 5_636_096
_big = [g for g in _GRAN if g[0] != "wd"]
_wdg = [g for g in _GRAN if g[0] == "wd"]
assert len(_big) == 26 and len(_wdg) == 16
_assign = []
_bi = 0
for _s in range(8):
    _nb = 4 if _s < 2 else 3
    _assign.append(_big[_bi:_bi + _nb] + _wdg[2 * _s:2 * _s + 2])
    _bi += _nb
GRAN_OFF = {}
for _s, _gl in enumerate(_assign):
    _off = _x_own_elems
    for _g in _gl:
        GRAN_OFF[(_g[0], _g[1])] = _s * S_SH + _off
        _off += _g[2]
    assert _off <= S_SH
PAY_TOT = 8 * S_SH


class _Buf:
    __slots__ = ("name", "last_write", "readers", "xread", "fence")

    def __init__(self, name, xread, fence=()):
        self.name = name
        self.last_write = None
        self.readers = []
        self.xread = xread
        self.fence = list(fence)


class _Op:
    __slots__ = ("engine", "emit", "inc_amount", "needs_inc", "cum", "deps",
                 "tag", "barrier", "phase")

    def __init__(self, engine, emit, inc_amount, tag, phase):
        self.engine = engine
        self.emit = emit
        self.inc_amount = inc_amount
        self.needs_inc = False
        self.cum = -1
        self.deps = []
        self.tag = tag
        self.barrier = False
        self.phase = phase


class Sched:
    SECTION = {"T": "tensor", "V": "vector", "S": "scalar", "Q": "sync",
               "G": "gpsimd", "C": "gpsimd", "P": "gpsimd", "R": "scalar"}

    def __init__(self):
        self.pending = {k: [] for k in self.SECTION.values()}
        self.bufs = {}
        self.phase = 0
        self.tails = {}
        self.free_fence = {}
        self.sems = None
        self.cum = {k: 0 for k in ("T", "V", "S", "Q", "G", "C", "P", "R")}
        self.waited = {sec: {k: 0 for k in ("T", "V", "S", "Q", "G", "C", "P", "R")}
                       for sec in set(self.SECTION.values())}
        self.prev_dma = {"Q": 0, "G": 0, "R": 0}

    def buf(self, name, xread=False):
        b = self.bufs.get(name)
        if b is None:
            b = self.bufs[name] = _Buf(name, xread, self.free_fence.values())
        if xread:
            b.xread = True
        return b

    def op(self, engine, emit, reads=(), writes=(), tag="", barrier=False):
        rd = [self.buf(b) if isinstance(b, str) else b for b in reads]
        wr = [self.buf(b) if isinstance(b, str) else b for b in writes]
        o = _Op(engine, emit, 16 if engine in "QGR" else 1, tag, self.phase)
        o.barrier = barrier
        if engine in "QGCR" and not barrier:
            o.needs_inc = True
        deps = []
        for b in rd:
            if b.last_write is not None:
                deps.append(b.last_write)
            if b.xread:
                for r in b.readers:
                    if r.engine != engine:
                        deps.append(r)
        for b in wr:
            deps.extend(b.readers)
            if b.last_write is not None:
                deps.append(b.last_write)
            elif not b.readers:
                deps.extend(b.fence)
        res = []
        for d in deps:
            if d is o:
                continue
            if d.phase == self.phase:
                if d.barrier:
                    raise RuntimeError(f"dep on barrier op {d.tag} <- {tag}")
                d.needs_inc = True
                res.append(d)
            elif d.needs_inc:
                res.append(d)
            else:
                res.extend(self.tails.values())
        seen = set()
        o.deps = [d for d in res if not (id(d) in seen or seen.add(id(d)))]
        for b in rd:
            b.readers.append(o)
        for b in wr:
            b.last_write = o
            b.readers = []
        self.pending[self.SECTION[engine]].append(o)
        return o

    def emit_phase(self, nc, ctx):
        if self.sems is None:
            self.sems = {k: ctx.enter_context(nc.semaphore(f"sem_{k}"))
                         for k in ("T", "V", "S", "Q", "G", "C", "P", "R")}
        for sec, ops in self.pending.items():
            if ops and not ops[-1].barrier:
                ops[-1].needs_inc = True
        for sec, ops in self.pending.items():
            for o in ops:
                if o.needs_inc:
                    self.cum[o.engine] += o.inc_amount
                    o.cum = self.cum[o.engine]
        sched = self

        def make_body(section):
            def body(eng):
                waited = sched.waited[section]
                for o in sched.pending[section]:
                    if o.engine in sched.prev_dma:
                        pd = sched.prev_dma[o.engine]
                        if pd > waited[o.engine]:
                            eng.wait_ge(sched.sems[o.engine], pd)
                            waited[o.engine] = pd
                    for d in o.deps:
                        if d.cum <= 0:
                            raise RuntimeError(f"dep no inc: {d.tag}<-{o.tag}")
                        if d.cum > waited[d.engine]:
                            eng.wait_ge(sched.sems[d.engine], d.cum)
                            waited[d.engine] = d.cum
                    inst = o.emit(eng)
                    if o.needs_inc:
                        if inst is None:
                            raise RuntimeError(f"{o.tag}: inc w/o inst")
                        inst.then_inc(sched.sems[o.engine], o.inc_amount)
                        if o.engine in sched.prev_dma:
                            sched.prev_dma[o.engine] = o.cum
            return body

        active = [sec for sec in ("sync", "gpsimd", "vector", "scalar", "tensor")
                  if self.pending[sec]]
        if active:
            with nc.Block() as blk:
                for sec in active:
                    getattr(blk, sec)(make_body(sec))
        for sec, ops in self.pending.items():
            if ops:
                for o in reversed(ops):
                    if o.needs_inc:
                        self.tails[o.engine] = o
                        break
        self.pending = {k: [] for k in self.SECTION.values()}
        self.phase += 1

    def note_free(self):
        self.free_fence = dict(self.tails)


def build_program(phases=99, probes=()):
    nc = bass.Bass("TRN2", target_bir_lowering=False)

    shard = nc.dram_tensor("shard", [1, S_SH], F32, kind="ExternalInput")
    wqk_d = nc.dram_tensor("wqk", [1, N_WQK * GR_E], F32, kind="ExternalInput")
    xin_d = nc.dram_tensor("xin", [T, D], F32, kind="ExternalInput")
    xpr_d = nc.dram_tensor("xpr", [T, D], F32, kind="ExternalInput")
    cpk_d = nc.dram_tensor("cpk", [128, 24], F32, kind="ExternalInput")
    arrq_d = nc.dram_tensor("arrq", [128, 256], F32, kind="ExternalInput")
    cmask_d = nc.dram_tensor("cmask", [2, 128, 256], F32, kind="ExternalInput")
    eye32_d = nc.dram_tensor("eye32", [128, 128], F32, kind="ExternalInput")
    eyebf_d = nc.dram_tensor("eyebf", [128, 128], BF16, kind="ExternalInput")
    eye8_d = nc.dram_tensor("eye8", [128, 128], FP8, kind="ExternalInput")
    onesbf_d = nc.dram_tensor("onesbf", [128, 128], BF16, kind="ExternalInput")
    ones32_d = nc.dram_tensor("ones32", [1, 128], F32, kind="ExternalInput")
    out_d = nc.dram_tensor("out", [T, D], F32, kind="ExternalOutput")

    shard_b = nc.dram_tensor("shard_b", [1, S_SH], F32)
    pay = nc.dram_tensor("pay", [8, S_SH], F32, addr_space="Shared")

    q_sc = nc.dram_tensor("q_sc", [H, 128, T], BF16)
    k_sc = nc.dram_tensor("k_sc", [H, 128, T], BF16)
    u_sc = nc.dram_tensor("u_sc", [H, 128, T], BF16)
    kp_sc = nc.dram_tensor("kp_sc", [H, 128, T], BF16)
    v_sc = nc.dram_tensor("v_sc", [4, 16, 128, 512], BF16)
    vp_sc = nc.dram_tensor("vp_sc", [4, 16, 128, 512], BF16)
    x2_sc = nc.dram_tensor("x2_sc", [16, 128, D], F32)
    t_sc = nc.dram_tensor("t_sc", [11, 128, 16, 512], F32)

    s = Sched()

    with ExitStack() as ctx:
        e = ctx.enter_context

        cpk = e(nc.sbuf_tensor("cpk_s", [128, 24], F32))
        eye32 = e(nc.sbuf_tensor("eye32_s", [128, 128], F32))
        eyebf = e(nc.sbuf_tensor("eyebf_s", [128, 128], BF16))
        eye8 = e(nc.sbuf_tensor("eye8_s", [128, 128], FP8))
        onesbf = e(nc.sbuf_tensor("onesbf_s", [128, 128], BF16))
        ones32 = e(nc.sbuf_tensor("ones32_s", [1, 128], F32))
        mxs = e(nc.sbuf_tensor("mxs", [128, 16], F32))
        m2s = e(nc.sbuf_tensor("m2s", [128, 16], F32))
        m2f = e(nc.sbuf_tensor("m2f", [128, 16], F32))
        tqs = e(nc.sbuf_tensor("tqs", [128, 16], F32))
        rxs = e(nc.sbuf_tensor("rxs", [128, 16], F32))

        for nm, dst, src in (("cpk", cpk, cpk_d),
                             ("eye32", eye32, eye32_d),
                             ("eyebf", eyebf, eyebf_d), ("eye8", eye8, eye8_d),
                             ("onesbf", onesbf, onesbf_d), ("ones32", ones32, ones32_d)):
            s.op("Q", (lambda dst, src: lambda g: g.dma_start(out=dst[...], in_=src[...]))(dst, src),
                 writes=[nm], tag=f"ld_{nm}")

        s.op("G", lambda g: g.dma_start(out=shard_b[:, :], in_=shard[:, :]),
             writes=["shard_b"], tag="bounce")
        s.op("C", lambda g: g.collective_compute(
                "AllGather", ALU.bypass, replica_groups=[list(range(8))],
                ins=[shard_b.ap().opt()], outs=[pay.ap().opt()]),
             reads=["shard_b"], writes=["pay"], tag="ag")


        def pay_dma(eng, dst_ap, elem_off, dims, base=None):
            return eng.dma_start(out=dst_ap, in_=bass.AP(base if base is not None else pay,
                                                         elem_off, dims))

        def s2ap(guard, n=128):
            h = guard[0:1, 0:1].tensor
            return bass.AP(h, 0, [[2 * n, 128], [2, n]])

        sc12 = ExitStack()
        xl = sc12.enter_context(nc.sbuf_tensor("xl", [128, 16, T], BF16))
        xlp = sc12.enter_context(nc.sbuf_tensor("xlp", [128, 16, T], BF16))
        ph2 = ExitStack()
        wb2 = [ph2.enter_context(nc.sbuf_tensor(f"wb2_{i}", [128, 16, 512], BF16)) for i in range(2)]
        stg = [ph2.enter_context(nc.sbuf_tensor(f"stg{i}", [128, T], BF16)) for i in range(2)]
        vst = [ph2.enter_context(nc.sbuf_tensor(f"vst{i}", [128, 512], BF16)) for i in range(4)]
        wstg = ph2.enter_context(nc.sbuf_tensor("wstg", [128, 4, 512], F32))
        psp = [ph2.enter_context(nc.psum_tensor(f"psp{i}", [128, 512], F32)) for i in range(4)]
        s.buf("wstg")
        for i in range(2):
            s.buf(f"wb2_{i}")
            s.buf(f"stg{i}")
        for i in range(4):
            s.buf(f"vst{i}")
            s.buf(f"psp{i}", xread=True)
        ph1 = ExitStack()
        xin = [ph1.enter_context(nc.sbuf_tensor(f"xin{i}", [128, D], F32)) for i in range(2)]
        xlt = [ph1.enter_context(nc.sbuf_tensor(f"xlt{i}", [128, D], BF16)) for i in range(2)]
        st1 = [ph1.enter_context(nc.sbuf_tensor(f"st1_{i}", [128, 8], F32)) for i in range(2)]
        ps1 = [ph1.enter_context(nc.psum_tensor(f"ps1t_{i}", [128, 128], BF16)) for i in range(2)]
        s.buf("ps1t_0", xread=True)
        s.buf("ps1t_1", xread=True)

        def ln_chunk(which, tc, dest, destn):
            i2 = (which * 16 + tc) % 2
            xi, xt = xin[i2], xlt[i2]
            xn, tn = f"xin{i2}", f"xlt{i2}"
            st1_ = st1[i2]
            sn1 = f"st1_{i2}"
            tg = f"{which}_{tc}"
            xsrc = xin_d if which == 0 else xpr_d
            xsn = "xin_d" if which == 0 else "xpr_d"
            ldq = "R" if (which == 0 and tc % 2 == 1) or which == 1 else "Q"
            s.op(ldq, (lambda xi, tc, xsrc: lambda g: g.dma_start(
                    out=xi[:, :], in_=xsrc.ap()[tc * 128:(tc + 1) * 128, :]))(xi, tc, xsrc),
                 reads=[xsn], writes=[xn], tag=f"ldx{tg}")
            s.op("V", (lambda xi, st1_: lambda v: v.reduce_sum(st1_[:, 0:1], xi[:, :], axis=AX))(xi, st1_),
                 reads=[xn], writes=[sn1], tag=f"sum{tg}")
            s.op("S", (lambda xi, st1_, xt: lambda sc: sc.activation(
                    xt[:, :], xi[:, :], AF.Square, accum_out=st1_[:, 1:2]))(xi, st1_, xt),
                 reads=[xn], writes=[tn, sn1], tag=f"sq{tg}")

            def fstat(st1_):
                def f(v):
                    v.tensor_scalar(st1_[:, 2:3], st1_[:, 0:1], 1.0 / D, None, op0=ALU.mult)
                    return v.tensor_scalar(st1_[:, 3:4], st1_[:, 1:2], 1.0 / D, None, op0=ALU.mult)
                return f
            s.op("V", fstat(st1_), reads=[sn1], writes=[sn1], tag=f"st{tg}")
            s.op("V", (lambda st1_: lambda v: v.tensor_tensor(
                    st1_[:, 4:5], st1_[:, 2:3], st1_[:, 2:3], op=ALU.mult))(st1_),
                 reads=[sn1], writes=[sn1], tag=f"st2{tg}")
            s.op("V", (lambda st1_: lambda v: v.tensor_tensor(
                    st1_[:, 3:4], st1_[:, 3:4], st1_[:, 4:5], op=ALU.subtract))(st1_),
                 reads=[sn1], writes=[sn1], tag=f"st3{tg}")
            s.op("S", (lambda st1_: lambda sc: sc.activation(
                    st1_[:, 5:6], st1_[:, 3:4], AF.Sqrt, bias=cpk[:, 19:20]))(st1_),
                 reads=[sn1, "cpk"], writes=[sn1], tag=f"sqr{tg}")
            s.op("V", (lambda st1_: lambda v: v.reciprocal(st1_[:, 6:7], st1_[:, 5:6]))(st1_),
                 reads=[sn1], writes=[sn1], tag=f"rc{tg}")
            s.op("V", (lambda xi, xt, st1_: lambda v: v.tensor_scalar(
                    xt[:, :], xi[:, :], st1_[:, 2:3], st1_[:, 6:7],
                    op0=ALU.subtract, op1=ALU.mult))(xi, xt, st1_),
                 reads=[xn, sn1], writes=[tn], tag=f"nm{tg}")
            for a in range(16):
                ai = a % 2
                s.op("T", (lambda a, xt, ai: lambda t: t.transpose(
                        ps1[ai][:, :], xt[:, a * 128:(a + 1) * 128], eyebf[:, :]))(a, xt, ai),
                     reads=[tn, "eyebf"], writes=[f"ps1t_{ai}"], tag=f"tr{tg}_{a}")
                if ai == 0:
                    s.op("S", (lambda a, tc, dest, ai: lambda sc: sc.activation(
                            dest[:, a, tc * 128:(tc + 1) * 128], ps1[ai][:, :], AF.Identity))(a, tc, dest, ai),
                         reads=[f"ps1t_{ai}"], writes=[f"{destn}0"], tag=f"te{tg}_{a}")
                else:
                    s.op("V", (lambda a, tc, dest, ai: lambda v: v.tensor_copy(
                            dest[:, a, tc * 128:(tc + 1) * 128], ps1[ai][:, :]))(a, tc, dest, ai),
                         reads=[f"ps1t_{ai}"], writes=[f"{destn}1"], tag=f"te{tg}_{a}")

        CAST_SET = (0, 1, 2, 3, 4, 5, 6, 7, 12, 13, 14, 15)

        def emit_oc(oc):
                wb, wn = wb2[oc % 2], f"wb2_{oc%2}"
                if oc in CAST_SET:
                    for h4 in range(4):
                        s.op("Q", (lambda oc, h4: lambda g: g.dma_start(
                                out=wstg[:, :, :],
                                in_=bass.AP(wqk_d, oc * GR_E + h4 * 4 * 512,
                                            [[16 * 512, 128], [512, 4], [1, 512]])))(oc, h4),
                             reads=["wqk"], writes=["wstg"], tag=f"lws{oc}_{h4}")
                        s.op("V", (lambda wb, h4: lambda v: v.tensor_copy(
                                wb[:, 4 * h4:4 * h4 + 4, :], wstg[:, :, :]))(wb, h4),
                             reads=["wstg"], writes=[wn], tag=f"cst{oc}_{h4}")
                else:
                    s.op("G", (lambda oc, wb: lambda g: pay_dma(
                            g, wb[:, :, :], oc * GR_E, [[16 * 512, 128], [512, 16], [1, 512]],
                            base=wqk_d))(oc, wb),
                         reads=["wqk"], writes=[wn], tag=f"ldw{oc}")
                fam = ("q", "k", "v", "u")[oc // 4]
                if fam != "v":
                    srcs = [("o", xl, "xl")]
                    if fam == "k":
                        srcs.append(("p", xlp, "xlp"))
                    for tagp, xsrc, xsn in srcs:
                        for sub in range(4):
                            head = (oc % 4) * 4 + sub
                            sg, sn = stg[sub % 2], f"stg{sub%2}"
                            for tch in range(4):
                                pi = tch % 4

                                def fmm(wb, sub, xsrc, tch, pi):
                                    def f(t):
                                        r = None
                                        for a in range(16):
                                            r = t.matmul(psp[pi][:, :],
                                                         wb[:, a, sub * 128:(sub + 1) * 128],
                                                         xsrc[:, a, tch * 512:(tch + 1) * 512],
                                                         start=(a == 0), stop=(a == 15))
                                        return r
                                    return f
                                s.op("T", fmm(wb, sub, xsrc, tch, pi),
                                     reads=[wn, f"{xsn}0", f"{xsn}1"],
                                     writes=[f"psp{pi}"], tag=f"mm{oc}{tagp}_{sub}_{tch}")
                                func = AF.Silu if fam in ("q", "k") else AF.Identity
                                s.op("S", (lambda sg, tch, pi, func: lambda sc: sc.activation(
                                        sg[:, tch * 512:(tch + 1) * 512], psp[pi][:, :], func))(sg, tch, pi, func),
                                     reads=[f"psp{pi}"], writes=[sn], tag=f"ev{oc}{tagp}_{sub}_{tch}")
                            dst = {"q": q_sc, "k": k_sc, "u": u_sc}[fam]
                            dnm = fam + "_sc"
                            if tagp == "p":
                                dst, dnm = kp_sc, "kp_sc"
                            s.op("G", (lambda dst, head, sg: lambda g: g.dma_start(
                                    out=dst[head], in_=sg[:, :]))(dst, head, sg),
                                 reads=[sn], writes=[dnm], tag=f"sp{fam}{tagp}{head}")
                else:
                    vcc = oc - 8
                    for tagp, xsrc, xsn, dst, dnm in (("o", xl, "xl", v_sc, "v_sc"),
                                                      ("p", xlp, "xlp", vp_sc, "vp_sc")):
                        for tc in range(16):
                            pi = tc % 4

                            def fmmv(wb, tc, xsrc, pi):
                                def f(t):
                                    r = None
                                    for a in range(16):
                                        r = t.matmul(psp[pi][:, :],
                                                     xsrc[:, a, tc * 128:(tc + 1) * 128],
                                                     wb[:, a, :], start=(a == 0), stop=(a == 15))
                                    return r
                                return f
                            s.op("T", fmmv(wb, tc, xsrc, pi),
                                 reads=[wn, f"{xsn}0", f"{xsn}1"],
                                 writes=[f"psp{pi}"], tag=f"vm{oc}{tagp}_{tc}")
                            vs, vn = vst[tc % 4], f"vst{tc%4}"
                            s.op("S", (lambda vs, pi: lambda sc: sc.activation(
                                    vs[:, :], psp[pi][:, :], AF.Identity))(vs, pi),
                                 reads=[f"psp{pi}"], writes=[vn], tag=f"vev{oc}{tagp}_{tc}")
                            s.op("G", (lambda dst, vcc, tc, vs: lambda g: g.dma_start(
                                    out=dst[vcc, tc], in_=vs[:, :]))(dst, vcc, tc, vs),
                                 reads=[vn], writes=[dnm], tag=f"vsp{oc}{tagp}_{tc}")
        if phases >= 1:
            for tc in range(16):
                ln_chunk(0, tc, xl, "xl")
        if phases >= 2:
            for k, oc in enumerate((0, 1, 2, 3)):
                emit_oc(oc)
                if phases >= 1:
                    for j in range(4):
                        ln_chunk(1, 4 * k + j, xlp, "xlp")
            for oc in (4, 5, 6, 7, 12, 13, 14, 15, 8, 9, 10, 11):
                emit_oc(oc)
        elif phases >= 1:
            for tc in range(16):
                ln_chunk(1, tc, xlp, "xlp")
        s.emit_phase(nc, ctx)
        ph1.close()
        ph2.close()
        sc12.close()
        s.note_free()

        sc5bc = ExitStack()
        wsd_row = sc5bc.enter_context(nc.sbuf_tensor("wsd_row", [1, D], F32))
        sc34 = ExitStack()
        og = sc34.enter_context(nc.sbuf_tensor("og", [128, 16, T], BF16))
        ph3 = ExitStack()
        if phases >= 3:
            arrq = ph3.enter_context(nc.sbuf_tensor("arrq_s", [128, 256], F32))
            cmask = ph3.enter_context(nc.sbuf_tensor("cmask_s", [128, 2, 256], F32))
            s.op("Q", lambda g: g.dma_start(out=arrq[:, :], in_=arrq_d[:, :]),
                 writes=["arrq"], tag="ld_arrq")
            s.op("Q", lambda g: g.dma_start(out=cmask[:, :, :],
                                            in_=cmask_d.ap().rearrange("a p t -> p a t")),
                 writes=["cmask"], tag="ld_cmask")
            qdec = ph3.enter_context(nc.sbuf_tensor("qdec", [128, 16, 256], BF16))
            mask = ph3.enter_context(nc.sbuf_tensor("mask", [128, 16, 2, 256], BF16))
            kdec = ph3.enter_context(nc.sbuf_tensor("kdec", [128, 16, 2], F32))
            bdc = ph3.enter_context(nc.sbuf_tensor("bdc", [128, 16], F32))
            tmpd = ph3.enter_context(nc.sbuf_tensor("tmpd", [128, 2, 256], F32))
            kv = ph3.enter_context(nc.sbuf_tensor("kv", [128, 16, 128], F32))
            kvb = [ph3.enter_context(nc.sbuf_tensor(f"kvb{i}", [128, 128], BF16)) for i in range(2)]
            qb = [ph3.enter_context(nc.sbuf_tensor(f"qb{i}", [128, 16, 256], BF16)) for i in range(2)]
            kb = [ph3.enter_context(nc.sbuf_tensor(f"kb{i}", [128, 16, 256], BF16)) for i in range(2)]
            vb = [ph3.enter_context(nc.sbuf_tensor(f"vb{i}", [128, 2, 2048], BF16)) for i in range(2)]
            ub = [ph3.enter_context(nc.sbuf_tensor("ub0", [128, 16, 256], BF16))]
            kqm = [ph3.enter_context(nc.sbuf_tensor(f"kqm{i}", [128, 2, 256], BF16)) for i in range(2)]
            ktda = [ph3.enter_context(nc.sbuf_tensor(f"ktda{i}", [128, 2, 16, 128], BF16)) for i in range(2)]
            qdq = [ph3.enter_context(nc.sbuf_tensor(f"qdq{i}", [128, 256], BF16)) for i in range(2)]
            obf = ph3.enter_context(nc.sbuf_tensor("obf", [128, 16, 256], BF16))
            osq = ph3.enter_context(nc.sbuf_tensor("osq", [128, 16, 256], BF16))
            rsq = ph3.enter_context(nc.sbuf_tensor("rsq", [128, 256], F32))
            ogt = ph3.enter_context(nc.sbuf_tensor("ogt", [128, 2, 256], F32))
            ps_kq = [ph3.enter_context(nc.psum_tensor(f"ps_kq{i}", [128, 512], F32)) for i in range(2)]
            ps_o = [ph3.enter_context(nc.psum_tensor(f"ps_o{i}", [128, 256], F32)) for i in range(2)]
            ps_kv = ph3.enter_context(nc.psum_tensor("ps_kv", [128, 128], F32))
            ps_kt = [ph3.enter_context(nc.psum_tensor(f"ps_kt{i}", [128, 256], BF16)) for i in range(2)]
            ps_rms = ph3.enter_context(nc.psum_tensor("ps_rms", [128, 256], F32))
            for n in ("ps_kq0", "ps_kq1", "ps_o0", "ps_o1", "ps_kv", "ps_kt0", "ps_kt1", "ps_rms"):
                s.buf(n, xread=True)

            for h in range(16):
                s.op("V", (lambda h: lambda v: v.tensor_scalar(
                        tmpd[:, 0, :], arrq[:, :], cpk[:, 2 + h:3 + h], None, op0=ALU.mult))(h),
                     reads=["arrq", "cpk"], writes=["tmpd"], tag=f"qd0_{h}")
                s.op("S", (lambda h: lambda sc: sc.activation(
                        qdec[:, h, :], tmpd[:, 0, :], AF.Exp, scale=-1.0))(h),
                     reads=["tmpd"], writes=["qdec"], tag=f"qd1_{h}")

                def fm(h):
                    def g(v):
                        v.tensor_scalar(tmpd[:, 0, :], cmask[:, 0, :], cpk[:, 2 + h:3 + h], None, op0=ALU.mult)
                        return v.tensor_scalar(tmpd[:, 1, :], cmask[:, 1, :], cpk[:, 2 + h:3 + h], None, op0=ALU.mult)
                    return g
                s.op("V", fm(h), reads=["cmask", "cpk"], writes=["tmpd"], tag=f"mk0_{h}")

                def fe(h):
                    def g(sc):
                        sc.activation(mask[:, h, 0, :], tmpd[:, 0, :], AF.Exp, scale=-1.0)
                        return sc.activation(mask[:, h, 1, :], tmpd[:, 1, :], AF.Exp, scale=-1.0)
                    return g
                s.op("S", fe(h), reads=["tmpd"], writes=["mask"], tag=f"mk1_{h}")
                s.op("V", (lambda h: lambda v: v.tensor_scalar(
                        tmpd[:, 0, 0:2], cpk[:, 0:2], cpk[:, 2 + h:3 + h], None, op0=ALU.mult))(h),
                     reads=["cpk"], writes=["tmpd"], tag=f"kd0_{h}")
                s.op("S", (lambda h: lambda sc: sc.activation(
                        kdec[:, h, :], tmpd[:, 0, 0:2], AF.Exp, scale=-1.0))(h),
                     reads=["tmpd"], writes=["kdec"], tag=f"kd1_{h}")
            s.op("S", lambda sc: sc.activation(bdc[:, :], cpk[:, 2:18], AF.Exp, scale=-256.0),
                 reads=["cpk"], writes=["bdc"], tag="bd")
            s.op("V", lambda v: v.memset(kv[:, :, :], 0.0),
                 writes=[f"kv_h{h}" for h in range(16)], tag="kv0")

            def load_blk(b, par, pre):
                ksrc, kn = (kp_sc, "kp_sc") if pre else (k_sc, "k_sc")
                vsrc, vn = (vp_sc, "vp_sc") if pre else (v_sc, "v_sc")
                if not pre:
                    s.op("Q", (lambda b, par: lambda g: g.dma_start(
                            out=qb[par][:, :, :],
                            in_=q_sc.ap()[:, :, b * 256:(b + 1) * 256].rearrange("h p t -> p h t")))(b, par),
                         reads=["q_sc"], writes=[f"qb{par}"], tag=f"lq{b}")
                    s.op("Q", (lambda b: lambda g: g.dma_start(
                            out=ub[0][:, :, :],
                            in_=u_sc.ap()[:, :, b * 256:(b + 1) * 256].rearrange("h p t -> p h t")))(b),
                         reads=["u_sc"], writes=["ub0"], tag=f"lu{b}")
                s.op("Q", (lambda b, par, ksrc: lambda g: g.dma_start(
                        out=kb[par][:, :, :],
                        in_=ksrc.ap()[:, :, b * 256:(b + 1) * 256].rearrange("h p t -> p h t")))(b, par, ksrc),
                     reads=[kn], writes=[f"kb{par}"], tag=f"lk{pre}{b}")
                for vcc in range(4):
                    s.op("Q", (lambda b, par, vcc, vsrc: lambda g: g.dma_start(
                            out=vb[par][:, :, vcc * 512:(vcc + 1) * 512],
                            in_=vsrc.ap()[vcc, 2 * b:2 * b + 2, :, :].rearrange("c p n -> p c n")))(b, par, vcc, vsrc),
                         reads=[vn], writes=[f"vb{par}"], tag=f"lv{pre}{b}_{vcc}")

            def ktrans_blk(b, par, pre):
                kn = "kp_sc" if pre else "k_sc"
                for h in range(16):
                    pk = ps_kt[(2 * b + h) % 2]
                    pkn = f"ps_kt{(2 * b + h) % 2}"

                    def ftr(h, par, pk):
                        def f(t):
                            t.transpose(pk[:, 0:128], kb[par][:, h, 0:128], eyebf[:, :])
                            return t.transpose(pk[:, 128:256], kb[par][:, h, 128:256], eyebf[:, :])
                        return f
                    s.op("T", ftr(h, par, pk), reads=[f"kb{par}", "eyebf"], writes=[pkn],
                         tag=f"kt{pre}{b}_{h}")

                    def fkd(h, par, pk):
                        def f(sc):
                            sc.activation(ktda[par][:, 0, h, :], pk[:, 0:128], AF.Identity,
                                          scale=kdec[:, h, 0:1])
                            return sc.activation(ktda[par][:, 1, h, :], pk[:, 128:256], AF.Identity,
                                                 scale=kdec[:, h, 1:2])
                        return f
                    s.op("S", fkd(h, par, pk), reads=[pkn, "kdec"], writes=[f"ktda{par}_h{h}"],
                         tag=f"kd{pre}{b}_{h}")

            def km_op(h, b, par, pre, pk, pkn):
                def fkv(h, par, pk):
                    def f(t):
                        t.matmul(pk[:, :], ktda[par][:, 0, h, :], vb[par][:, 0, h * 128:(h + 1) * 128],
                                 start=True, stop=False)
                        return t.matmul(pk[:, :], ktda[par][:, 1, h, :], vb[par][:, 1, h * 128:(h + 1) * 128],
                                        start=False, stop=True)
                    return f
                s.op("T", fkv(h, par, pk), reads=[f"ktda{par}_h{h}", f"vb{par}"], writes=[pkn],
                     tag=f"km{pre}{b}_{h}")

            def ku_op(h, b, pre, pk, pkn):
                s.op("V", (lambda h, pk: lambda v: v.scalar_tensor_tensor(
                        kv[:, h, :], kv[:, h, :], bdc[:, h:h + 1], pk[:, :],
                        op0=ALU.mult, op1=ALU.add))(h, pk),
                     reads=[pkn, "bdc", f"kv_h{h}"], writes=[f"kv_h{h}"], tag=f"ku{pre}{b}_{h}")

            pre_pk = lambda h: (ps_kq[h % 2][:, 0:128], f"ps_kq{h % 2}")
            for b in range(NB):
                load_blk(b, b % 2, True)
                ktrans_blk(b, b % 2, True)
                for h in range(16):
                    if h >= 1:
                        ku_op(h - 1, b, True, *pre_pk(h - 1))
                    km_op(h, b, b % 2, True, *pre_pk(h))
                ku_op(15, b, True, *pre_pk(15))
            s.op("V", lambda v: v.tensor_scalar(kv[:, :, :], kv[:, :, :], cpk[:, 18:19], None, op0=ALU.mult),
                 reads=["cpk"] + [f"kv_h{h}" for h in range(16)],
                 writes=[f"kv_h{h}" for h in range(16)], tag="kvflag")

            def kq_mm(h, b, par):
                i = h % 2
                def fkq(h, par, i):
                    def f(t):
                        t.matmul(ps_kq[i][:, 0:256], kb[par][:, h, 0:128], qb[par][:, h, :],
                                 start=True, stop=True)
                        return t.matmul(ps_kq[i][:, 256:512], kb[par][:, h, 128:256], qb[par][:, h, :],
                                        start=True, stop=True)
                    return f
                s.op("T", fkq(h, par, i), reads=[f"kb{par}", f"qb{par}"],
                     writes=[f"ps_kq{i}"], tag=f"kq{b}_{h}")

            def mkq_qdq(h, b, par):
                i = h % 2
                def fmk(h, i):
                    def f(v):
                        v.tensor_tensor(kqm[i][:, 0, :], ps_kq[i][:, 0:256], mask[:, h, 0, :], op=ALU.mult)
                        return v.tensor_tensor(kqm[i][:, 1, :], ps_kq[i][:, 256:512], mask[:, h, 1, :], op=ALU.mult)
                    return f
                s.op("V", fmk(h, i), reads=[f"ps_kq{i}", "mask"], writes=[f"kqm{i}"], tag=f"mq{b}_{h}")
                s.op("P", (lambda h, par, i: lambda g: g.tensor_tensor(
                        qdq[i][:, :], qb[par][:, h, :], qdec[:, h, :], op=ALU.mult))(h, par, i),
                     reads=[f"qb{par}", "qdec"], writes=[f"qdq{i}"], tag=f"qq{b}_{h}")

            def kc_op(h, b):
                i = h % 2
                s.op("S", (lambda h, i: lambda sc: sc.activation(
                        kvb[i][:, :], kv[:, h, :], AF.Identity))(h, i),
                     reads=[f"kv_h{h}"], writes=[f"kvb{i}"], tag=f"kc{b}_{h}")

            def o_mm(h, b, par):
                i = h % 2
                def fo(h, par, i):
                    def f(t):
                        t.matmul(ps_o[i][:, 0:256], kvb[i][:, :], qdq[i][:, :], start=True, stop=False)
                        t.matmul(ps_o[i][:, 0:256], vb[par][:, 0, h * 128:(h + 1) * 128],
                                 kqm[i][:, 0, :], start=False, stop=False)
                        return t.matmul(ps_o[i][:, 0:256], vb[par][:, 1, h * 128:(h + 1) * 128],
                                        kqm[i][:, 1, :], start=False, stop=True)
                    return f
                s.op("T", fo(h, par, i), reads=[f"kvb{i}", f"qdq{i}", f"vb{par}", f"kqm{i}"],
                     writes=[f"ps_o{i}"], tag=f"om{b}_{h}")

            def o_evac(h, b):
                i = h % 2
                s.op("S", (lambda h, i: lambda sc: sc.activation(obf[:, h, :], ps_o[i][:, 0:256],
                                                                 AF.Identity))(h, i),
                     reads=[f"ps_o{i}"], writes=[f"obf_h{h}"], tag=f"oe{b}_{h}")
                s.op("S", (lambda h, i: lambda sc: sc.activation(osq[:, h, :], ps_o[i][:, 0:256],
                                                                 AF.Square))(h, i),
                     reads=[f"ps_o{i}"], writes=[f"osq_h{h}"], tag=f"os{b}_{h}")

            for b in range(NB):
                par = b % 2
                load_blk(b, par, False)
                ktrans_blk(b, par, False)
                kq_mm(0, b, par)
                kc_op(0, b)
                for h in range(16):
                    mkq_qdq(h, b, par)
                    if h + 1 < 16:
                        kq_mm(h + 1, b, par)
                        kc_op(h + 1, b)
                    o_mm(h, b, par)
                    o_evac(h, b)
                    if h >= 1:
                        ku_op(h - 1, b, False, ps_kv, "ps_kv")
                    km_op(h, b, par, False, ps_kv, "ps_kv")
                ku_op(15, b, False, ps_kv, "ps_kv")

                def frms(t):
                    r = None
                    for ct in range(16):
                        r = t.matmul(ps_rms[:, :], onesbf[:, :], osq[:, ct, :],
                                     start=(ct == 0), stop=(ct == 15))
                    return r
                s.op("T", frms, reads=["onesbf"] + [f"osq_h{h}" for h in range(16)],
                     writes=["ps_rms"], tag=f"rm{b}")
                s.op("S", lambda sc: sc.activation(rsq[:, :], ps_rms[:, :], AF.Sqrt,
                                                   scale=1.0 / D, bias=cpk[:, 20:21]),
                     reads=["ps_rms", "cpk"], writes=["rsq"], tag=f"rs{b}")
                s.op("V", lambda v: v.reciprocal(rsq[:, :], rsq[:, :]),
                     reads=["rsq"], writes=["rsq"], tag=f"rr{b}")
                for ct in range(16):
                    s.op("V", (lambda ct, b: lambda v: v.tensor_tensor(
                            ogt[:, ct % 2, :], obf[:, ct, :], rsq[:, :], op=ALU.mult))(ct, b),
                         reads=[f"obf_h{ct}", "rsq"], writes=["ogt"], tag=f"ogA{b}_{ct}")
                    s.op("V", (lambda ct, b: lambda v: v.tensor_tensor(
                            og[:, ct, b * 256:(b + 1) * 256], ogt[:, ct % 2, :],
                            ub[0][:, ct, :], op=ALU.mult))(ct, b),
                         reads=["ogt", "ub0"], writes=["og"], tag=f"ogB{b}_{ct}")
        s.emit_phase(nc, ctx)
        ph3.close()
        s.note_free()

        ph4 = ExitStack()
        if phases >= 4:
            wo = ph4.enter_context(nc.sbuf_tensor("wo", [128, 16, D], BF16))
            x2s = [ph4.enter_context(nc.sbuf_tensor(f"x2s{i}", [128, D], F32)) for i in range(2)]
            ps4 = [ph4.enter_context(nc.psum_tensor(f"ps4_{i}", [128, 512], F32)) for i in range(4)]
            for i in range(4):
                s.buf(f"ps4_{i}", xread=True)
            for wi_ in range(4):
                s.op("G", (lambda wi_: lambda g: pay_dma(
                        g, wo[:, :, wi_ * 512:(wi_ + 1) * 512], GRAN_OFF[("wout", wi_)],
                        [[16 * 512, 128], [512, 16], [1, 512]]))(wi_),
                     reads=["pay"], writes=["wo"], tag=f"ldwo{wi_}")
            for tc in range(16):
                i2 = tc % 2
                s.op("Q", (lambda tc, i2: lambda g: g.dma_start(
                        out=x2s[i2][:, :], in_=xin_d.ap()[tc * 128:(tc + 1) * 128, :]))(tc, i2),
                     reads=["xin_d"], writes=[f"x2s{i2}"], tag=f"lr{tc}")
                for fc in range(4):
                    pi = fc

                    def fmm(tc, fc, pi):
                        def f(t):
                            r = None
                            for cc in range(16):
                                r = t.matmul(ps4[pi][:, :], og[:, cc, tc * 128:(tc + 1) * 128],
                                             wo[:, cc, fc * 512:(fc + 1) * 512],
                                             start=(cc == 0), stop=(cc == 15))
                            return r
                        return f
                    s.op("T", fmm(tc, fc, pi), reads=["og", "wo"], writes=[f"ps4_{pi}"],
                         tag=f"o2_{tc}_{fc}")
                    s.op("V", (lambda tc, fc, pi, i2: lambda v: v.tensor_tensor(
                            x2s[i2][:, fc * 512:(fc + 1) * 512], ps4[pi][:, :],
                            x2s[i2][:, fc * 512:(fc + 1) * 512], op=ALU.add))(tc, fc, pi, i2),
                         reads=[f"ps4_{pi}", f"x2s{i2}"], writes=[f"x2s{i2}"], tag=f"x2_{tc}_{fc}")
                s.op("Q", (lambda tc, i2: lambda g: g.dma_start(
                        out=x2_sc[tc], in_=x2s[i2][:, :]))(tc, i2),
                     reads=[f"x2s{i2}"], writes=["x2_sc"], tag=f"sx2{tc}")
            wdm = ph4.enter_context(nc.sbuf_tensor("wdm", [128, 512], F32))
            wdm2 = ph4.enter_context(nc.sbuf_tensor("wdm2", [128, 512], F32))
            wdi = [ph4.enter_context(nc.sbuf_tensor("wdi0", [128, 11, 512], F32))]
            ps4w = ph4.enter_context(nc.psum_tensor("ps4w", [128, 512], F32))
            s.buf("ps4w", xread=True)
            for fc in range(4):
                for g4 in range(4):
                    off = GRAN_OFF[("wd", fc * 4 + g4)]
                    s.op("G", (lambda off: lambda g: pay_dma(
                            g, wdi[0][:, :, :], off, [[512, 128], [128 * 512, 11], [1, 512]]))(off),
                         reads=["pay"], writes=["wdi0"], tag=f"wd{fc}_{g4}")

                    for j in range(11):
                        if g4 == 0 and j == 0:
                            s.op("V", lambda v: v.scalar_tensor_tensor(
                                    wdm[:, :], wdi[0][:, 0, :], -1.0, wdi[0][:, 0, :],
                                    op0=ALU.mult, op1=ALU.max),
                                 reads=["wdi0"], writes=["wdm"], tag=f"wm{fc}_{g4}_0")
                        else:
                            s.op("V", (lambda j: lambda v: v.scalar_tensor_tensor(
                                    wdm2[:, :], wdi[0][:, j, :], -1.0, wdi[0][:, j, :],
                                    op0=ALU.mult, op1=ALU.max))(j),
                                 reads=["wdi0"], writes=["wdm2"], tag=f"wmA{fc}_{g4}_{j}")
                            s.op("V", lambda v: v.tensor_tensor(
                                    wdm[:, :], wdm[:, :], wdm2[:, :], op=ALU.max),
                                 reads=["wdm", "wdm2"], writes=["wdm"], tag=f"wmB{fc}_{g4}_{j}")

                def ftr5(t):
                    r = None
                    for j in range(4):
                        r = t.transpose(ps4w[:, j * 128:(j + 1) * 128],
                                        wdm[:, j * 128:(j + 1) * 128], eye32[:, :])
                    return r
                s.op("T", ftr5, reads=["wdm", "eye32"], writes=["ps4w"], tag=f"wt{fc}")

                def fred(v):
                    r = None
                    for j in range(4):
                        r = v.reduce_max(wdm[:, j:j + 1], ps4w[:, j * 128:(j + 1) * 128], axis=AX)
                    return r
                s.op("V", fred, reads=["ps4w"], writes=["wdm"], tag=f"wr{fc}")
                s.op("V", lambda v: v.tensor_scalar(wdm[:, 0:4], wdm[:, 0:4], 1.0 / 448.0, None,
                                                    op0=ALU.mult),
                     reads=["wdm"], writes=["wdm"], tag=f"ws{fc}")
                for jj in range(4):
                    s.op("T", (lambda jj: lambda t: t.transpose(
                            ps4w[0:1, 256:384], wdm[:, jj:jj + 1], eye32[:, :]))(jj),
                         reads=["wdm", "eye32"], writes=["ps4w"], tag=f"wfT{fc}_{jj}")
                    s.op("V", (lambda fc, jj: lambda v: v.tensor_copy(
                            wsd_row[0:1, fc * 512 + jj * 128:fc * 512 + (jj + 1) * 128],
                            ps4w[0:1, 256:384]))(fc, jj),
                         reads=["ps4w"], writes=["wsd_row"], tag=f"wfC{fc}_{jj}")


        s.emit_phase(nc, ctx)
        ph4.close()
        sc34.close()
        s.note_free()

        qt = sc5bc.enter_context(nc.sbuf_tensor("qt", [128, 44, T], FP8))
        sc5 = ExitStack()
        qx = sc5.enter_context(nc.sbuf_tensor("qx", [128, 16, T], FP8))
        ph5a = ExitStack()
        if phases >= 5:
            x2i = [ph5a.enter_context(nc.sbuf_tensor(f"x2i{i}", [128, D], F32)) for i in range(3)]
            xl2 = [ph5a.enter_context(nc.sbuf_tensor(f"xl2_{i}", [128, D], F32)) for i in range(3)]
            q8t = [ph5a.enter_context(nc.sbuf_tensor(f"q8t_{i}", [128, D], FP8)) for i in range(3)]
            st5 = [ph5a.enter_context(nc.sbuf_tensor(f"st5_{i}", [128, 8], F32)) for i in range(3)]
            ps5 = ph5a.enter_context(nc.psum_tensor("ps5", [128, 512], F32))
            ps5q = [ph5a.enter_context(nc.psum_tensor(f"ps5q_{i}", [128, 256], FP8)) for i in range(2)]
            s.buf("ps5", xread=True)
            s.buf("ps5q_0", xread=True)
            s.buf("ps5q_1", xread=True)

            for tc in range(16):
                i2 = tc % 3
                xi, xn = x2i[i2], f"x2i{i2}"
                sv, svn = st5[i2], f"st5_{i2}"
                xlv, xlvn = xl2[i2], f"xl2_{i2}"
                qv, qvn = q8t[i2], f"q8t_{i2}"
                s.op("Q", (lambda tc, xi: lambda g: g.dma_start(
                        out=xi[:, :], in_=x2_sc[tc]))(tc, xi),
                     reads=["x2_sc"], writes=[xn], tag=f"l2{tc}")
                s.op("V", (lambda xi, sv: lambda v: v.reduce_sum(sv[:, 0:1], xi[:, :], axis=AX))(xi, sv),
                     reads=[xn], writes=[svn], tag=f"a{tc}")
                s.op("S", (lambda xi, sv, xlv: lambda sc: sc.activation(
                        xlv[:, :], xi[:, :], AF.Square, accum_out=sv[:, 1:2]))(xi, sv, xlv),
                     reads=[xn], writes=[xlvn, svn], tag=f"b{tc}")

                def fst(sv):
                    def f(v):
                        v.tensor_scalar(sv[:, 2:3], sv[:, 0:1], 1.0 / D, None, op0=ALU.mult)
                        return v.tensor_scalar(sv[:, 3:4], sv[:, 1:2], 1.0 / D, None, op0=ALU.mult)
                    return f
                s.op("V", fst(sv), reads=[svn], writes=[svn], tag=f"c{tc}")
                s.op("V", (lambda sv: lambda v: v.tensor_tensor(
                        sv[:, 4:5], sv[:, 2:3], sv[:, 2:3], op=ALU.mult))(sv),
                     reads=[svn], writes=[svn], tag=f"c2{tc}")
                s.op("V", (lambda sv: lambda v: v.tensor_tensor(
                        sv[:, 3:4], sv[:, 3:4], sv[:, 4:5], op=ALU.subtract))(sv),
                     reads=[svn], writes=[svn], tag=f"c3{tc}")
                s.op("S", (lambda sv: lambda sc: sc.activation(
                        sv[:, 5:6], sv[:, 3:4], AF.Sqrt, bias=cpk[:, 19:20]))(sv),
                     reads=[svn, "cpk"], writes=[svn], tag=f"d{tc}")
                s.op("V", (lambda sv: lambda v: v.reciprocal(sv[:, 6:7], sv[:, 5:6]))(sv),
                     reads=[svn], writes=[svn], tag=f"e{tc}")
                s.op("V", (lambda xi, sv, xlv: lambda v: v.tensor_scalar(
                        xlv[:, :], xi[:, :], sv[:, 2:3], sv[:, 6:7],
                        op0=ALU.subtract, op1=ALU.mult))(xi, sv, xlv),
                     reads=[xn, svn], writes=[xlvn], tag=f"f{tc}")
                s.op("V", (lambda sv, xlv: lambda v: v.reduce_max(
                        sv[:, 0:1], xlv[:, :], axis=AX, apply_absolute_value=True))(sv, xlv),
                     reads=[xlvn], writes=[svn], tag=f"g{tc}")

                s.op("V", (lambda sv: lambda v: v.reciprocal(sv[:, 1:2], sv[:, 0:1]))(sv),
                     reads=[svn], writes=[svn], tag=f"h{tc}")
                s.op("V", (lambda sv: lambda v: v.tensor_scalar(
                        sv[:, 1:2], sv[:, 1:2], 224.0, None, op0=ALU.mult))(sv),
                     reads=[svn], writes=[svn], tag=f"h2{tc}")
                s.op("S", (lambda sv, xlv, qv: lambda sc: sc.activation(
                        qv[:, :], xlv[:, :], AF.Identity, scale=sv[:, 1:2]))(sv, xlv, qv),
                     reads=[xlvn, svn], writes=[qvn], tag=f"i{tc}")
                s.op("V", (lambda tc, sv: lambda v: v.tensor_copy(mxs[:, tc:tc + 1], sv[:, 0:1]))(tc, sv),
                     reads=[svn], writes=["mxs"], tag=f"j{tc}")
                for a in range(16):
                    ai = a % 2
                    s.op("T", (lambda a, qv, ai: lambda t: t.transpose(
                            s2ap(ps5q[ai]), qv[:, a * 128:(a + 1) * 128], eye8[:, :]))(a, qv, ai),
                         reads=[qvn, "eye8"], writes=[f"ps5q_{ai}"], tag=f"k{tc}_{a}")
                    if ai == 0:
                        s.op("S", (lambda a, tc, ai: lambda sc: sc.activation(
                                qx[:, a, tc * 128:(tc + 1) * 128], s2ap(ps5q[ai]), AF.Identity))(a, tc, ai),
                             reads=[f"ps5q_{ai}"], writes=["qx0"], tag=f"l{tc}_{a}")
                    else:
                        s.op("V", (lambda a, tc, ai: lambda v: v.tensor_copy(
                                qx[:, a, tc * 128:(tc + 1) * 128], s2ap(ps5q[ai])))(a, tc, ai),
                             reads=[f"ps5q_{ai}"], writes=["qx1"], tag=f"l{tc}_{a}")
        s.emit_phase(nc, ctx)
        ph5a.close()
        s.note_free()

        ph5b = ExitStack()
        if phases >= 6:
            wmac = [ph5b.enter_context(nc.sbuf_tensor(f"wmac{i}", [128, 128], F32)) for i in range(2)]
            wabs = ph5b.enter_context(nc.sbuf_tensor("wabs", [128, 8, 128], F32))
            wred = ph5b.enter_context(nc.sbuf_tensor("wred", [128, 2], F32))
            wrow = ph5b.enter_context(nc.sbuf_tensor("wrow", [1, 512], F32))
            wibc = ph5b.enter_context(nc.sbuf_tensor("wibc", [128, 2, 512], F32))
            wsbc = ph5b.enter_context(nc.sbuf_tensor("wsbc", [128, 4, 512], F32))
            gtmp = [ph5b.enter_context(nc.sbuf_tensor(f"gtmp{i}", [128, 512], F32)) for i in range(2)]
            tstg = [ph5b.enter_context(nc.sbuf_tensor(f"tstg{i}", [128, 512], F32)) for i in range(2)]
            tq8 = [ph5b.enter_context(nc.sbuf_tensor(f"tq8_{i}", [128, 512], FP8))
                   for i in range(2)]
            mtmp = ph5b.enter_context(nc.sbuf_tensor("mtmp", [128, 2], F32))
            gup = ExitStack()
            wgi = gup.enter_context(nc.sbuf_tensor("wgi", [128, 16, 128], F32))
            wui = gup.enter_context(nc.sbuf_tensor("wui", [128, 16, 128], F32))
            qwg = [gup.enter_context(nc.sbuf_tensor(f"qwg{i}", [128, 16, 512], FP8)) for i in range(2)]
            qwu = [gup.enter_context(nc.sbuf_tensor(f"qwu{i}", [128, 16, 512], FP8)) for i in range(2)]
            wrowt = gup.enter_context(nc.sbuf_tensor("wrowt", [1, 256], F32))
            psg = [gup.enter_context(nc.psum_tensor(f"psg{i}", [128, 512], F32)) for i in range(3)]
            psu = [gup.enter_context(nc.psum_tensor(f"psu{i}", [128, 512], F32)) for i in range(3)]
            psb = gup.enter_context(nc.psum_tensor("psb", [128, 512], F32))
            psw = gup.enter_context(nc.psum_tensor("psw", [128, 512], F32))
            for n in ("psg0", "psg1", "psg2", "psu0", "psu1", "psu2", "psb", "psw"):
                s.buf(n, xread=True)

            s.op("V", lambda v: v.memset(m2s[:, :], 0.0), writes=["m2s"], tag="m2z")
            s.op("V", lambda v: v.tensor_scalar(rxs[:, :], mxs[:, :], 1.0 / 112.0, None, op0=ALU.mult),
                 reads=["mxs"], writes=["rxs"], tag="rxs")

            def wprep_sub(icc, sub, mat, wi, win, qw, qwn, slot, off, stage, wm, wmn):
                ib = icc % 2
                """Load one [2048f x 128ic] fp32 sub-chunk, compute its ws,
                bcast scale slices, quantize into qw[:, :, sub*128...]."""
                if stage == 2:
                    pass
                else:
                    s.op("G", (lambda off, wi: lambda g: pay_dma(
                            g, wi[:, :, :], off, [[16 * 512, 128], [512, 16], [1, 128]]))(off, wi),
                         reads=["pay"], writes=[win], tag=f"lw{mat}{icc}_{sub}")

                for rnd in ([] if stage == 2 else range(2)):
                    s.op("S", (lambda wi, rnd: lambda sc: sc.activation(
                            wabs[:, :, :], wi[:, 8 * rnd:8 * rnd + 8, :], AF.Abs))(wi, rnd),
                         reads=[win, wmn] if rnd else [win], writes=["wabs"],
                         tag=f"wx0{mat}{icc}_{sub}_{rnd}")
                    for lvl, st in ((1, 1), (2, 2)):
                        def ftree(st):
                            def f(v):
                                r = None
                                for a in range(0, 8, 2 * st):
                                    r = v.tensor_tensor(wabs[:, a, :], wabs[:, a, :],
                                                        wabs[:, a + st, :], op=ALU.max)
                                return r
                            return f
                        s.op("V", ftree(st), reads=["wabs"], writes=["wabs"],
                             tag=f"wx{lvl}{mat}{icc}_{sub}_{rnd}")
                    if rnd == 0:
                        s.op("V", (lambda wm: lambda v: v.tensor_tensor(wm[:, :], wabs[:, 0, :],
                                                                        wabs[:, 4, :], op=ALU.max))(wm),
                             reads=["wabs"], writes=[wmn], tag=f"wx3{mat}{icc}_{sub}_0")
                    else:
                        def flast(v):
                            v.tensor_tensor(wabs[:, 0, :], wabs[:, 0, :], wabs[:, 4, :], op=ALU.max)
                            return None
                        s.op("V", lambda v: v.tensor_tensor(wabs[:, 0, :], wabs[:, 0, :],
                                                            wabs[:, 4, :], op=ALU.max),
                             reads=["wabs"], writes=["wabs"], tag=f"wx3{mat}{icc}_{sub}_1")
                        s.op("V", (lambda wm: lambda v: v.tensor_tensor(wm[:, :], wm[:, :],
                                                                          wabs[:, 0, :], op=ALU.max))(wm),
                             reads=["wabs", wmn], writes=[wmn], tag=f"wx4{mat}{icc}_{sub}")
                if stage == 1:
                    return
                s.op("T", (lambda wm: lambda t: t.transpose(psw[:, 0:128], wm[:, :], eye32[:, :]))(wm),
                     reads=[wmn, "eye32"], writes=["psw"], tag=f"wt{mat}{icc}_{sub}")

                s.op("V", lambda v: v.reduce_max(wred[:, 0:1], psw[:, 0:128], axis=AX),
                     reads=["psw"], writes=["wred"], tag=f"wr{mat}{icc}_{sub}")
                s.op("V", lambda v: v.tensor_scalar(wred[:, 1:2], wred[:, 0:1], 1.0 / 448.0, None,
                                                    op0=ALU.mult),
                     reads=["wred"], writes=["wred"], tag=f"wr2{mat}{icc}_{sub}")
                s.op("T", lambda t: t.transpose(psw[0:1, 128:256], wred[:, 1:2], eye32[:, :]),
                     reads=["wred", "eye32"], writes=["psw"], tag=f"wfT{mat}{icc}_{sub}")
                s.op("V", (lambda sub: lambda v: v.tensor_copy(
                        wrow[0:1, sub * 128:(sub + 1) * 128], psw[0:1, 128:256]))(sub),
                     reads=["psw"], writes=["wrow"], tag=f"wfC{mat}{icc}_{sub}")

                s.op("V", (lambda sub: lambda v: v.reciprocal(
                        wrowt[0:1, 0:128], wrow[0:1, sub * 128:(sub + 1) * 128]))(sub),
                     reads=["wrow"], writes=["wrowt"], tag=f"wi{mat}{icc}_{sub}")
                s.op("V", lambda v: v.tensor_scalar(wrowt[0:1, 128:256], wrowt[0:1, 0:128], 0.5, None,
                                                    op0=ALU.mult),
                     reads=["wrowt"], writes=["wrowt"], tag=f"wi2{mat}{icc}_{sub}")

                def fbc(t):
                    t.matmul(psb[:, 0:128], ones32[0:1, 0:128], wrowt[0:1, 128:256], start=True, stop=True)
                    return t.matmul(psb[:, 128:256], ones32[0:1, 0:128],
                                    wrow[0:1, sub * 128:(sub + 1) * 128], start=True, stop=True)
                s.op("T", fbc, reads=["wrowt", "wrow", "ones32"], writes=["psb"], tag=f"wb{mat}{icc}_{sub}")

                sl = slot * 2 + ib

                def fev(v):
                    v.tensor_copy(wibc[:, slot, sub * 128:(sub + 1) * 128], psb[:, 0:128])
                    return v.tensor_copy(wsbc[:, sl, sub * 128:(sub + 1) * 128], psb[:, 128:256])
                s.op("V", fev, reads=["psb"], writes=["wibc", "wsbc"], tag=f"we{mat}{icc}_{sub}")

                def fq(wi, qw, sub, slot):
                    def f(v):
                        r = None
                        for a in range(16):
                            r = v.tensor_tensor(qw[:, a, sub * 128:(sub + 1) * 128], wi[:, a, :],
                                                wibc[:, slot, sub * 128:(sub + 1) * 128], op=ALU.mult)
                        return r
                    return f
                s.op("P", fq(wi, qw, sub, slot), reads=[win, "wibc"], writes=[qwn],
                     tag=f"wq{mat}{icc}_{sub}")

            def wprep_piece(icc2, j, stage=0):
                ib2 = icc2 % 2
                sub, mat = j // 2, ("g", "u")[j % 2]
                wm, wmn = wmac[j % 2], f"wmac{j % 2}"
                if mat == "g":
                    off = GRAN_OFF[("wg", icc2)] + sub * 128
                    wprep_sub(icc2, sub, "g", wgi, "wgi", qwg[ib2], f"qwg{ib2}", 0, off, stage, wm, wmn)
                else:
                    off = GRAN_OFF[("wu", icc2)] + sub * 128
                    wprep_sub(icc2, sub, "u", wui, "wui", qwu[ib2], f"qwu{ib2}", 1, off, stage, wm, wmn)

            for j in range(8):
                wprep_piece(0, j)
            for icc in range(11):
                ib = icc % 2
                def tail(tc):
                    pt = tc % 3
                    i2 = tc % 2
                    s.op("V", (lambda ib, i2: lambda v: v.tensor_tensor(
                            gtmp[i2][:, :], gtmp[i2][:, :], wsbc[:, 2 + ib, :], op=ALU.mult))(ib, i2),
                         reads=[f"gtmp{i2}", "wsbc"], writes=[f"gtmp{i2}"], tag=f"ue{icc}_{tc}")
                    s.op("V", (lambda tc, pt, i2: lambda v: v.scalar_tensor_tensor(
                            tstg[i2][:, :], psu[pt][:, :], rxs[:, tc:tc + 1], gtmp[i2][:, :],
                            op0=ALU.mult, op1=ALU.mult))(tc, pt, i2),
                         reads=[f"psu{pt}", "rxs", f"gtmp{i2}"], writes=[f"tstg{i2}"],
                         tag=f"tm{icc}_{tc}")
                    s.op("V", (lambda tc, i2: lambda v: v.reduce_max(
                            mtmp[:, i2:i2 + 1], tstg[i2][:, :], axis=AX,
                            apply_absolute_value=True))(tc, i2),
                         reads=[f"tstg{i2}"], writes=["mtmp"], tag=f"mxa{icc}_{tc}")
                    s.op("V", (lambda tc, i2: lambda v: v.tensor_tensor(
                            m2s[:, tc:tc + 1], m2s[:, tc:tc + 1], mtmp[:, i2:i2 + 1], op=ALU.max))(tc, i2),
                         reads=["mtmp", "m2s"], writes=["m2s"], tag=f"mxb{icc}_{tc}")
                    s.op("Q", (lambda icc, tc, i2: lambda g: g.dma_start(
                            out=t_sc.ap()[icc, :, tc, :], in_=tstg[i2][:, :]))(icc, tc, i2),
                         reads=[f"tstg{i2}"], writes=["t_sc"], tag=f"ts{icc}_{tc}")

                for tc in range(16):
                    pt = tc % 3
                    i2 = tc % 2
                    def fg(tc, ib, pt):
                        def f(t):
                            r = None
                            for a in range(8):
                                r = t.matmul(psg[pt][:, :], qx[:, 2 * a:2 * a + 2, tc * 128:(tc + 1) * 128],
                                             qwg[ib][:, 2 * a:2 * a + 2, :], start=(a == 0), stop=(a == 7),
                                             perf_mode=PM.DoubleRow)
                            return r
                        return f
                    s.op("T", fg(tc, ib, pt), reads=["qx0", "qx1", f"qwg{ib}"], writes=[f"psg{pt}"],
                         tag=f"gm{icc}_{tc}")

                    def fu(tc, ib, pt):
                        def f(t):
                            r = None
                            for a in range(8):
                                r = t.matmul(psu[pt][:, :], qx[:, 2 * a:2 * a + 2, tc * 128:(tc + 1) * 128],
                                             qwu[ib][:, 2 * a:2 * a + 2, :], start=(a == 0), stop=(a == 7),
                                             perf_mode=PM.DoubleRow)
                            return r
                        return f
                    s.op("T", fu(tc, ib, pt), reads=["qx0", "qx1", f"qwu{ib}"], writes=[f"psu{pt}"],
                         tag=f"um{icc}_{tc}")
                    s.op("V", (lambda tc, ib, pt, i2: lambda v: v.scalar_tensor_tensor(
                            gtmp[i2][:, :], psg[pt][:, :], rxs[:, tc:tc + 1], wsbc[:, ib, :],
                            op0=ALU.mult, op1=ALU.mult))(tc, ib, pt, i2),
                         reads=[f"psg{pt}", "rxs", "wsbc"], writes=[f"gtmp{i2}"], tag=f"ge{icc}_{tc}")
                    if tc % 2 == 1:
                        def gsp(sc):
                            sc.activation(gtmp[0][:, :], gtmp[0][:, :], AF.Silu)
                            return sc.activation(gtmp[1][:, :], gtmp[1][:, :], AF.Silu)
                        s.op("S", gsp, reads=["gtmp0", "gtmp1"],
                             writes=["gtmp0", "gtmp1"], tag=f"gs{icc}_{tc}")
                        tail(tc - 1)
                        tail(tc)
                    if icc + 1 < 11 and tc % 2 == 1:
                        j = tc // 2
                        wprep_piece(icc + 1, j, stage=1)
                        if j >= 1:
                            wprep_piece(icc + 1, j - 1, stage=2)
                if icc + 1 < 11:
                    wprep_piece(icc + 1, 7, stage=2)
            s.emit_phase(nc, ctx)
            gup.close()
            s.note_free()
            psw8 = [ph5b.enter_context(nc.psum_tensor(f"psw8_{i}", [128, 256], FP8)) for i in range(8)]
            for i in range(8):
                s.buf(f"psw8_{i}", xread=True)

            s.op("V", lambda v: v.reciprocal(tqs[:, :], m2s[:, :]),
                 reads=["m2s"], writes=["tqs"], tag="m2inv0")
            s.op("V", lambda v: v.tensor_scalar(tqs[:, :], tqs[:, :], 224.0, None, op0=ALU.mult),
                 reads=["tqs"], writes=["tqs"], tag="m2inv1")
            s.op("V", lambda v: v.tensor_scalar(m2f[:, :], m2s[:, :], 1.0 / 112.0, None, op0=ALU.mult),
                 reads=["m2s"], writes=["m2f"], tag="m2fold")

            tri = [ph5b.enter_context(nc.sbuf_tensor(f"tri{i}", [128, 8, 512], F32)) for i in range(2)]
            for icc in range(11):
                for half in range(2):
                    i2 = half
                    ti, tn = tri[i2], f"tri{i2}"
                    s.op("Q", (lambda icc, half, ti: lambda g: g.dma_start(
                            out=ti[:, :, :], in_=t_sc.ap()[icc, :, half * 8:(half + 1) * 8, :]))(icc, half, ti),
                         reads=["t_sc"], writes=[tn], tag=f"tl{icc}_{half}")
                    for tcl in range(8):
                        tc = half * 8 + tcl
                        kt8 = (icc * 16 + tc) % 2
                        bs = 4 * kt8
                        s.op("S", (lambda ti, tcl, tc, kt8: lambda sc: sc.activation(
                                tq8[kt8][:, :], ti[:, tcl, :], AF.Identity,
                                scale=tqs[:, tc:tc + 1]))(ti, tcl, tc, kt8),
                             reads=[tn, "tqs"], writes=[f"tq8_{kt8}"], tag=f"tq{icc}_{tc}")
                        for j in range(4):
                            s.op("T", (lambda j, kt8, bs: lambda t: t.transpose(
                                    s2ap(psw8[bs + j]), tq8[kt8][:, j * 128:(j + 1) * 128],
                                    eye8[:, :]))(j, kt8, bs),
                                 reads=[f"tq8_{kt8}", "eye8"], writes=[f"psw8_{bs + j}"],
                                 tag=f"tt{icc}_{tc}_{j}")

                        def fte(icc, tc, bs):
                            def f(v):
                                r = None
                                for j in range(4):
                                    r = v.tensor_copy(qt[:, 4 * icc + j, tc * 128:(tc + 1) * 128],
                                                      s2ap(psw8[bs + j]))
                                return r
                            return f
                        s.op("V", fte(icc, tc, bs),
                             reads=[f"psw8_{bs + j}" for j in range(4)],
                             writes=["qt"], tag=f"te{icc}_{tc}")
        s.emit_phase(nc, ctx)
        ph5b.close()
        sc5.close()
        s.note_free()

        ph5c = ExitStack()
        if phases >= 7:
            wdi2 = [ph5c.enter_context(nc.sbuf_tensor("wdi2_0", [128, 11, 512], F32))]
            wdq = [ph5c.enter_context(nc.sbuf_tensor(f"wdq{i}", [128, 44, 512], FP8)) for i in range(2)]
            ibcd = [ph5c.enter_context(nc.sbuf_tensor(f"ibcd{i}", [128, 512], F32)) for i in range(2)]
            sbcd = [ph5c.enter_context(nc.sbuf_tensor(f"sbcd{i}", [128, 512], F32)) for i in range(2)]
            rowt = ph5c.enter_context(nc.sbuf_tensor("rowt", [1, 512], F32))
            x2r = [ph5c.enter_context(nc.sbuf_tensor(f"x2r{i}", [128, 512], F32)) for i in range(2)]
            outs = [ph5c.enter_context(nc.sbuf_tensor(f"outs{i}", [128, 512], F32)) for i in range(2)]
            outt = [ph5c.enter_context(nc.sbuf_tensor(f"outt{i}", [128, 512], F32)) for i in range(2)]
            psd = [ph5c.enter_context(nc.psum_tensor(f"psd{i}", [128, 512], F32)) for i in range(4)]
            psb2 = ph5c.enter_context(nc.psum_tensor("psb2", [128, 512], F32))
            for i in range(4):
                s.buf(f"psd{i}", xread=True)
            s.buf("psb2", xread=True)

            for fc in range(4):
                fcp = fc % 2
                s.op("V", (lambda fc: lambda v: v.reciprocal(
                        rowt[0:1, :], wsd_row[0:1, fc * 512:(fc + 1) * 512]))(fc),
                     reads=["wsd_row"], writes=["rowt"], tag=f"div{fc}")
                s.op("V", lambda v: v.tensor_scalar(rowt[0:1, :], rowt[0:1, :], 0.5, None, op0=ALU.mult),
                     reads=["rowt"], writes=["rowt"], tag=f"div2{fc}")
                s.op("T", lambda t: t.matmul(psb2[:, :], ones32[0:1, 0:128], rowt[0:1, :],
                                             start=True, stop=True),
                     reads=["rowt", "ones32"], writes=["psb2"], tag=f"dbc{fc}")
                s.op("V", (lambda fcp: lambda v: v.tensor_copy(ibcd[fcp][:, :], psb2[:, :]))(fcp),
                     reads=["psb2"], writes=[f"ibcd{fcp}"], tag=f"dbe{fc}")
                s.op("V", (lambda fc: lambda v: v.tensor_copy(
                        rowt[0:1, :], wsd_row[0:1, fc * 512:(fc + 1) * 512]))(fc),
                     reads=["wsd_row"], writes=["rowt"], tag=f"dsv{fc}")
                s.op("T", lambda t: t.matmul(psb2[:, :], ones32[0:1, 0:128], rowt[0:1, :],
                                             start=True, stop=True),
                     reads=["rowt", "ones32"], writes=["psb2"], tag=f"dbc2{fc}")
                s.op("V", (lambda fcp: lambda v: v.tensor_copy(sbcd[fcp][:, :], psb2[:, :]))(fcp),
                     reads=["psb2"], writes=[f"sbcd{fcp}"], tag=f"dbe2{fc}")
                for g4 in range(4):
                    gp = 0
                    off = GRAN_OFF[("wd", fc * 4 + g4)]
                    s.op("G", (lambda off: lambda g: pay_dma(
                            g, wdi2[0][:, :, :], off, [[512, 128], [128 * 512, 11], [1, 512]]))(off),
                         reads=["pay"], writes=["wdi2_0"], tag=f"dl{fc}_{g4}")

                    def fqd(g4, gp, fcp):
                        def f(v):
                            r = None
                            for j in range(11):
                                r = v.tensor_tensor(wdq[fcp][:, g4 * 11 + j, :], wdi2[gp][:, j, :],
                                                    ibcd[fcp][:, :], op=ALU.mult)
                            return r
                        return f
                    s.op("V", fqd(g4, gp, fcp), reads=[f"wdi2_{gp}", f"ibcd{fcp}"],
                         writes=[f"wdq{fcp}"], tag=f"dq{fc}_{g4}")
                for tc in range(16):
                    pi = tc % 4
                    i2 = tc % 2

                    def fdm(fc, tc, pi, fcp):
                        def f(t):
                            r = None
                            for icc in range(22):
                                r = t.matmul(psd[pi][:, :],
                                             qt[:, 2 * icc:2 * icc + 2, tc * 128:(tc + 1) * 128],
                                             wdq[fcp][:, 2 * icc:2 * icc + 2, :],
                                             start=(icc == 0), stop=(icc == 21),
                                             perf_mode=PM.DoubleRow)
                            return r
                        return f
                    s.op("T", fdm(fc, tc, pi, fcp), reads=["qt", f"wdq{fcp}"], writes=[f"psd{pi}"],
                         tag=f"dm{fc}_{tc}")
                    s.op("Q", (lambda fc, tc, i2: lambda g: g.dma_start(
                            out=x2r[i2][:, :],
                            in_=x2_sc.ap()[tc, :, fc * 512:(fc + 1) * 512]))(fc, tc, i2),
                         reads=["x2_sc"], writes=[f"x2r{i2}"], tag=f"dx{fc}_{tc}")

                    s.op("V", (lambda tc, pi, i2, fcp: lambda v: v.scalar_tensor_tensor(
                            outt[i2][:, :], psd[pi][:, :], m2f[:, tc:tc + 1],
                            sbcd[fcp][:, :], op0=ALU.mult, op1=ALU.mult))(tc, pi, i2, fcp),
                         reads=[f"psd{pi}", "m2f", f"sbcd{fcp}"], writes=[f"outt{i2}"], tag=f"deA{fc}_{tc}")
                    s.op("V", (lambda pi, i2: lambda v: v.tensor_tensor(
                            outs[i2][:, :], outt[i2][:, :], x2r[i2][:, :], op=ALU.add))(pi, i2),
                         reads=[f"outt{i2}", f"x2r{i2}"], writes=[f"outs{i2}"], tag=f"deB{fc}_{tc}")
                    s.op("G", (lambda fc, tc, i2: lambda g: g.dma_start(
                            out=out_d.ap()[tc * 128:(tc + 1) * 128, fc * 512:(fc + 1) * 512],
                            in_=outs[i2][:, :]))(fc, tc, i2),
                         reads=[f"outs{i2}"], writes=["outdone"], tag=f"do{fc}_{tc}")
            s.op("G", lambda g: g.nop(), reads=["outdone"], tag="final", barrier=True)
        s.emit_phase(nc, ctx)
        ph5c.close()
        sc5bc.close()
        s.note_free()

        s.emit_phase(nc, ctx)
    return nc


def _pack_payload(x, wqkvu, wout, wg_p, wu_p, wd):
    pay = np.zeros((8, S_SH), np.float32)
    flat = pay.reshape(-1)

    def put(key, arr):
        off = GRAN_OFF[key]
        flat[off:off + arr.size] = arr.ravel()

    wq = np.ascontiguousarray(wqkvu.T).reshape(16, 128, 16, 512).transpose(2, 1, 0, 3)
    wqk = np.ascontiguousarray(wq).reshape(1, N_WQK * GR_E)
    wo = np.ascontiguousarray(wout.T).reshape(16, 128, D).transpose(1, 0, 2)
    wo4 = wo.reshape(128, 16, 4, 512).transpose(2, 0, 1, 3)
    for i in range(4):
        put(("wout", i), wo4[i])
    for base, w in (("wg", wg_p), ("wu", wu_p)):
        wt = np.ascontiguousarray(w.T).reshape(16, 128, 11, 512).transpose(2, 1, 0, 3)
        for i in range(11):
            put((base, i), wt[i])
    wdt = np.ascontiguousarray(wd.T).reshape(44, 128, 4, 512).transpose(2, 0, 1, 3)
    for fc in range(4):
        for g4 in range(4):
            put(("wd", fc * 4 + g4), wdt[fc, 11 * g4:11 * (g4 + 1)])
    return pay, wqk


def _consts():
    arrq = np.broadcast_to(np.arange(1, 257, dtype=np.float32), (128, 256)).copy()
    p = np.arange(128, dtype=np.float32)
    kdc = np.stack([255.0 - p, 127.0 - p], axis=1).astype(np.float32)
    cm = np.empty((2, 128, 256), np.float32)
    for st in range(2):
        sidx = st * 128 + np.arange(128)[:, None]
        tidx = np.arange(256)[None, :]
        d = tidx - sidx
        cm[st] = np.where(d >= 0, d, 1.0e9).astype(np.float32)
    eye32 = np.eye(128, dtype=np.float32)
    eyebf = np.eye(128, dtype=ml_dtypes.bfloat16)
    eye8 = np.eye(128, dtype=ml_dtypes.float8_e4m3)
    onesbf = np.ones((128, 128), ml_dtypes.bfloat16)
    ones32 = np.ones((1, 128), np.float32)
    return arrq, kdc, cm, eye32, eyebf, eye8, onesbf, ones32


def kernel(L0_attn_x, L0_attn_slope_rate, L0_attn_W_qkvu, L0_attn_W_out,
           L0_ffn_w_gate, L0_ffn_w_up, L0_ffn_w_down,
           L0_ffn_up_reorder_index, L0_ffn_down_reorder_index):
    x = np.asarray(L0_attn_x, np.float32)
    wqkvu = np.asarray(L0_attn_W_qkvu, np.float32)
    wout = np.asarray(L0_attn_W_out, np.float32)
    didx = np.asarray(L0_ffn_down_reorder_index).astype(np.int64)
    wg_p = np.take(np.asarray(L0_ffn_w_gate, np.float32), didx, axis=0)
    wu_p = np.take(np.asarray(L0_ffn_w_up, np.float32), didx, axis=0)
    wd = np.asarray(L0_ffn_w_down, np.float32)
    slope = np.asarray(L0_attn_slope_rate, np.float32).reshape(1, 16)
    slope_b = np.broadcast_to(slope, (128, 16)).copy()

    pay, wqk = _pack_payload(x, wqkvu, wout, wg_p, wu_p, wd)
    arrq, kdc, cm, eye32, eyebf, eye8, onesbf, ones32 = _consts()

    nc = build_program()
    in_maps = []
    for c in range(8):
        seq, half = c // 2, c % 2
        x_own = np.ascontiguousarray(x[seq, half * T:(half + 1) * T], np.float32)
        x_prev = np.ascontiguousarray(x[seq, 0:T], np.float32)
        cpkv = np.zeros((128, 24), np.float32)
        cpkv[:, 0:2] = kdc
        cpkv[:, 2:18] = slope_b
        cpkv[:, 18] = 1.0 if half == 1 else 0.0
        cpkv[:, 19] = EPS1
        cpkv[:, 20] = EPS2
        in_maps.append({
            "shard": pay[c:c + 1], "wqk": wqk, "xin": x_own, "xpr": x_prev,
            "cpk": cpkv,
            "arrq": arrq, "cmask": cm,
            "eye32": eye32, "eyebf": eyebf, "eye8": eye8,
            "onesbf": onesbf, "ones32": ones32,
        })
    res = run_bass_kernel_spmd(nc, in_maps, core_ids=list(range(8)))
    out = np.empty((B, N, D), np.float32)
    for c in range(8):
        seq, half = c // 2, c % 2
        out[seq, half * T:(half + 1) * T] = res.results[c]["out"]
    return out


# version 14
# speedup vs baseline: 1.0842x; 1.0081x over previous
import os
os.environ["NEURON_SCRATCHPAD_PAGE_SIZE"] = "512"
import numpy as np
import ml_dtypes
from contextlib import ExitStack

import concourse.bass as bass
import concourse.mybir as mybir
from concourse.bass_utils import run_bass_kernel_spmd

F32 = mybir.dt.float32
BF16 = mybir.dt.bfloat16
FP8 = mybir.dt.float8e4
I32 = mybir.dt.int32
AF = mybir.ActivationFunctionType
ALU = mybir.AluOpType
AX = mybir.AxisListType.X
PM = mybir.MatmulPerfMode

B, N, D = 4, 4096, 2048
H, DH = 16, 128
BLK = 256
T = 2048
NB = T // BLK
IC = 5632
EPS1 = 1e-5
EPS2 = 1e-6

_x_own_elems = 0
GR_E = 128 * 16 * 512
N_WQK = 16
_GRAN = []
for _i in range(4):
    _GRAN.append(("wout", _i, GR_E))
for _i in range(11):
    _GRAN.append(("wg", _i, GR_E))
for _i in range(11):
    _GRAN.append(("wu", _i, GR_E))
for _i in range(16):
    _GRAN.append(("wd", _i, 44 * 128 * 128))
S_SH = 5_636_096
_big = [g for g in _GRAN if g[0] != "wd"]
_wdg = [g for g in _GRAN if g[0] == "wd"]
assert len(_big) == 26 and len(_wdg) == 16
_assign = []
_bi = 0
for _s in range(8):
    _nb = 4 if _s < 2 else 3
    _assign.append(_big[_bi:_bi + _nb] + _wdg[2 * _s:2 * _s + 2])
    _bi += _nb
GRAN_OFF = {}
for _s, _gl in enumerate(_assign):
    _off = _x_own_elems
    for _g in _gl:
        GRAN_OFF[(_g[0], _g[1])] = _s * S_SH + _off
        _off += _g[2]
    assert _off <= S_SH
PAY_TOT = 8 * S_SH


class _Buf:
    __slots__ = ("name", "last_write", "readers", "xread", "fence")

    def __init__(self, name, xread, fence=()):
        self.name = name
        self.last_write = None
        self.readers = []
        self.xread = xread
        self.fence = list(fence)


class _Op:
    __slots__ = ("engine", "emit", "inc_amount", "needs_inc", "cum", "deps",
                 "tag", "barrier", "phase")

    def __init__(self, engine, emit, inc_amount, tag, phase):
        self.engine = engine
        self.emit = emit
        self.inc_amount = inc_amount
        self.needs_inc = False
        self.cum = -1
        self.deps = []
        self.tag = tag
        self.barrier = False
        self.phase = phase


class Sched:
    SECTION = {"T": "tensor", "V": "vector", "S": "scalar", "Q": "sync",
               "G": "gpsimd", "C": "gpsimd", "P": "gpsimd", "R": "scalar"}

    def __init__(self):
        self.pending = {k: [] for k in self.SECTION.values()}
        self.bufs = {}
        self.phase = 0
        self.tails = {}
        self.free_fence = {}
        self.sems = None
        self.cum = {k: 0 for k in ("T", "V", "S", "Q", "G", "C", "P", "R")}
        self.waited = {sec: {k: 0 for k in ("T", "V", "S", "Q", "G", "C", "P", "R")}
                       for sec in set(self.SECTION.values())}
        self.prev_dma = {"Q": 0, "G": 0, "R": 0}

    def buf(self, name, xread=False):
        b = self.bufs.get(name)
        if b is None:
            b = self.bufs[name] = _Buf(name, xread, self.free_fence.values())
        if xread:
            b.xread = True
        return b

    def op(self, engine, emit, reads=(), writes=(), tag="", barrier=False):
        rd = [self.buf(b) if isinstance(b, str) else b for b in reads]
        wr = [self.buf(b) if isinstance(b, str) else b for b in writes]
        o = _Op(engine, emit, 16 if engine in "QGR" else 1, tag, self.phase)
        o.barrier = barrier
        if engine in "QGCR" and not barrier:
            o.needs_inc = True
        deps = []
        for b in rd:
            if b.last_write is not None:
                deps.append(b.last_write)
            if b.xread:
                for r in b.readers:
                    if r.engine != engine:
                        deps.append(r)
        for b in wr:
            deps.extend(b.readers)
            if b.last_write is not None:
                deps.append(b.last_write)
            elif not b.readers:
                deps.extend(b.fence)
        res = []
        for d in deps:
            if d is o:
                continue
            if d.phase == self.phase:
                if d.barrier:
                    raise RuntimeError(f"dep on barrier op {d.tag} <- {tag}")
                d.needs_inc = True
                res.append(d)
            elif d.needs_inc:
                res.append(d)
            else:
                res.extend(self.tails.values())
        seen = set()
        o.deps = [d for d in res if not (id(d) in seen or seen.add(id(d)))]
        for b in rd:
            b.readers.append(o)
        for b in wr:
            b.last_write = o
            b.readers = []
        self.pending[self.SECTION[engine]].append(o)
        return o

    def emit_phase(self, nc, ctx):
        if self.sems is None:
            self.sems = {k: ctx.enter_context(nc.semaphore(f"sem_{k}"))
                         for k in ("T", "V", "S", "Q", "G", "C", "P", "R")}
        for sec, ops in self.pending.items():
            if ops and not ops[-1].barrier:
                ops[-1].needs_inc = True
        for sec, ops in self.pending.items():
            for o in ops:
                if o.needs_inc:
                    self.cum[o.engine] += o.inc_amount
                    o.cum = self.cum[o.engine]
        sched = self

        def make_body(section):
            def body(eng):
                waited = sched.waited[section]
                for o in sched.pending[section]:
                    if o.engine in sched.prev_dma:
                        pd = sched.prev_dma[o.engine]
                        if pd > waited[o.engine]:
                            eng.wait_ge(sched.sems[o.engine], pd)
                            waited[o.engine] = pd
                    for d in o.deps:
                        if d.cum <= 0:
                            raise RuntimeError(f"dep no inc: {d.tag}<-{o.tag}")
                        if d.cum > waited[d.engine]:
                            eng.wait_ge(sched.sems[d.engine], d.cum)
                            waited[d.engine] = d.cum
                    inst = o.emit(eng)
                    if o.needs_inc:
                        if inst is None:
                            raise RuntimeError(f"{o.tag}: inc w/o inst")
                        inst.then_inc(sched.sems[o.engine], o.inc_amount)
                        if o.engine in sched.prev_dma:
                            sched.prev_dma[o.engine] = o.cum
            return body

        active = [sec for sec in ("sync", "gpsimd", "vector", "scalar", "tensor")
                  if self.pending[sec]]
        if active:
            with nc.Block() as blk:
                for sec in active:
                    getattr(blk, sec)(make_body(sec))
        for sec, ops in self.pending.items():
            if ops:
                for o in reversed(ops):
                    if o.needs_inc:
                        self.tails[o.engine] = o
                        break
        self.pending = {k: [] for k in self.SECTION.values()}
        self.phase += 1

    def note_free(self):
        self.free_fence = dict(self.tails)


def build_program(phases=99, probes=()):
    nc = bass.Bass("TRN2", target_bir_lowering=False)

    shard = nc.dram_tensor("shard", [1, S_SH], F32, kind="ExternalInput")
    wqk_d = nc.dram_tensor("wqk", [1, N_WQK * GR_E], F32, kind="ExternalInput")
    xin_d = nc.dram_tensor("xin", [T, D], F32, kind="ExternalInput")
    xpr_d = nc.dram_tensor("xpr", [T, D], F32, kind="ExternalInput")
    cpk_d = nc.dram_tensor("cpk", [128, 24], F32, kind="ExternalInput")
    arrq_d = nc.dram_tensor("arrq", [128, 256], F32, kind="ExternalInput")
    cmask_d = nc.dram_tensor("cmask", [2, 128, 256], F32, kind="ExternalInput")
    eye32_d = nc.dram_tensor("eye32", [128, 128], F32, kind="ExternalInput")
    eyebf_d = nc.dram_tensor("eyebf", [128, 128], BF16, kind="ExternalInput")
    eye8_d = nc.dram_tensor("eye8", [128, 128], FP8, kind="ExternalInput")
    onesbf_d = nc.dram_tensor("onesbf", [128, 128], BF16, kind="ExternalInput")
    ones32_d = nc.dram_tensor("ones32", [1, 128], F32, kind="ExternalInput")
    out_d = nc.dram_tensor("out", [T, D], F32, kind="ExternalOutput")

    shard_b = nc.dram_tensor("shard_b", [1, S_SH], F32)
    pay = nc.dram_tensor("pay", [8, S_SH], F32, addr_space="Shared")

    q_sc = nc.dram_tensor("q_sc", [H, 128, T], BF16)
    k_sc = nc.dram_tensor("k_sc", [H, 128, T], BF16)
    u_sc = nc.dram_tensor("u_sc", [H, 128, T], BF16)
    kp_sc = nc.dram_tensor("kp_sc", [H, 128, T], BF16)
    v_sc = nc.dram_tensor("v_sc", [4, 16, 128, 512], BF16)
    vp_sc = nc.dram_tensor("vp_sc", [4, 16, 128, 512], BF16)
    x2_sc = nc.dram_tensor("x2_sc", [16, 128, D], F32)
    t_sc = nc.dram_tensor("t_sc", [11, 128, 16, 512], F32)

    s = Sched()

    with ExitStack() as ctx:
        e = ctx.enter_context

        cpk = e(nc.sbuf_tensor("cpk_s", [128, 24], F32))
        eye32 = e(nc.sbuf_tensor("eye32_s", [128, 128], F32))
        eyebf = e(nc.sbuf_tensor("eyebf_s", [128, 128], BF16))
        eye8 = e(nc.sbuf_tensor("eye8_s", [128, 128], FP8))
        onesbf = e(nc.sbuf_tensor("onesbf_s", [128, 128], BF16))
        ones32 = e(nc.sbuf_tensor("ones32_s", [1, 128], F32))
        mxs = e(nc.sbuf_tensor("mxs", [128, 16], F32))
        m2s = e(nc.sbuf_tensor("m2s", [128, 16], F32))
        m2f = e(nc.sbuf_tensor("m2f", [128, 16], F32))
        tqs = e(nc.sbuf_tensor("tqs", [128, 16], F32))
        rxs = e(nc.sbuf_tensor("rxs", [128, 16], F32))

        for nm, dst, src in (("cpk", cpk, cpk_d),
                             ("eye32", eye32, eye32_d),
                             ("eyebf", eyebf, eyebf_d), ("eye8", eye8, eye8_d),
                             ("onesbf", onesbf, onesbf_d), ("ones32", ones32, ones32_d)):
            s.op("Q", (lambda dst, src: lambda g: g.dma_start(out=dst[...], in_=src[...]))(dst, src),
                 writes=[nm], tag=f"ld_{nm}")

        s.op("G", lambda g: g.dma_start(out=shard_b[:, :], in_=shard[:, :]),
             writes=["shard_b"], tag="bounce")
        s.op("C", lambda g: g.collective_compute(
                "AllGather", ALU.bypass, replica_groups=[list(range(8))],
                ins=[shard_b.ap().opt()], outs=[pay.ap().opt()]),
             reads=["shard_b"], writes=["pay"], tag="ag")


        def pay_dma(eng, dst_ap, elem_off, dims, base=None):
            return eng.dma_start(out=dst_ap, in_=bass.AP(base if base is not None else pay,
                                                         elem_off, dims))

        def s2ap(guard, n=128):
            h = guard[0:1, 0:1].tensor
            return bass.AP(h, 0, [[2 * n, 128], [2, n]])

        sc12 = ExitStack()
        xl = sc12.enter_context(nc.sbuf_tensor("xl", [128, 16, T], BF16))
        xlp = sc12.enter_context(nc.sbuf_tensor("xlp", [128, 16, T], BF16))
        ph2 = ExitStack()
        wb2 = [ph2.enter_context(nc.sbuf_tensor(f"wb2_{i}", [128, 16, 512], BF16)) for i in range(2)]
        stg = [ph2.enter_context(nc.sbuf_tensor(f"stg{i}", [128, T], BF16)) for i in range(2)]
        vst = [ph2.enter_context(nc.sbuf_tensor(f"vst{i}", [128, 512], BF16)) for i in range(4)]
        wstg = ph2.enter_context(nc.sbuf_tensor("wstg", [128, 4, 512], F32))
        psp = [ph2.enter_context(nc.psum_tensor(f"psp{i}", [128, 512], F32)) for i in range(4)]
        s.buf("wstg")
        for i in range(2):
            s.buf(f"wb2_{i}")
            s.buf(f"stg{i}")
        for i in range(4):
            s.buf(f"vst{i}")
            s.buf(f"psp{i}", xread=True)
        ph1 = ExitStack()
        xin = [ph1.enter_context(nc.sbuf_tensor(f"xin{i}", [128, D], F32)) for i in range(2)]
        xlt = [ph1.enter_context(nc.sbuf_tensor(f"xlt{i}", [128, D], BF16)) for i in range(2)]
        st1 = [ph1.enter_context(nc.sbuf_tensor(f"st1_{i}", [128, 8], F32)) for i in range(2)]
        ps1 = [ph1.enter_context(nc.psum_tensor(f"ps1t_{i}", [128, 128], BF16)) for i in range(2)]
        s.buf("ps1t_0", xread=True)
        s.buf("ps1t_1", xread=True)

        def ln_chunk(which, tc, dest, destn, stage=0):
            i2 = (which * 16 + tc) % 2
            xi, xt = xin[i2], xlt[i2]
            xn, tn = f"xin{i2}", f"xlt{i2}"
            st1_ = st1[i2]
            sn1 = f"st1_{i2}"
            tg = f"{which}_{tc}"
            xsrc = xin_d if which == 0 else xpr_d
            xsn = "xin" if which == 0 else "xpr"
            if stage == 2:
                for a in range(16):
                    ai = a % 2
                    xt = xlt[i2]
                    tn = f"xlt{i2}"
                    s.op("T", (lambda a, xt, ai: lambda t: t.transpose(
                            ps1[ai][:, :], xt[:, a * 128:(a + 1) * 128], eyebf[:, :]))(a, xt, ai),
                         reads=[tn, "eyebf"], writes=[f"ps1t_{ai}"], tag=f"tr{tg}_{a}")
                    if ai == 0:
                        s.op("S", (lambda a, tc, dest, ai: lambda sc: sc.activation(
                                dest[:, a, tc * 128:(tc + 1) * 128], ps1[ai][:, :], AF.Identity))(a, tc, dest, ai),
                             reads=[f"ps1t_{ai}"], writes=[f"{destn}0"], tag=f"te{tg}_{a}")
                    else:
                        s.op("V", (lambda a, tc, dest, ai: lambda v: v.tensor_copy(
                                dest[:, a, tc * 128:(tc + 1) * 128], ps1[ai][:, :]))(a, tc, dest, ai),
                             reads=[f"ps1t_{ai}"], writes=[f"{destn}1"], tag=f"te{tg}_{a}")
                return
            ldq = "R" if (which == 0 and tc % 2 == 1) or which == 1 else "Q"
            s.op(ldq, (lambda xi, tc, xsrc: lambda g: g.dma_start(
                    out=xi[:, :], in_=xsrc.ap()[tc * 128:(tc + 1) * 128, :]))(xi, tc, xsrc),
                 reads=[xsn], writes=[xn], tag=f"ldx{tg}")
            s.op("V", (lambda xi, st1_: lambda v: v.reduce_sum(st1_[:, 0:1], xi[:, :], axis=AX))(xi, st1_),
                 reads=[xn], writes=[sn1], tag=f"sum{tg}")
            s.op("S", (lambda xi, st1_, xt: lambda sc: sc.activation(
                    xt[:, :], xi[:, :], AF.Square, accum_out=st1_[:, 1:2]))(xi, st1_, xt),
                 reads=[xn], writes=[tn, sn1], tag=f"sq{tg}")

            def fstat(st1_):
                def f(v):
                    v.tensor_scalar(st1_[:, 2:3], st1_[:, 0:1], 1.0 / D, None, op0=ALU.mult)
                    return v.tensor_scalar(st1_[:, 3:4], st1_[:, 1:2], 1.0 / D, None, op0=ALU.mult)
                return f
            s.op("V", fstat(st1_), reads=[sn1], writes=[sn1], tag=f"st{tg}")
            s.op("V", (lambda st1_: lambda v: v.tensor_tensor(
                    st1_[:, 4:5], st1_[:, 2:3], st1_[:, 2:3], op=ALU.mult))(st1_),
                 reads=[sn1], writes=[sn1], tag=f"st2{tg}")
            s.op("V", (lambda st1_: lambda v: v.tensor_tensor(
                    st1_[:, 3:4], st1_[:, 3:4], st1_[:, 4:5], op=ALU.subtract))(st1_),
                 reads=[sn1], writes=[sn1], tag=f"st3{tg}")
            s.op("S", (lambda st1_: lambda sc: sc.activation(
                    st1_[:, 5:6], st1_[:, 3:4], AF.Sqrt, bias=cpk[:, 19:20]))(st1_),
                 reads=[sn1, "cpk"], writes=[sn1], tag=f"sqr{tg}")
            s.op("V", (lambda st1_: lambda v: v.reciprocal(st1_[:, 6:7], st1_[:, 5:6]))(st1_),
                 reads=[sn1], writes=[sn1], tag=f"rc{tg}")
            s.op("V", (lambda xi, xt, st1_: lambda v: v.tensor_scalar(
                    xt[:, :], xi[:, :], st1_[:, 2:3], st1_[:, 6:7],
                    op0=ALU.subtract, op1=ALU.mult))(xi, xt, st1_),
                 reads=[xn, sn1], writes=[tn], tag=f"nm{tg}")
            if stage == 0:
                ln_chunk(which, tc, dest, destn, stage=2)

        CAST_SET = (0, 1, 2, 3, 4, 5, 6, 7, 12, 13, 14, 15)

        def emit_oc(oc):
                wb, wn = wb2[oc % 2], f"wb2_{oc%2}"
                if oc in CAST_SET:
                    for h4 in range(4):
                        s.op("Q", (lambda oc, h4: lambda g: g.dma_start(
                                out=wstg[:, :, :],
                                in_=bass.AP(wqk_d, oc * GR_E + h4 * 4 * 512,
                                            [[16 * 512, 128], [512, 4], [1, 512]])))(oc, h4),
                             reads=["wqk"], writes=["wstg"], tag=f"lws{oc}_{h4}")
                        s.op("V", (lambda wb, h4: lambda v: v.tensor_copy(
                                wb[:, 4 * h4:4 * h4 + 4, :], wstg[:, :, :]))(wb, h4),
                             reads=["wstg"], writes=[wn], tag=f"cst{oc}_{h4}")
                else:
                    s.op("G", (lambda oc, wb: lambda g: pay_dma(
                            g, wb[:, :, :], oc * GR_E, [[16 * 512, 128], [512, 16], [1, 512]],
                            base=wqk_d))(oc, wb),
                         reads=["wqk"], writes=[wn], tag=f"ldw{oc}")
                fam = ("q", "k", "v", "u")[oc // 4]
                if fam != "v":
                    srcs = [("o", xl, "xl")]
                    if fam == "k":
                        srcs.append(("p", xlp, "xlp"))
                    for tagp, xsrc, xsn in srcs:
                        for sub in range(4):
                            head = (oc % 4) * 4 + sub
                            sg, sn = stg[sub % 2], f"stg{sub%2}"
                            for tch in range(4):
                                pi = tch % 4

                                def fmm(wb, sub, xsrc, tch, pi):
                                    def f(t):
                                        r = None
                                        for a in range(16):
                                            r = t.matmul(psp[pi][:, :],
                                                         wb[:, a, sub * 128:(sub + 1) * 128],
                                                         xsrc[:, a, tch * 512:(tch + 1) * 512],
                                                         start=(a == 0), stop=(a == 15))
                                        return r
                                    return f
                                s.op("T", fmm(wb, sub, xsrc, tch, pi),
                                     reads=[wn, f"{xsn}0", f"{xsn}1"],
                                     writes=[f"psp{pi}"], tag=f"mm{oc}{tagp}_{sub}_{tch}")
                                func = AF.Silu if fam in ("q", "k") else AF.Identity
                                s.op("S", (lambda sg, tch, pi, func: lambda sc: sc.activation(
                                        sg[:, tch * 512:(tch + 1) * 512], psp[pi][:, :], func))(sg, tch, pi, func),
                                     reads=[f"psp{pi}"], writes=[sn], tag=f"ev{oc}{tagp}_{sub}_{tch}")
                            dst = {"q": q_sc, "k": k_sc, "u": u_sc}[fam]
                            dnm = fam + "_sc"
                            if tagp == "p":
                                dst, dnm = kp_sc, "kp_sc"
                            s.op("G", (lambda dst, head, sg: lambda g: g.dma_start(
                                    out=dst[head], in_=sg[:, :]))(dst, head, sg),
                                 reads=[sn], writes=[dnm], tag=f"sp{fam}{tagp}{head}")
                else:
                    vcc = oc - 8
                    for tagp, xsrc, xsn, dst, dnm in (("o", xl, "xl", v_sc, "v_sc"),
                                                      ("p", xlp, "xlp", vp_sc, "vp_sc")):
                        for tc in range(16):
                            pi = tc % 4

                            def fmmv(wb, tc, xsrc, pi):
                                def f(t):
                                    r = None
                                    for a in range(16):
                                        r = t.matmul(psp[pi][:, :],
                                                     xsrc[:, a, tc * 128:(tc + 1) * 128],
                                                     wb[:, a, :], start=(a == 0), stop=(a == 15))
                                    return r
                                return f
                            s.op("T", fmmv(wb, tc, xsrc, pi),
                                 reads=[wn, f"{xsn}0", f"{xsn}1"],
                                 writes=[f"psp{pi}"], tag=f"vm{oc}{tagp}_{tc}")
                            vs, vn = vst[tc % 4], f"vst{tc%4}"
                            s.op("S", (lambda vs, pi: lambda sc: sc.activation(
                                    vs[:, :], psp[pi][:, :], AF.Identity))(vs, pi),
                                 reads=[f"psp{pi}"], writes=[vn], tag=f"vev{oc}{tagp}_{tc}")
                            s.op("G", (lambda dst, vcc, tc, vs: lambda g: g.dma_start(
                                    out=dst[vcc, tc], in_=vs[:, :]))(dst, vcc, tc, vs),
                                 reads=[vn], writes=[dnm], tag=f"vsp{oc}{tagp}_{tc}")
        if phases >= 1:
            for tc in range(16):
                ln_chunk(0, tc, xl, "xl", stage=1)
                if tc >= 1:
                    ln_chunk(0, tc - 1, xl, "xl", stage=2)
            ln_chunk(0, 15, xl, "xl", stage=2)
        if phases >= 2:
            for k, oc in enumerate((0, 1, 2, 3)):
                emit_oc(oc)
                if phases >= 1:
                    for j in range(4):
                        tc = 4 * k + j
                        ln_chunk(1, tc, xlp, "xlp", stage=1)
                        if tc >= 1:
                            ln_chunk(1, tc - 1, xlp, "xlp", stage=2)
            if phases >= 1:
                ln_chunk(1, 15, xlp, "xlp", stage=2)
            for oc in (4, 5, 6, 7, 12, 13, 14, 15, 8, 9, 10, 11):
                emit_oc(oc)
        elif phases >= 1:
            for tc in range(16):
                ln_chunk(1, tc, xlp, "xlp")
        s.emit_phase(nc, ctx)
        ph1.close()
        ph2.close()
        sc12.close()
        s.note_free()

        sc5bc = ExitStack()
        wsd_row = sc5bc.enter_context(nc.sbuf_tensor("wsd_row", [1, D], F32))
        sc34 = ExitStack()
        og = sc34.enter_context(nc.sbuf_tensor("og", [128, 16, T], BF16))
        ph3 = ExitStack()
        if phases >= 3:
            arrq = ph3.enter_context(nc.sbuf_tensor("arrq_s", [128, 256], F32))
            cmask = ph3.enter_context(nc.sbuf_tensor("cmask_s", [128, 2, 256], F32))
            s.op("Q", lambda g: g.dma_start(out=arrq[:, :], in_=arrq_d[:, :]),
                 writes=["arrq"], tag="ld_arrq")
            s.op("Q", lambda g: g.dma_start(out=cmask[:, :, :],
                                            in_=cmask_d.ap().rearrange("a p t -> p a t")),
                 writes=["cmask"], tag="ld_cmask")
            qdec = ph3.enter_context(nc.sbuf_tensor("qdec", [128, 16, 256], BF16))
            mask = ph3.enter_context(nc.sbuf_tensor("mask", [128, 16, 2, 256], BF16))
            kdec = ph3.enter_context(nc.sbuf_tensor("kdec", [128, 16, 2], F32))
            bdc = ph3.enter_context(nc.sbuf_tensor("bdc", [128, 16], F32))
            tmpd = ph3.enter_context(nc.sbuf_tensor("tmpd", [128, 2, 256], F32))
            kv = ph3.enter_context(nc.sbuf_tensor("kv", [128, 16, 128], F32))
            kvb = [ph3.enter_context(nc.sbuf_tensor(f"kvb{i}", [128, 128], BF16)) for i in range(2)]
            qb = [ph3.enter_context(nc.sbuf_tensor(f"qb{i}", [128, 16, 256], BF16)) for i in range(2)]
            kb = [ph3.enter_context(nc.sbuf_tensor(f"kb{i}", [128, 16, 256], BF16)) for i in range(2)]
            vb = [ph3.enter_context(nc.sbuf_tensor(f"vb{i}", [128, 2, 2048], BF16)) for i in range(2)]
            ub = [ph3.enter_context(nc.sbuf_tensor("ub0", [128, 16, 256], BF16))]
            kqm = [ph3.enter_context(nc.sbuf_tensor(f"kqm{i}", [128, 2, 256], BF16)) for i in range(2)]
            ktda = [ph3.enter_context(nc.sbuf_tensor(f"ktda{i}", [128, 2, 16, 128], BF16)) for i in range(2)]
            qdq = [ph3.enter_context(nc.sbuf_tensor(f"qdq{i}", [128, 256], BF16)) for i in range(2)]
            obf = ph3.enter_context(nc.sbuf_tensor("obf", [128, 16, 256], BF16))
            osq = ph3.enter_context(nc.sbuf_tensor("osq", [128, 16, 256], BF16))
            rsq = ph3.enter_context(nc.sbuf_tensor("rsq", [128, 256], F32))
            ogt = ph3.enter_context(nc.sbuf_tensor("ogt", [128, 2, 256], F32))
            ps_kq = [ph3.enter_context(nc.psum_tensor(f"ps_kq{i}", [128, 512], F32)) for i in range(2)]
            ps_o = [ph3.enter_context(nc.psum_tensor(f"ps_o{i}", [128, 256], F32)) for i in range(2)]
            ps_kv = ph3.enter_context(nc.psum_tensor("ps_kv", [128, 128], F32))
            ps_kt = [ph3.enter_context(nc.psum_tensor(f"ps_kt{i}", [128, 256], BF16)) for i in range(2)]
            ps_rms = ph3.enter_context(nc.psum_tensor("ps_rms", [128, 256], F32))
            for n in ("ps_kq0", "ps_kq1", "ps_o0", "ps_o1", "ps_kv", "ps_kt0", "ps_kt1", "ps_rms"):
                s.buf(n, xread=True)

            for h in range(16):
                s.op("V", (lambda h: lambda v: v.tensor_scalar(
                        tmpd[:, 0, :], arrq[:, :], cpk[:, 2 + h:3 + h], None, op0=ALU.mult))(h),
                     reads=["arrq", "cpk"], writes=["tmpd"], tag=f"qd0_{h}")
                s.op("S", (lambda h: lambda sc: sc.activation(
                        qdec[:, h, :], tmpd[:, 0, :], AF.Exp, scale=-1.0))(h),
                     reads=["tmpd"], writes=["qdec"], tag=f"qd1_{h}")

                def fm(h):
                    def g(v):
                        v.tensor_scalar(tmpd[:, 0, :], cmask[:, 0, :], cpk[:, 2 + h:3 + h], None, op0=ALU.mult)
                        return v.tensor_scalar(tmpd[:, 1, :], cmask[:, 1, :], cpk[:, 2 + h:3 + h], None, op0=ALU.mult)
                    return g
                s.op("V", fm(h), reads=["cmask", "cpk"], writes=["tmpd"], tag=f"mk0_{h}")

                def fe(h):
                    def g(sc):
                        sc.activation(mask[:, h, 0, :], tmpd[:, 0, :], AF.Exp, scale=-1.0)
                        return sc.activation(mask[:, h, 1, :], tmpd[:, 1, :], AF.Exp, scale=-1.0)
                    return g
                s.op("S", fe(h), reads=["tmpd"], writes=["mask"], tag=f"mk1_{h}")
                s.op("V", (lambda h: lambda v: v.tensor_scalar(
                        tmpd[:, 0, 0:2], cpk[:, 0:2], cpk[:, 2 + h:3 + h], None, op0=ALU.mult))(h),
                     reads=["cpk"], writes=["tmpd"], tag=f"kd0_{h}")
                s.op("S", (lambda h: lambda sc: sc.activation(
                        kdec[:, h, :], tmpd[:, 0, 0:2], AF.Exp, scale=-1.0))(h),
                     reads=["tmpd"], writes=["kdec"], tag=f"kd1_{h}")
            s.op("S", lambda sc: sc.activation(bdc[:, :], cpk[:, 2:18], AF.Exp, scale=-256.0),
                 reads=["cpk"], writes=["bdc"], tag="bd")
            s.op("V", lambda v: v.memset(kv[:, :, :], 0.0),
                 writes=[f"kv_h{h}" for h in range(16)], tag="kv0")

            def load_blk(b, par, pre):
                ksrc, kn = (kp_sc, "kp_sc") if pre else (k_sc, "k_sc")
                vsrc, vn = (vp_sc, "vp_sc") if pre else (v_sc, "v_sc")
                if not pre:
                    s.op("Q", (lambda b, par: lambda g: g.dma_start(
                            out=qb[par][:, :, :],
                            in_=q_sc.ap()[:, :, b * 256:(b + 1) * 256].rearrange("h p t -> p h t")))(b, par),
                         reads=["q_sc"], writes=[f"qb{par}"], tag=f"lq{b}")
                    s.op("Q", (lambda b: lambda g: g.dma_start(
                            out=ub[0][:, :, :],
                            in_=u_sc.ap()[:, :, b * 256:(b + 1) * 256].rearrange("h p t -> p h t")))(b),
                         reads=["u_sc"], writes=["ub0"], tag=f"lu{b}")
                s.op("Q", (lambda b, par, ksrc: lambda g: g.dma_start(
                        out=kb[par][:, :, :],
                        in_=ksrc.ap()[:, :, b * 256:(b + 1) * 256].rearrange("h p t -> p h t")))(b, par, ksrc),
                     reads=[kn], writes=[f"kb{par}"], tag=f"lk{pre}{b}")
                for vcc in range(4):
                    s.op("Q", (lambda b, par, vcc, vsrc: lambda g: g.dma_start(
                            out=vb[par][:, :, vcc * 512:(vcc + 1) * 512],
                            in_=vsrc.ap()[vcc, 2 * b:2 * b + 2, :, :].rearrange("c p n -> p c n")))(b, par, vcc, vsrc),
                         reads=[vn], writes=[f"vb{par}"], tag=f"lv{pre}{b}_{vcc}")

            def ktrans_blk(b, par, pre):
                kn = "kp_sc" if pre else "k_sc"
                for h in range(16):
                    pk = ps_kt[(2 * b + h) % 2]
                    pkn = f"ps_kt{(2 * b + h) % 2}"

                    def ftr(h, par, pk):
                        def f(t):
                            t.transpose(pk[:, 0:128], kb[par][:, h, 0:128], eyebf[:, :])
                            return t.transpose(pk[:, 128:256], kb[par][:, h, 128:256], eyebf[:, :])
                        return f
                    s.op("T", ftr(h, par, pk), reads=[f"kb{par}", "eyebf"], writes=[pkn],
                         tag=f"kt{pre}{b}_{h}")

                    def fkd(h, par, pk):
                        def f(sc):
                            sc.activation(ktda[par][:, 0, h, :], pk[:, 0:128], AF.Identity,
                                          scale=kdec[:, h, 0:1])
                            return sc.activation(ktda[par][:, 1, h, :], pk[:, 128:256], AF.Identity,
                                                 scale=kdec[:, h, 1:2])
                        return f
                    s.op("S", fkd(h, par, pk), reads=[pkn, "kdec"], writes=[f"ktda{par}_h{h}"],
                         tag=f"kd{pre}{b}_{h}")

            def km_op(h, b, par, pre, pk, pkn):
                def fkv(h, par, pk):
                    def f(t):
                        t.matmul(pk[:, :], ktda[par][:, 0, h, :], vb[par][:, 0, h * 128:(h + 1) * 128],
                                 start=True, stop=False)
                        return t.matmul(pk[:, :], ktda[par][:, 1, h, :], vb[par][:, 1, h * 128:(h + 1) * 128],
                                        start=False, stop=True)
                    return f
                s.op("T", fkv(h, par, pk), reads=[f"ktda{par}_h{h}", f"vb{par}"], writes=[pkn],
                     tag=f"km{pre}{b}_{h}")

            def ku_op(h, b, pre, pk, pkn):
                s.op("V", (lambda h, pk: lambda v: v.scalar_tensor_tensor(
                        kv[:, h, :], kv[:, h, :], bdc[:, h:h + 1], pk[:, :],
                        op0=ALU.mult, op1=ALU.add))(h, pk),
                     reads=[pkn, "bdc", f"kv_h{h}"], writes=[f"kv_h{h}"], tag=f"ku{pre}{b}_{h}")

            pre_pk = lambda h: (ps_kq[h % 2][:, 0:128], f"ps_kq{h % 2}")
            for b in range(NB):
                load_blk(b, b % 2, True)
                ktrans_blk(b, b % 2, True)
                for h in range(16):
                    if h >= 1:
                        ku_op(h - 1, b, True, *pre_pk(h - 1))
                    km_op(h, b, b % 2, True, *pre_pk(h))
                ku_op(15, b, True, *pre_pk(15))
            s.op("V", lambda v: v.tensor_scalar(kv[:, :, :], kv[:, :, :], cpk[:, 18:19], None, op0=ALU.mult),
                 reads=["cpk"] + [f"kv_h{h}" for h in range(16)],
                 writes=[f"kv_h{h}" for h in range(16)], tag="kvflag")

            def kq_mm(h, b, par):
                i = h % 2
                def fkq(h, par, i):
                    def f(t):
                        t.matmul(ps_kq[i][:, 0:256], kb[par][:, h, 0:128], qb[par][:, h, :],
                                 start=True, stop=True)
                        return t.matmul(ps_kq[i][:, 256:512], kb[par][:, h, 128:256], qb[par][:, h, :],
                                        start=True, stop=True)
                    return f
                s.op("T", fkq(h, par, i), reads=[f"kb{par}", f"qb{par}"],
                     writes=[f"ps_kq{i}"], tag=f"kq{b}_{h}")

            def mkq_qdq(h, b, par):
                i = h % 2
                def fmk(h, i):
                    def f(v):
                        v.tensor_tensor(kqm[i][:, 0, :], ps_kq[i][:, 0:256], mask[:, h, 0, :], op=ALU.mult)
                        return v.tensor_tensor(kqm[i][:, 1, :], ps_kq[i][:, 256:512], mask[:, h, 1, :], op=ALU.mult)
                    return f
                s.op("V", fmk(h, i), reads=[f"ps_kq{i}", "mask"], writes=[f"kqm{i}"], tag=f"mq{b}_{h}")
                s.op("P", (lambda h, par, i: lambda g: g.tensor_tensor(
                        qdq[i][:, :], qb[par][:, h, :], qdec[:, h, :], op=ALU.mult))(h, par, i),
                     reads=[f"qb{par}", "qdec"], writes=[f"qdq{i}"], tag=f"qq{b}_{h}")

            def kc_op(h, b):
                i = h % 2
                s.op("S", (lambda h, i: lambda sc: sc.activation(
                        kvb[i][:, :], kv[:, h, :], AF.Identity))(h, i),
                     reads=[f"kv_h{h}"], writes=[f"kvb{i}"], tag=f"kc{b}_{h}")

            def o_mm(h, b, par):
                i = h % 2
                def fo(h, par, i):
                    def f(t):
                        t.matmul(ps_o[i][:, 0:256], kvb[i][:, :], qdq[i][:, :], start=True, stop=False)
                        t.matmul(ps_o[i][:, 0:256], vb[par][:, 0, h * 128:(h + 1) * 128],
                                 kqm[i][:, 0, :], start=False, stop=False)
                        return t.matmul(ps_o[i][:, 0:256], vb[par][:, 1, h * 128:(h + 1) * 128],
                                        kqm[i][:, 1, :], start=False, stop=True)
                    return f
                s.op("T", fo(h, par, i), reads=[f"kvb{i}", f"qdq{i}", f"vb{par}", f"kqm{i}"],
                     writes=[f"ps_o{i}"], tag=f"om{b}_{h}")

            def o_evac(h, b):
                i = h % 2
                s.op("S", (lambda h, i: lambda sc: sc.activation(obf[:, h, :], ps_o[i][:, 0:256],
                                                                 AF.Identity))(h, i),
                     reads=[f"ps_o{i}"], writes=[f"obf_h{h}"], tag=f"oe{b}_{h}")
                s.op("S", (lambda h, i: lambda sc: sc.activation(osq[:, h, :], ps_o[i][:, 0:256],
                                                                 AF.Square))(h, i),
                     reads=[f"ps_o{i}"], writes=[f"osq_h{h}"], tag=f"os{b}_{h}")

            for b in range(NB):
                par = b % 2
                load_blk(b, par, False)
                ktrans_blk(b, par, False)
                kq_mm(0, b, par)
                kc_op(0, b)
                for h in range(16):
                    mkq_qdq(h, b, par)
                    if h + 1 < 16:
                        kq_mm(h + 1, b, par)
                        kc_op(h + 1, b)
                    o_mm(h, b, par)
                    o_evac(h, b)
                    if h >= 1:
                        ku_op(h - 1, b, False, ps_kv, "ps_kv")
                    km_op(h, b, par, False, ps_kv, "ps_kv")
                ku_op(15, b, False, ps_kv, "ps_kv")

                def frms(t):
                    r = None
                    for ct in range(16):
                        r = t.matmul(ps_rms[:, :], onesbf[:, :], osq[:, ct, :],
                                     start=(ct == 0), stop=(ct == 15))
                    return r
                s.op("T", frms, reads=["onesbf"] + [f"osq_h{h}" for h in range(16)],
                     writes=["ps_rms"], tag=f"rm{b}")
                s.op("S", lambda sc: sc.activation(rsq[:, :], ps_rms[:, :], AF.Sqrt,
                                                   scale=1.0 / D, bias=cpk[:, 20:21]),
                     reads=["ps_rms", "cpk"], writes=["rsq"], tag=f"rs{b}")
                s.op("V", lambda v: v.reciprocal(rsq[:, :], rsq[:, :]),
                     reads=["rsq"], writes=["rsq"], tag=f"rr{b}")
                for ct in range(16):
                    s.op("V", (lambda ct, b: lambda v: v.tensor_tensor(
                            ogt[:, ct % 2, :], obf[:, ct, :], rsq[:, :], op=ALU.mult))(ct, b),
                         reads=[f"obf_h{ct}", "rsq"], writes=["ogt"], tag=f"ogA{b}_{ct}")
                    s.op("V", (lambda ct, b: lambda v: v.tensor_tensor(
                            og[:, ct, b * 256:(b + 1) * 256], ogt[:, ct % 2, :],
                            ub[0][:, ct, :], op=ALU.mult))(ct, b),
                         reads=["ogt", "ub0"], writes=["og"], tag=f"ogB{b}_{ct}")
        s.emit_phase(nc, ctx)
        ph3.close()
        s.note_free()

        ph4 = ExitStack()
        if phases >= 4:
            wo = ph4.enter_context(nc.sbuf_tensor("wo", [128, 16, D], BF16))
            x2s = [ph4.enter_context(nc.sbuf_tensor(f"x2s{i}", [128, D], F32)) for i in range(2)]
            ps4 = [ph4.enter_context(nc.psum_tensor(f"ps4_{i}", [128, 512], F32)) for i in range(4)]
            for i in range(4):
                s.buf(f"ps4_{i}", xread=True)
            for wi_ in range(4):
                s.op("G", (lambda wi_: lambda g: pay_dma(
                        g, wo[:, :, wi_ * 512:(wi_ + 1) * 512], GRAN_OFF[("wout", wi_)],
                        [[16 * 512, 128], [512, 16], [1, 512]]))(wi_),
                     reads=["pay"], writes=["wo"], tag=f"ldwo{wi_}")
            for tc in range(16):
                i2 = tc % 2
                s.op("Q", (lambda tc, i2: lambda g: g.dma_start(
                        out=x2s[i2][:, :], in_=xin_d.ap()[tc * 128:(tc + 1) * 128, :]))(tc, i2),
                     reads=["xin_d"], writes=[f"x2s{i2}"], tag=f"lr{tc}")
                for fc in range(4):
                    pi = fc

                    def fmm(tc, fc, pi):
                        def f(t):
                            r = None
                            for cc in range(16):
                                r = t.matmul(ps4[pi][:, :], og[:, cc, tc * 128:(tc + 1) * 128],
                                             wo[:, cc, fc * 512:(fc + 1) * 512],
                                             start=(cc == 0), stop=(cc == 15))
                            return r
                        return f
                    s.op("T", fmm(tc, fc, pi), reads=["og", "wo"], writes=[f"ps4_{pi}"],
                         tag=f"o2_{tc}_{fc}")
                    s.op("V", (lambda tc, fc, pi, i2: lambda v: v.tensor_tensor(
                            x2s[i2][:, fc * 512:(fc + 1) * 512], ps4[pi][:, :],
                            x2s[i2][:, fc * 512:(fc + 1) * 512], op=ALU.add))(tc, fc, pi, i2),
                         reads=[f"ps4_{pi}", f"x2s{i2}"], writes=[f"x2s{i2}"], tag=f"x2_{tc}_{fc}")
                s.op("Q", (lambda tc, i2: lambda g: g.dma_start(
                        out=x2_sc[tc], in_=x2s[i2][:, :]))(tc, i2),
                     reads=[f"x2s{i2}"], writes=["x2_sc"], tag=f"sx2{tc}")
            wdm = ph4.enter_context(nc.sbuf_tensor("wdm", [128, 512], F32))
            wdm2 = ph4.enter_context(nc.sbuf_tensor("wdm2", [128, 512], F32))
            wdi = [ph4.enter_context(nc.sbuf_tensor("wdi0", [128, 11, 512], F32))]
            ps4w = ph4.enter_context(nc.psum_tensor("ps4w", [128, 512], F32))
            s.buf("ps4w", xread=True)
            for fc in range(4):
                for g4 in range(4):
                    off = GRAN_OFF[("wd", fc * 4 + g4)]
                    s.op("G", (lambda off: lambda g: pay_dma(
                            g, wdi[0][:, :, :], off, [[512, 128], [128 * 512, 11], [1, 512]]))(off),
                         reads=["pay"], writes=["wdi0"], tag=f"wd{fc}_{g4}")

                    for j in range(11):
                        if g4 == 0 and j == 0:
                            s.op("V", lambda v: v.scalar_tensor_tensor(
                                    wdm[:, :], wdi[0][:, 0, :], -1.0, wdi[0][:, 0, :],
                                    op0=ALU.mult, op1=ALU.max),
                                 reads=["wdi0"], writes=["wdm"], tag=f"wm{fc}_{g4}_0")
                        else:
                            s.op("V", (lambda j: lambda v: v.scalar_tensor_tensor(
                                    wdm2[:, :], wdi[0][:, j, :], -1.0, wdi[0][:, j, :],
                                    op0=ALU.mult, op1=ALU.max))(j),
                                 reads=["wdi0"], writes=["wdm2"], tag=f"wmA{fc}_{g4}_{j}")
                            s.op("V", lambda v: v.tensor_tensor(
                                    wdm[:, :], wdm[:, :], wdm2[:, :], op=ALU.max),
                                 reads=["wdm", "wdm2"], writes=["wdm"], tag=f"wmB{fc}_{g4}_{j}")

                def ftr5(t):
                    r = None
                    for j in range(4):
                        r = t.transpose(ps4w[:, j * 128:(j + 1) * 128],
                                        wdm[:, j * 128:(j + 1) * 128], eye32[:, :])
                    return r
                s.op("T", ftr5, reads=["wdm", "eye32"], writes=["ps4w"], tag=f"wt{fc}")

                def fred(v):
                    r = None
                    for j in range(4):
                        r = v.reduce_max(wdm[:, j:j + 1], ps4w[:, j * 128:(j + 1) * 128], axis=AX)
                    return r
                s.op("V", fred, reads=["ps4w"], writes=["wdm"], tag=f"wr{fc}")
                s.op("V", lambda v: v.tensor_scalar(wdm[:, 0:4], wdm[:, 0:4], 1.0 / 448.0, None,
                                                    op0=ALU.mult),
                     reads=["wdm"], writes=["wdm"], tag=f"ws{fc}")
                for jj in range(4):
                    s.op("T", (lambda jj: lambda t: t.transpose(
                            ps4w[0:1, 256:384], wdm[:, jj:jj + 1], eye32[:, :]))(jj),
                         reads=["wdm", "eye32"], writes=["ps4w"], tag=f"wfT{fc}_{jj}")
                    s.op("V", (lambda fc, jj: lambda v: v.tensor_copy(
                            wsd_row[0:1, fc * 512 + jj * 128:fc * 512 + (jj + 1) * 128],
                            ps4w[0:1, 256:384]))(fc, jj),
                         reads=["ps4w"], writes=["wsd_row"], tag=f"wfC{fc}_{jj}")


        s.emit_phase(nc, ctx)
        ph4.close()
        sc34.close()
        s.note_free()

        qt = sc5bc.enter_context(nc.sbuf_tensor("qt", [128, 44, T], FP8))
        sc5 = ExitStack()
        qx = sc5.enter_context(nc.sbuf_tensor("qx", [128, 16, T], FP8))
        ph5a = ExitStack()
        if phases >= 5:
            x2i = [ph5a.enter_context(nc.sbuf_tensor(f"x2i{i}", [128, D], F32)) for i in range(3)]
            xl2 = [ph5a.enter_context(nc.sbuf_tensor(f"xl2_{i}", [128, D], F32)) for i in range(3)]
            q8t = [ph5a.enter_context(nc.sbuf_tensor(f"q8t_{i}", [128, D], FP8)) for i in range(3)]
            st5 = [ph5a.enter_context(nc.sbuf_tensor(f"st5_{i}", [128, 8], F32)) for i in range(3)]
            ps5 = ph5a.enter_context(nc.psum_tensor("ps5", [128, 512], F32))
            ps5q = [ph5a.enter_context(nc.psum_tensor(f"ps5q_{i}", [128, 256], FP8)) for i in range(2)]
            s.buf("ps5", xread=True)
            s.buf("ps5q_0", xread=True)
            s.buf("ps5q_1", xread=True)

            for tc in range(16):
                i2 = tc % 3
                xi, xn = x2i[i2], f"x2i{i2}"
                sv, svn = st5[i2], f"st5_{i2}"
                xlv, xlvn = xl2[i2], f"xl2_{i2}"
                qv, qvn = q8t[i2], f"q8t_{i2}"
                s.op("Q", (lambda tc, xi: lambda g: g.dma_start(
                        out=xi[:, :], in_=x2_sc[tc]))(tc, xi),
                     reads=["x2_sc"], writes=[xn], tag=f"l2{tc}")
                s.op("V", (lambda xi, sv: lambda v: v.reduce_sum(sv[:, 0:1], xi[:, :], axis=AX))(xi, sv),
                     reads=[xn], writes=[svn], tag=f"a{tc}")
                s.op("S", (lambda xi, sv, xlv: lambda sc: sc.activation(
                        xlv[:, :], xi[:, :], AF.Square, accum_out=sv[:, 1:2]))(xi, sv, xlv),
                     reads=[xn], writes=[xlvn, svn], tag=f"b{tc}")

                def fst(sv):
                    def f(v):
                        v.tensor_scalar(sv[:, 2:3], sv[:, 0:1], 1.0 / D, None, op0=ALU.mult)
                        return v.tensor_scalar(sv[:, 3:4], sv[:, 1:2], 1.0 / D, None, op0=ALU.mult)
                    return f
                s.op("V", fst(sv), reads=[svn], writes=[svn], tag=f"c{tc}")
                s.op("V", (lambda sv: lambda v: v.tensor_tensor(
                        sv[:, 4:5], sv[:, 2:3], sv[:, 2:3], op=ALU.mult))(sv),
                     reads=[svn], writes=[svn], tag=f"c2{tc}")
                s.op("V", (lambda sv: lambda v: v.tensor_tensor(
                        sv[:, 3:4], sv[:, 3:4], sv[:, 4:5], op=ALU.subtract))(sv),
                     reads=[svn], writes=[svn], tag=f"c3{tc}")
                s.op("S", (lambda sv: lambda sc: sc.activation(
                        sv[:, 5:6], sv[:, 3:4], AF.Sqrt, bias=cpk[:, 19:20]))(sv),
                     reads=[svn, "cpk"], writes=[svn], tag=f"d{tc}")
                s.op("V", (lambda sv: lambda v: v.reciprocal(sv[:, 6:7], sv[:, 5:6]))(sv),
                     reads=[svn], writes=[svn], tag=f"e{tc}")
                s.op("V", (lambda xi, sv, xlv: lambda v: v.tensor_scalar(
                        xlv[:, :], xi[:, :], sv[:, 2:3], sv[:, 6:7],
                        op0=ALU.subtract, op1=ALU.mult))(xi, sv, xlv),
                     reads=[xn, svn], writes=[xlvn], tag=f"f{tc}")
                s.op("V", (lambda sv, xlv: lambda v: v.reduce_max(
                        sv[:, 0:1], xlv[:, :], axis=AX, apply_absolute_value=True))(sv, xlv),
                     reads=[xlvn], writes=[svn], tag=f"g{tc}")

                s.op("V", (lambda sv: lambda v: v.reciprocal(sv[:, 1:2], sv[:, 0:1]))(sv),
                     reads=[svn], writes=[svn], tag=f"h{tc}")
                s.op("V", (lambda sv: lambda v: v.tensor_scalar(
                        sv[:, 1:2], sv[:, 1:2], 224.0, None, op0=ALU.mult))(sv),
                     reads=[svn], writes=[svn], tag=f"h2{tc}")
                s.op("S", (lambda sv, xlv, qv: lambda sc: sc.activation(
                        qv[:, :], xlv[:, :], AF.Identity, scale=sv[:, 1:2]))(sv, xlv, qv),
                     reads=[xlvn, svn], writes=[qvn], tag=f"i{tc}")
                s.op("V", (lambda tc, sv: lambda v: v.tensor_copy(mxs[:, tc:tc + 1], sv[:, 0:1]))(tc, sv),
                     reads=[svn], writes=["mxs"], tag=f"j{tc}")
                for a in range(16):
                    ai = a % 2
                    s.op("T", (lambda a, qv, ai: lambda t: t.transpose(
                            s2ap(ps5q[ai]), qv[:, a * 128:(a + 1) * 128], eye8[:, :]))(a, qv, ai),
                         reads=[qvn, "eye8"], writes=[f"ps5q_{ai}"], tag=f"k{tc}_{a}")
                    if ai == 0:
                        s.op("S", (lambda a, tc, ai: lambda sc: sc.activation(
                                qx[:, a, tc * 128:(tc + 1) * 128], s2ap(ps5q[ai]), AF.Identity))(a, tc, ai),
                             reads=[f"ps5q_{ai}"], writes=["qx0"], tag=f"l{tc}_{a}")
                    else:
                        s.op("V", (lambda a, tc, ai: lambda v: v.tensor_copy(
                                qx[:, a, tc * 128:(tc + 1) * 128], s2ap(ps5q[ai])))(a, tc, ai),
                             reads=[f"ps5q_{ai}"], writes=["qx1"], tag=f"l{tc}_{a}")
        s.emit_phase(nc, ctx)
        ph5a.close()
        s.note_free()

        ph5b = ExitStack()
        if phases >= 6:
            wmac = [ph5b.enter_context(nc.sbuf_tensor(f"wmac{i}", [128, 128], F32)) for i in range(2)]
            wabs = ph5b.enter_context(nc.sbuf_tensor("wabs", [128, 8, 128], F32))
            wred = ph5b.enter_context(nc.sbuf_tensor("wred", [128, 2], F32))
            wrow = ph5b.enter_context(nc.sbuf_tensor("wrow", [1, 512], F32))
            wibc = ph5b.enter_context(nc.sbuf_tensor("wibc", [128, 2, 512], F32))
            wsbc = ph5b.enter_context(nc.sbuf_tensor("wsbc", [128, 4, 512], F32))
            gtmp = [ph5b.enter_context(nc.sbuf_tensor(f"gtmp{i}", [128, 512], F32)) for i in range(2)]
            tstg = [ph5b.enter_context(nc.sbuf_tensor(f"tstg{i}", [128, 512], F32)) for i in range(2)]
            tq8 = [ph5b.enter_context(nc.sbuf_tensor(f"tq8_{i}", [128, 512], FP8))
                   for i in range(2)]
            mtmp = ph5b.enter_context(nc.sbuf_tensor("mtmp", [128, 2], F32))
            gup = ExitStack()
            wgi = gup.enter_context(nc.sbuf_tensor("wgi", [128, 16, 128], F32))
            wui = gup.enter_context(nc.sbuf_tensor("wui", [128, 16, 128], F32))
            qwg = [gup.enter_context(nc.sbuf_tensor(f"qwg{i}", [128, 16, 512], FP8)) for i in range(2)]
            qwu = [gup.enter_context(nc.sbuf_tensor(f"qwu{i}", [128, 16, 512], FP8)) for i in range(2)]
            wrowt = gup.enter_context(nc.sbuf_tensor("wrowt", [1, 256], F32))
            psg = [gup.enter_context(nc.psum_tensor(f"psg{i}", [128, 512], F32)) for i in range(3)]
            psu = [gup.enter_context(nc.psum_tensor(f"psu{i}", [128, 512], F32)) for i in range(3)]
            psb = gup.enter_context(nc.psum_tensor("psb", [128, 512], F32))
            psw = gup.enter_context(nc.psum_tensor("psw", [128, 512], F32))
            for n in ("psg0", "psg1", "psg2", "psu0", "psu1", "psu2", "psb", "psw"):
                s.buf(n, xread=True)

            s.op("V", lambda v: v.memset(m2s[:, :], 0.0), writes=["m2s"], tag="m2z")
            s.op("V", lambda v: v.tensor_scalar(rxs[:, :], mxs[:, :], 1.0 / 112.0, None, op0=ALU.mult),
                 reads=["mxs"], writes=["rxs"], tag="rxs")

            def wprep_sub(icc, sub, mat, wi, win, qw, qwn, slot, off, stage, wm, wmn):
                ib = icc % 2
                """Load one [2048f x 128ic] fp32 sub-chunk, compute its ws,
                bcast scale slices, quantize into qw[:, :, sub*128...]."""
                if stage == 2:
                    pass
                else:
                    s.op("G", (lambda off, wi: lambda g: pay_dma(
                            g, wi[:, :, :], off, [[16 * 512, 128], [512, 16], [1, 128]]))(off, wi),
                         reads=["pay"], writes=[win], tag=f"lw{mat}{icc}_{sub}")

                for rnd in ([] if stage == 2 else range(2)):
                    s.op("S", (lambda wi, rnd: lambda sc: sc.activation(
                            wabs[:, :, :], wi[:, 8 * rnd:8 * rnd + 8, :], AF.Abs))(wi, rnd),
                         reads=[win, wmn] if rnd else [win], writes=["wabs"],
                         tag=f"wx0{mat}{icc}_{sub}_{rnd}")
                    for lvl, st in ((1, 1), (2, 2)):
                        def ftree(st):
                            def f(v):
                                r = None
                                for a in range(0, 8, 2 * st):
                                    r = v.tensor_tensor(wabs[:, a, :], wabs[:, a, :],
                                                        wabs[:, a + st, :], op=ALU.max)
                                return r
                            return f
                        s.op("V", ftree(st), reads=["wabs"], writes=["wabs"],
                             tag=f"wx{lvl}{mat}{icc}_{sub}_{rnd}")
                    if rnd == 0:
                        s.op("V", (lambda wm: lambda v: v.tensor_tensor(wm[:, :], wabs[:, 0, :],
                                                                        wabs[:, 4, :], op=ALU.max))(wm),
                             reads=["wabs"], writes=[wmn], tag=f"wx3{mat}{icc}_{sub}_0")
                    else:
                        def flast(v):
                            v.tensor_tensor(wabs[:, 0, :], wabs[:, 0, :], wabs[:, 4, :], op=ALU.max)
                            return None
                        s.op("V", lambda v: v.tensor_tensor(wabs[:, 0, :], wabs[:, 0, :],
                                                            wabs[:, 4, :], op=ALU.max),
                             reads=["wabs"], writes=["wabs"], tag=f"wx3{mat}{icc}_{sub}_1")
                        s.op("V", (lambda wm: lambda v: v.tensor_tensor(wm[:, :], wm[:, :],
                                                                          wabs[:, 0, :], op=ALU.max))(wm),
                             reads=["wabs", wmn], writes=[wmn], tag=f"wx4{mat}{icc}_{sub}")
                if stage == 1:
                    return
                s.op("T", (lambda wm: lambda t: t.transpose(psw[:, 0:128], wm[:, :], eye32[:, :]))(wm),
                     reads=[wmn, "eye32"], writes=["psw"], tag=f"wt{mat}{icc}_{sub}")

                s.op("V", lambda v: v.reduce_max(wred[:, 0:1], psw[:, 0:128], axis=AX),
                     reads=["psw"], writes=["wred"], tag=f"wr{mat}{icc}_{sub}")
                s.op("V", lambda v: v.tensor_scalar(wred[:, 1:2], wred[:, 0:1], 1.0 / 448.0, None,
                                                    op0=ALU.mult),
                     reads=["wred"], writes=["wred"], tag=f"wr2{mat}{icc}_{sub}")
                s.op("T", lambda t: t.transpose(psw[0:1, 128:256], wred[:, 1:2], eye32[:, :]),
                     reads=["wred", "eye32"], writes=["psw"], tag=f"wfT{mat}{icc}_{sub}")
                s.op("V", (lambda sub: lambda v: v.tensor_copy(
                        wrow[0:1, sub * 128:(sub + 1) * 128], psw[0:1, 128:256]))(sub),
                     reads=["psw"], writes=["wrow"], tag=f"wfC{mat}{icc}_{sub}")

                s.op("V", (lambda sub: lambda v: v.reciprocal(
                        wrowt[0:1, 0:128], wrow[0:1, sub * 128:(sub + 1) * 128]))(sub),
                     reads=["wrow"], writes=["wrowt"], tag=f"wi{mat}{icc}_{sub}")
                s.op("V", lambda v: v.tensor_scalar(wrowt[0:1, 128:256], wrowt[0:1, 0:128], 0.5, None,
                                                    op0=ALU.mult),
                     reads=["wrowt"], writes=["wrowt"], tag=f"wi2{mat}{icc}_{sub}")

                def fbc(t):
                    t.matmul(psb[:, 0:128], ones32[0:1, 0:128], wrowt[0:1, 128:256], start=True, stop=True)
                    return t.matmul(psb[:, 128:256], ones32[0:1, 0:128],
                                    wrow[0:1, sub * 128:(sub + 1) * 128], start=True, stop=True)
                s.op("T", fbc, reads=["wrowt", "wrow", "ones32"], writes=["psb"], tag=f"wb{mat}{icc}_{sub}")

                sl = slot * 2 + ib

                def fev(v):
                    v.tensor_copy(wibc[:, slot, sub * 128:(sub + 1) * 128], psb[:, 0:128])
                    return v.tensor_copy(wsbc[:, sl, sub * 128:(sub + 1) * 128], psb[:, 128:256])
                s.op("V", fev, reads=["psb"], writes=["wibc", "wsbc"], tag=f"we{mat}{icc}_{sub}")

                def fq(wi, qw, sub, slot):
                    def f(v):
                        r = None
                        for a in range(16):
                            r = v.tensor_tensor(qw[:, a, sub * 128:(sub + 1) * 128], wi[:, a, :],
                                                wibc[:, slot, sub * 128:(sub + 1) * 128], op=ALU.mult)
                        return r
                    return f
                s.op("P", fq(wi, qw, sub, slot), reads=[win, "wibc"], writes=[qwn],
                     tag=f"wq{mat}{icc}_{sub}")

            def wprep_piece(icc2, j, stage=0):
                ib2 = icc2 % 2
                sub, mat = j // 2, ("g", "u")[j % 2]
                wm, wmn = wmac[j % 2], f"wmac{j % 2}"
                if mat == "g":
                    off = GRAN_OFF[("wg", icc2)] + sub * 128
                    wprep_sub(icc2, sub, "g", wgi, "wgi", qwg[ib2], f"qwg{ib2}", 0, off, stage, wm, wmn)
                else:
                    off = GRAN_OFF[("wu", icc2)] + sub * 128
                    wprep_sub(icc2, sub, "u", wui, "wui", qwu[ib2], f"qwu{ib2}", 1, off, stage, wm, wmn)

            for j in range(8):
                wprep_piece(0, j)
            for icc in range(11):
                ib = icc % 2
                def tail(tc):
                    pt = tc % 3
                    i2 = tc % 2
                    s.op("V", (lambda ib, i2: lambda v: v.tensor_tensor(
                            gtmp[i2][:, :], gtmp[i2][:, :], wsbc[:, 2 + ib, :], op=ALU.mult))(ib, i2),
                         reads=[f"gtmp{i2}", "wsbc"], writes=[f"gtmp{i2}"], tag=f"ue{icc}_{tc}")
                    s.op("V", (lambda tc, pt, i2: lambda v: v.scalar_tensor_tensor(
                            tstg[i2][:, :], psu[pt][:, :], rxs[:, tc:tc + 1], gtmp[i2][:, :],
                            op0=ALU.mult, op1=ALU.mult))(tc, pt, i2),
                         reads=[f"psu{pt}", "rxs", f"gtmp{i2}"], writes=[f"tstg{i2}"],
                         tag=f"tm{icc}_{tc}")
                    s.op("V", (lambda tc, i2: lambda v: v.reduce_max(
                            mtmp[:, i2:i2 + 1], tstg[i2][:, :], axis=AX,
                            apply_absolute_value=True))(tc, i2),
                         reads=[f"tstg{i2}"], writes=["mtmp"], tag=f"mxa{icc}_{tc}")
                    s.op("V", (lambda tc, i2: lambda v: v.tensor_tensor(
                            m2s[:, tc:tc + 1], m2s[:, tc:tc + 1], mtmp[:, i2:i2 + 1], op=ALU.max))(tc, i2),
                         reads=["mtmp", "m2s"], writes=["m2s"], tag=f"mxb{icc}_{tc}")
                    s.op("Q", (lambda icc, tc, i2: lambda g: g.dma_start(
                            out=t_sc.ap()[icc, :, tc, :], in_=tstg[i2][:, :]))(icc, tc, i2),
                         reads=[f"tstg{i2}"], writes=["t_sc"], tag=f"ts{icc}_{tc}")

                for tc in range(16):
                    pt = tc % 3
                    i2 = tc % 2
                    def fg(tc, ib, pt):
                        def f(t):
                            r = None
                            for a in range(8):
                                r = t.matmul(psg[pt][:, :], qx[:, 2 * a:2 * a + 2, tc * 128:(tc + 1) * 128],
                                             qwg[ib][:, 2 * a:2 * a + 2, :], start=(a == 0), stop=(a == 7),
                                             perf_mode=PM.DoubleRow)
                            return r
                        return f
                    s.op("T", fg(tc, ib, pt), reads=["qx0", "qx1", f"qwg{ib}"], writes=[f"psg{pt}"],
                         tag=f"gm{icc}_{tc}")

                    def fu(tc, ib, pt):
                        def f(t):
                            r = None
                            for a in range(8):
                                r = t.matmul(psu[pt][:, :], qx[:, 2 * a:2 * a + 2, tc * 128:(tc + 1) * 128],
                                             qwu[ib][:, 2 * a:2 * a + 2, :], start=(a == 0), stop=(a == 7),
                                             perf_mode=PM.DoubleRow)
                            return r
                        return f
                    s.op("T", fu(tc, ib, pt), reads=["qx0", "qx1", f"qwu{ib}"], writes=[f"psu{pt}"],
                         tag=f"um{icc}_{tc}")
                    s.op("V", (lambda tc, ib, pt, i2: lambda v: v.scalar_tensor_tensor(
                            gtmp[i2][:, :], psg[pt][:, :], rxs[:, tc:tc + 1], wsbc[:, ib, :],
                            op0=ALU.mult, op1=ALU.mult))(tc, ib, pt, i2),
                         reads=[f"psg{pt}", "rxs", "wsbc"], writes=[f"gtmp{i2}"], tag=f"ge{icc}_{tc}")
                    if tc % 2 == 1:
                        def gsp(sc):
                            sc.activation(gtmp[0][:, :], gtmp[0][:, :], AF.Silu)
                            return sc.activation(gtmp[1][:, :], gtmp[1][:, :], AF.Silu)
                        s.op("S", gsp, reads=["gtmp0", "gtmp1"],
                             writes=["gtmp0", "gtmp1"], tag=f"gs{icc}_{tc}")
                        tail(tc - 1)
                        tail(tc)
                    if icc + 1 < 11 and tc % 2 == 1:
                        j = tc // 2
                        wprep_piece(icc + 1, j, stage=1)
                        if j >= 1:
                            wprep_piece(icc + 1, j - 1, stage=2)
                if icc + 1 < 11:
                    wprep_piece(icc + 1, 7, stage=2)
            s.emit_phase(nc, ctx)
            gup.close()
            s.note_free()
            psw8 = [ph5b.enter_context(nc.psum_tensor(f"psw8_{i}", [128, 256], FP8)) for i in range(8)]
            for i in range(8):
                s.buf(f"psw8_{i}", xread=True)

            s.op("V", lambda v: v.reciprocal(tqs[:, :], m2s[:, :]),
                 reads=["m2s"], writes=["tqs"], tag="m2inv0")
            s.op("V", lambda v: v.tensor_scalar(tqs[:, :], tqs[:, :], 224.0, None, op0=ALU.mult),
                 reads=["tqs"], writes=["tqs"], tag="m2inv1")
            s.op("V", lambda v: v.tensor_scalar(m2f[:, :], m2s[:, :], 1.0 / 112.0, None, op0=ALU.mult),
                 reads=["m2s"], writes=["m2f"], tag="m2fold")

            tri = [ph5b.enter_context(nc.sbuf_tensor(f"tri{i}", [128, 8, 512], F32)) for i in range(2)]
            for icc in range(11):
                for half in range(2):
                    i2 = half
                    ti, tn = tri[i2], f"tri{i2}"
                    s.op("Q", (lambda icc, half, ti: lambda g: g.dma_start(
                            out=ti[:, :, :], in_=t_sc.ap()[icc, :, half * 8:(half + 1) * 8, :]))(icc, half, ti),
                         reads=["t_sc"], writes=[tn], tag=f"tl{icc}_{half}")
                    for tcl in range(8):
                        tc = half * 8 + tcl
                        kt8 = (icc * 16 + tc) % 2
                        bs = 4 * kt8
                        s.op("S", (lambda ti, tcl, tc, kt8: lambda sc: sc.activation(
                                tq8[kt8][:, :], ti[:, tcl, :], AF.Identity,
                                scale=tqs[:, tc:tc + 1]))(ti, tcl, tc, kt8),
                             reads=[tn, "tqs"], writes=[f"tq8_{kt8}"], tag=f"tq{icc}_{tc}")
                        for j in range(4):
                            s.op("T", (lambda j, kt8, bs: lambda t: t.transpose(
                                    s2ap(psw8[bs + j]), tq8[kt8][:, j * 128:(j + 1) * 128],
                                    eye8[:, :]))(j, kt8, bs),
                                 reads=[f"tq8_{kt8}", "eye8"], writes=[f"psw8_{bs + j}"],
                                 tag=f"tt{icc}_{tc}_{j}")

                        def fte(icc, tc, bs):
                            def f(v):
                                r = None
                                for j in range(4):
                                    r = v.tensor_copy(qt[:, 4 * icc + j, tc * 128:(tc + 1) * 128],
                                                      s2ap(psw8[bs + j]))
                                return r
                            return f
                        s.op("V", fte(icc, tc, bs),
                             reads=[f"psw8_{bs + j}" for j in range(4)],
                             writes=["qt"], tag=f"te{icc}_{tc}")
        s.emit_phase(nc, ctx)
        ph5b.close()
        sc5.close()
        s.note_free()

        ph5c = ExitStack()
        if phases >= 7:
            wdi2 = [ph5c.enter_context(nc.sbuf_tensor("wdi2_0", [128, 11, 512], F32))]
            wdq = [ph5c.enter_context(nc.sbuf_tensor(f"wdq{i}", [128, 44, 512], FP8)) for i in range(2)]
            ibcd = [ph5c.enter_context(nc.sbuf_tensor(f"ibcd{i}", [128, 512], F32)) for i in range(2)]
            sbcd = [ph5c.enter_context(nc.sbuf_tensor(f"sbcd{i}", [128, 512], F32)) for i in range(2)]
            rowt = ph5c.enter_context(nc.sbuf_tensor("rowt", [1, 512], F32))
            x2r = [ph5c.enter_context(nc.sbuf_tensor(f"x2r{i}", [128, 512], F32)) for i in range(2)]
            outs = [ph5c.enter_context(nc.sbuf_tensor(f"outs{i}", [128, 512], F32)) for i in range(2)]
            outt = [ph5c.enter_context(nc.sbuf_tensor(f"outt{i}", [128, 512], F32)) for i in range(2)]
            psd = [ph5c.enter_context(nc.psum_tensor(f"psd{i}", [128, 512], F32)) for i in range(4)]
            psb2 = ph5c.enter_context(nc.psum_tensor("psb2", [128, 512], F32))
            for i in range(4):
                s.buf(f"psd{i}", xread=True)
            s.buf("psb2", xread=True)

            for fc in range(4):
                fcp = fc % 2
                s.op("V", (lambda fc: lambda v: v.reciprocal(
                        rowt[0:1, :], wsd_row[0:1, fc * 512:(fc + 1) * 512]))(fc),
                     reads=["wsd_row"], writes=["rowt"], tag=f"div{fc}")
                s.op("V", lambda v: v.tensor_scalar(rowt[0:1, :], rowt[0:1, :], 0.5, None, op0=ALU.mult),
                     reads=["rowt"], writes=["rowt"], tag=f"div2{fc}")
                s.op("T", lambda t: t.matmul(psb2[:, :], ones32[0:1, 0:128], rowt[0:1, :],
                                             start=True, stop=True),
                     reads=["rowt", "ones32"], writes=["psb2"], tag=f"dbc{fc}")
                s.op("V", (lambda fcp: lambda v: v.tensor_copy(ibcd[fcp][:, :], psb2[:, :]))(fcp),
                     reads=["psb2"], writes=[f"ibcd{fcp}"], tag=f"dbe{fc}")
                s.op("V", (lambda fc: lambda v: v.tensor_copy(
                        rowt[0:1, :], wsd_row[0:1, fc * 512:(fc + 1) * 512]))(fc),
                     reads=["wsd_row"], writes=["rowt"], tag=f"dsv{fc}")
                s.op("T", lambda t: t.matmul(psb2[:, :], ones32[0:1, 0:128], rowt[0:1, :],
                                             start=True, stop=True),
                     reads=["rowt", "ones32"], writes=["psb2"], tag=f"dbc2{fc}")
                s.op("V", (lambda fcp: lambda v: v.tensor_copy(sbcd[fcp][:, :], psb2[:, :]))(fcp),
                     reads=["psb2"], writes=[f"sbcd{fcp}"], tag=f"dbe2{fc}")
                for g4 in range(4):
                    gp = 0
                    off = GRAN_OFF[("wd", fc * 4 + g4)]
                    s.op("G", (lambda off: lambda g: pay_dma(
                            g, wdi2[0][:, :, :], off, [[512, 128], [128 * 512, 11], [1, 512]]))(off),
                         reads=["pay"], writes=["wdi2_0"], tag=f"dl{fc}_{g4}")

                    def fqd(g4, gp, fcp):
                        def f(v):
                            r = None
                            for j in range(11):
                                r = v.tensor_tensor(wdq[fcp][:, g4 * 11 + j, :], wdi2[gp][:, j, :],
                                                    ibcd[fcp][:, :], op=ALU.mult)
                            return r
                        return f
                    s.op("V", fqd(g4, gp, fcp), reads=[f"wdi2_{gp}", f"ibcd{fcp}"],
                         writes=[f"wdq{fcp}"], tag=f"dq{fc}_{g4}")
                for tc in range(16):
                    pi = tc % 4
                    i2 = tc % 2

                    def fdm(fc, tc, pi, fcp):
                        def f(t):
                            r = None
                            for icc in range(22):
                                r = t.matmul(psd[pi][:, :],
                                             qt[:, 2 * icc:2 * icc + 2, tc * 128:(tc + 1) * 128],
                                             wdq[fcp][:, 2 * icc:2 * icc + 2, :],
                                             start=(icc == 0), stop=(icc == 21),
                                             perf_mode=PM.DoubleRow)
                            return r
                        return f
                    s.op("T", fdm(fc, tc, pi, fcp), reads=["qt", f"wdq{fcp}"], writes=[f"psd{pi}"],
                         tag=f"dm{fc}_{tc}")
                    s.op("Q", (lambda fc, tc, i2: lambda g: g.dma_start(
                            out=x2r[i2][:, :],
                            in_=x2_sc.ap()[tc, :, fc * 512:(fc + 1) * 512]))(fc, tc, i2),
                         reads=["x2_sc"], writes=[f"x2r{i2}"], tag=f"dx{fc}_{tc}")

                    s.op("V", (lambda tc, pi, i2, fcp: lambda v: v.scalar_tensor_tensor(
                            outt[i2][:, :], psd[pi][:, :], m2f[:, tc:tc + 1],
                            sbcd[fcp][:, :], op0=ALU.mult, op1=ALU.mult))(tc, pi, i2, fcp),
                         reads=[f"psd{pi}", "m2f", f"sbcd{fcp}"], writes=[f"outt{i2}"], tag=f"deA{fc}_{tc}")
                    s.op("V", (lambda pi, i2: lambda v: v.tensor_tensor(
                            outs[i2][:, :], outt[i2][:, :], x2r[i2][:, :], op=ALU.add))(pi, i2),
                         reads=[f"outt{i2}", f"x2r{i2}"], writes=[f"outs{i2}"], tag=f"deB{fc}_{tc}")
                    s.op("G", (lambda fc, tc, i2: lambda g: g.dma_start(
                            out=out_d.ap()[tc * 128:(tc + 1) * 128, fc * 512:(fc + 1) * 512],
                            in_=outs[i2][:, :]))(fc, tc, i2),
                         reads=[f"outs{i2}"], writes=["outdone"], tag=f"do{fc}_{tc}")
            s.op("G", lambda g: g.nop(), reads=["outdone"], tag="final", barrier=True)
        s.emit_phase(nc, ctx)
        ph5c.close()
        sc5bc.close()
        s.note_free()

        s.emit_phase(nc, ctx)
    return nc


def _pack_payload(x, wqkvu, wout, wg_p, wu_p, wd):
    pay = np.zeros((8, S_SH), np.float32)
    flat = pay.reshape(-1)

    def put(key, arr):
        off = GRAN_OFF[key]
        flat[off:off + arr.size] = arr.ravel()

    wq = np.ascontiguousarray(wqkvu.T).reshape(16, 128, 16, 512).transpose(2, 1, 0, 3)
    wqk = np.ascontiguousarray(wq).reshape(1, N_WQK * GR_E)
    wo = np.ascontiguousarray(wout.T).reshape(16, 128, D).transpose(1, 0, 2)
    wo4 = wo.reshape(128, 16, 4, 512).transpose(2, 0, 1, 3)
    for i in range(4):
        put(("wout", i), wo4[i])
    for base, w in (("wg", wg_p), ("wu", wu_p)):
        wt = np.ascontiguousarray(w.T).reshape(16, 128, 11, 512).transpose(2, 1, 0, 3)
        for i in range(11):
            put((base, i), wt[i])
    wdt = np.ascontiguousarray(wd.T).reshape(44, 128, 4, 512).transpose(2, 0, 1, 3)
    for fc in range(4):
        for g4 in range(4):
            put(("wd", fc * 4 + g4), wdt[fc, 11 * g4:11 * (g4 + 1)])
    return pay, wqk


def _consts():
    arrq = np.broadcast_to(np.arange(1, 257, dtype=np.float32), (128, 256)).copy()
    p = np.arange(128, dtype=np.float32)
    kdc = np.stack([255.0 - p, 127.0 - p], axis=1).astype(np.float32)
    cm = np.empty((2, 128, 256), np.float32)
    for st in range(2):
        sidx = st * 128 + np.arange(128)[:, None]
        tidx = np.arange(256)[None, :]
        d = tidx - sidx
        cm[st] = np.where(d >= 0, d, 1.0e9).astype(np.float32)
    eye32 = np.eye(128, dtype=np.float32)
    eyebf = np.eye(128, dtype=ml_dtypes.bfloat16)
    eye8 = np.eye(128, dtype=ml_dtypes.float8_e4m3)
    onesbf = np.ones((128, 128), ml_dtypes.bfloat16)
    ones32 = np.ones((1, 128), np.float32)
    return arrq, kdc, cm, eye32, eyebf, eye8, onesbf, ones32


def kernel(L0_attn_x, L0_attn_slope_rate, L0_attn_W_qkvu, L0_attn_W_out,
           L0_ffn_w_gate, L0_ffn_w_up, L0_ffn_w_down,
           L0_ffn_up_reorder_index, L0_ffn_down_reorder_index):
    x = np.asarray(L0_attn_x, np.float32)
    wqkvu = np.asarray(L0_attn_W_qkvu, np.float32)
    wout = np.asarray(L0_attn_W_out, np.float32)
    didx = np.asarray(L0_ffn_down_reorder_index).astype(np.int64)
    wg_p = np.take(np.asarray(L0_ffn_w_gate, np.float32), didx, axis=0)
    wu_p = np.take(np.asarray(L0_ffn_w_up, np.float32), didx, axis=0)
    wd = np.asarray(L0_ffn_w_down, np.float32)
    slope = np.asarray(L0_attn_slope_rate, np.float32).reshape(1, 16)
    slope_b = np.broadcast_to(slope, (128, 16)).copy()

    pay, wqk = _pack_payload(x, wqkvu, wout, wg_p, wu_p, wd)
    arrq, kdc, cm, eye32, eyebf, eye8, onesbf, ones32 = _consts()

    nc = build_program()
    in_maps = []
    for c in range(8):
        seq, half = c // 2, c % 2
        x_own = np.ascontiguousarray(x[seq, half * T:(half + 1) * T], np.float32)
        x_prev = np.ascontiguousarray(x[seq, 0:T], np.float32)
        cpkv = np.zeros((128, 24), np.float32)
        cpkv[:, 0:2] = kdc
        cpkv[:, 2:18] = slope_b
        cpkv[:, 18] = 1.0 if half == 1 else 0.0
        cpkv[:, 19] = EPS1
        cpkv[:, 20] = EPS2
        in_maps.append({
            "shard": pay[c:c + 1], "wqk": wqk, "xin": x_own, "xpr": x_prev,
            "cpk": cpkv,
            "arrq": arrq, "cmask": cm,
            "eye32": eye32, "eyebf": eyebf, "eye8": eye8,
            "onesbf": onesbf, "ones32": ones32,
        })
    res = run_bass_kernel_spmd(nc, in_maps, core_ids=list(range(8)))
    out = np.empty((B, N, D), np.float32)
    for c in range(8):
        seq, half = c // 2, c % 2
        out[seq, half * T:(half + 1) * T] = res.results[c]["out"]
    return out


# version 15 (final)
# speedup vs baseline: 1.1247x; 1.0373x over previous
import os
os.environ["NEURON_SCRATCHPAD_PAGE_SIZE"] = "512"
import numpy as np
import ml_dtypes
from contextlib import ExitStack

import concourse.bass as bass
import concourse.mybir as mybir
from concourse.bass_utils import run_bass_kernel_spmd

F32 = mybir.dt.float32
BF16 = mybir.dt.bfloat16
FP8 = mybir.dt.float8e4
I32 = mybir.dt.int32
AF = mybir.ActivationFunctionType
ALU = mybir.AluOpType
AX = mybir.AxisListType.X
PM = mybir.MatmulPerfMode

B, N, D = 4, 4096, 2048
H, DH = 16, 128
BLK = 256
T = 2048
NB = T // BLK
IC = 5632
EPS1 = 1e-5
EPS2 = 1e-6

_x_own_elems = 0
GR_E = 128 * 16 * 512
N_WQK = 16
_GRAN = []
for _i in range(4):
    _GRAN.append(("wout", _i, GR_E))
for _i in range(11):
    _GRAN.append(("wg", _i, GR_E))
for _i in range(11):
    _GRAN.append(("wu", _i, GR_E))
for _i in range(16):
    _GRAN.append(("wd", _i, 44 * 128 * 128))
S_SH = 5_636_096
_big = [g for g in _GRAN if g[0] != "wd"]
_wdg = [g for g in _GRAN if g[0] == "wd"]
assert len(_big) == 26 and len(_wdg) == 16
_assign = []
_bi = 0
for _s in range(8):
    _nb = 4 if _s < 2 else 3
    _assign.append(_big[_bi:_bi + _nb] + _wdg[2 * _s:2 * _s + 2])
    _bi += _nb
GRAN_OFF = {}
for _s, _gl in enumerate(_assign):
    _off = _x_own_elems
    for _g in _gl:
        GRAN_OFF[(_g[0], _g[1])] = _s * S_SH + _off
        _off += _g[2]
    assert _off <= S_SH
PAY_TOT = 8 * S_SH


class _Buf:
    __slots__ = ("name", "last_write", "readers", "xread", "fence")

    def __init__(self, name, xread, fence=()):
        self.name = name
        self.last_write = None
        self.readers = []
        self.xread = xread
        self.fence = list(fence)


class _Op:
    __slots__ = ("engine", "emit", "inc_amount", "needs_inc", "cum", "deps",
                 "tag", "barrier", "phase")

    def __init__(self, engine, emit, inc_amount, tag, phase):
        self.engine = engine
        self.emit = emit
        self.inc_amount = inc_amount
        self.needs_inc = False
        self.cum = -1
        self.deps = []
        self.tag = tag
        self.barrier = False
        self.phase = phase


class Sched:
    SECTION = {"T": "tensor", "V": "vector", "S": "scalar", "Q": "sync",
               "G": "gpsimd", "C": "gpsimd", "P": "gpsimd", "R": "scalar"}

    def __init__(self):
        self.pending = {k: [] for k in self.SECTION.values()}
        self.bufs = {}
        self.phase = 0
        self.tails = {}
        self.free_fence = {}
        self.sems = None
        self.cum = {k: 0 for k in ("T", "V", "S", "Q", "G", "C", "P", "R")}
        self.waited = {sec: {k: 0 for k in ("T", "V", "S", "Q", "G", "C", "P", "R")}
                       for sec in set(self.SECTION.values())}
        self.prev_dma = {"Q": 0, "G": 0, "R": 0}

    def buf(self, name, xread=False):
        b = self.bufs.get(name)
        if b is None:
            b = self.bufs[name] = _Buf(name, xread, self.free_fence.values())
        if xread:
            b.xread = True
        return b

    def op(self, engine, emit, reads=(), writes=(), tag="", barrier=False):
        rd = [self.buf(b) if isinstance(b, str) else b for b in reads]
        wr = [self.buf(b) if isinstance(b, str) else b for b in writes]
        o = _Op(engine, emit, 16 if engine in "QGR" else 1, tag, self.phase)
        o.barrier = barrier
        if engine in "QGCR" and not barrier:
            o.needs_inc = True
        deps = []
        for b in rd:
            if b.last_write is not None:
                deps.append(b.last_write)
            if b.xread:
                for r in b.readers:
                    if r.engine != engine:
                        deps.append(r)
        for b in wr:
            deps.extend(b.readers)
            if b.last_write is not None:
                deps.append(b.last_write)
            elif not b.readers:
                deps.extend(b.fence)
        res = []
        for d in deps:
            if d is o:
                continue
            if d.phase == self.phase:
                if d.barrier:
                    raise RuntimeError(f"dep on barrier op {d.tag} <- {tag}")
                d.needs_inc = True
                res.append(d)
            elif d.needs_inc:
                res.append(d)
            else:
                res.extend(self.tails.values())
        seen = set()
        o.deps = [d for d in res if not (id(d) in seen or seen.add(id(d)))]
        for b in rd:
            b.readers.append(o)
        for b in wr:
            b.last_write = o
            b.readers = []
        self.pending[self.SECTION[engine]].append(o)
        return o

    def emit_phase(self, nc, ctx):
        if self.sems is None:
            self.sems = {k: ctx.enter_context(nc.semaphore(f"sem_{k}"))
                         for k in ("T", "V", "S", "Q", "G", "C", "P", "R")}
        for sec, ops in self.pending.items():
            if ops and not ops[-1].barrier:
                ops[-1].needs_inc = True
        for sec, ops in self.pending.items():
            for o in ops:
                if o.needs_inc:
                    self.cum[o.engine] += o.inc_amount
                    o.cum = self.cum[o.engine]
        sched = self

        def make_body(section):
            def body(eng):
                waited = sched.waited[section]
                for o in sched.pending[section]:
                    if o.engine in sched.prev_dma:
                        pd = sched.prev_dma[o.engine]
                        if pd > waited[o.engine]:
                            eng.wait_ge(sched.sems[o.engine], pd)
                            waited[o.engine] = pd
                    for d in o.deps:
                        if d.cum <= 0:
                            raise RuntimeError(f"dep no inc: {d.tag}<-{o.tag}")
                        if d.cum > waited[d.engine]:
                            eng.wait_ge(sched.sems[d.engine], d.cum)
                            waited[d.engine] = d.cum
                    inst = o.emit(eng)
                    if o.needs_inc:
                        if inst is None:
                            raise RuntimeError(f"{o.tag}: inc w/o inst")
                        inst.then_inc(sched.sems[o.engine], o.inc_amount)
                        if o.engine in sched.prev_dma:
                            sched.prev_dma[o.engine] = o.cum
            return body

        active = [sec for sec in ("sync", "gpsimd", "vector", "scalar", "tensor")
                  if self.pending[sec]]
        if active:
            with nc.Block() as blk:
                for sec in active:
                    getattr(blk, sec)(make_body(sec))
        for sec, ops in self.pending.items():
            if ops:
                for o in reversed(ops):
                    if o.needs_inc:
                        self.tails[o.engine] = o
                        break
        self.pending = {k: [] for k in self.SECTION.values()}
        self.phase += 1

    def note_free(self):
        self.free_fence = dict(self.tails)


def build_program(phases=99, probes=()):
    nc = bass.Bass("TRN2", target_bir_lowering=False)

    shard = nc.dram_tensor("shard", [1, S_SH], F32, kind="ExternalInput")
    wqk_d = nc.dram_tensor("wqk", [1, N_WQK * GR_E], F32, kind="ExternalInput")
    xin_d = nc.dram_tensor("xin", [T, D], F32, kind="ExternalInput")
    xpr_d = nc.dram_tensor("xpr", [T, D], F32, kind="ExternalInput")
    cpk_d = nc.dram_tensor("cpk", [128, 24], F32, kind="ExternalInput")
    arrq_d = nc.dram_tensor("arrq", [128, 256], F32, kind="ExternalInput")
    cmask_d = nc.dram_tensor("cmask", [2, 128, 256], F32, kind="ExternalInput")
    eye32_d = nc.dram_tensor("eye32", [128, 128], F32, kind="ExternalInput")
    eyebf_d = nc.dram_tensor("eyebf", [128, 128], BF16, kind="ExternalInput")
    eye8_d = nc.dram_tensor("eye8", [128, 128], FP8, kind="ExternalInput")
    onesbf_d = nc.dram_tensor("onesbf", [128, 128], BF16, kind="ExternalInput")
    ones32_d = nc.dram_tensor("ones32", [1, 128], F32, kind="ExternalInput")
    out_d = nc.dram_tensor("out", [T, D], F32, kind="ExternalOutput")

    shard_b = nc.dram_tensor("shard_b", [1, S_SH], F32)
    pay = nc.dram_tensor("pay", [8, S_SH], F32, addr_space="Shared")

    q_sc = nc.dram_tensor("q_sc", [H, 128, T], BF16)
    k_sc = nc.dram_tensor("k_sc", [H, 128, T], BF16)
    u_sc = nc.dram_tensor("u_sc", [H, 128, T], BF16)
    kp_sc = nc.dram_tensor("kp_sc", [H, 128, T], BF16)
    v_sc = nc.dram_tensor("v_sc", [4, 16, 128, 512], BF16)
    vp_sc = nc.dram_tensor("vp_sc", [4, 16, 128, 512], BF16)
    x2_sc = nc.dram_tensor("x2_sc", [16, 128, D], F32)
    t_sc = nc.dram_tensor("t_sc", [11, 128, 16, 512], F32)

    s = Sched()

    with ExitStack() as ctx:
        e = ctx.enter_context

        cpk = e(nc.sbuf_tensor("cpk_s", [128, 24], F32))
        eye32 = e(nc.sbuf_tensor("eye32_s", [128, 128], F32))
        eyebf = e(nc.sbuf_tensor("eyebf_s", [128, 128], BF16))
        eye8 = e(nc.sbuf_tensor("eye8_s", [128, 128], FP8))
        onesbf = e(nc.sbuf_tensor("onesbf_s", [128, 128], BF16))
        ones32 = e(nc.sbuf_tensor("ones32_s", [1, 128], F32))
        mxs = e(nc.sbuf_tensor("mxs", [128, 16], F32))
        m2s = e(nc.sbuf_tensor("m2s", [128, 16], F32))
        m2f = e(nc.sbuf_tensor("m2f", [128, 16], F32))
        tqs = e(nc.sbuf_tensor("tqs", [128, 16], F32))
        rxs = e(nc.sbuf_tensor("rxs", [128, 16], F32))

        for nm, dst, src in (("cpk", cpk, cpk_d),
                             ("eye32", eye32, eye32_d),
                             ("eyebf", eyebf, eyebf_d), ("eye8", eye8, eye8_d),
                             ("onesbf", onesbf, onesbf_d), ("ones32", ones32, ones32_d)):
            s.op("Q", (lambda dst, src: lambda g: g.dma_start(out=dst[...], in_=src[...]))(dst, src),
                 writes=[nm], tag=f"ld_{nm}")

        s.op("G", lambda g: g.dma_start(out=shard_b[:, :], in_=shard[:, :]),
             writes=["shard_b"], tag="bounce")
        s.op("C", lambda g: g.collective_compute(
                "AllGather", ALU.bypass, replica_groups=[list(range(8))],
                ins=[shard_b.ap().opt()], outs=[pay.ap().opt()]),
             reads=["shard_b"], writes=["pay"], tag="ag")


        def pay_dma(eng, dst_ap, elem_off, dims, base=None):
            return eng.dma_start(out=dst_ap, in_=bass.AP(base if base is not None else pay,
                                                         elem_off, dims))

        def s2ap(guard, n=128):
            h = guard[0:1, 0:1].tensor
            return bass.AP(h, 0, [[2 * n, 128], [2, n]])

        sc12 = ExitStack()
        xl = sc12.enter_context(nc.sbuf_tensor("xl", [128, 16, T], BF16))
        xlp = sc12.enter_context(nc.sbuf_tensor("xlp", [128, 16, T], BF16))
        ph2 = ExitStack()
        wb2 = [ph2.enter_context(nc.sbuf_tensor(f"wb2_{i}", [128, 16, 512], BF16)) for i in range(2)]
        stg = [ph2.enter_context(nc.sbuf_tensor(f"stg{i}", [128, T], BF16)) for i in range(2)]
        vst = [ph2.enter_context(nc.sbuf_tensor(f"vst{i}", [128, 512], BF16)) for i in range(4)]
        wstg = ph2.enter_context(nc.sbuf_tensor("wstg", [128, 4, 512], F32))
        psp = [ph2.enter_context(nc.psum_tensor(f"psp{i}", [128, 512], F32)) for i in range(4)]
        s.buf("wstg")
        for i in range(2):
            s.buf(f"wb2_{i}")
            s.buf(f"stg{i}")
        for i in range(4):
            s.buf(f"vst{i}")
            s.buf(f"psp{i}", xread=True)
        ph1 = ExitStack()
        xin = [ph1.enter_context(nc.sbuf_tensor(f"xin{i}", [128, D], F32)) for i in range(2)]
        xlt = [ph1.enter_context(nc.sbuf_tensor(f"xlt{i}", [128, D], BF16)) for i in range(2)]
        st1 = [ph1.enter_context(nc.sbuf_tensor(f"st1_{i}", [128, 8], F32)) for i in range(2)]
        ps1 = [ph1.enter_context(nc.psum_tensor(f"ps1t_{i}", [128, 128], BF16)) for i in range(2)]
        s.buf("ps1t_0", xread=True)
        s.buf("ps1t_1", xread=True)

        def ln_chunk(which, tc, dest, destn, stage=0):
            i2 = (which * 16 + tc) % 2
            xi, xt = xin[i2], xlt[i2]
            xn, tn = f"xin{i2}", f"xlt{i2}"
            st1_ = st1[i2]
            sn1 = f"st1_{i2}"
            tg = f"{which}_{tc}"
            xsrc = xin_d if which == 0 else xpr_d
            xsn = "xin" if which == 0 else "xpr"
            if stage == 2:
                for a in range(16):
                    ai = a % 2
                    xt = xlt[i2]
                    tn = f"xlt{i2}"
                    s.op("T", (lambda a, xt, ai: lambda t: t.transpose(
                            ps1[ai][:, :], xt[:, a * 128:(a + 1) * 128], eyebf[:, :]))(a, xt, ai),
                         reads=[tn, "eyebf"], writes=[f"ps1t_{ai}"], tag=f"tr{tg}_{a}")
                    if ai == 0:
                        s.op("S", (lambda a, tc, dest, ai: lambda sc: sc.activation(
                                dest[:, a, tc * 128:(tc + 1) * 128], ps1[ai][:, :], AF.Identity))(a, tc, dest, ai),
                             reads=[f"ps1t_{ai}"], writes=[f"{destn}0"], tag=f"te{tg}_{a}")
                    else:
                        s.op("V", (lambda a, tc, dest, ai: lambda v: v.tensor_copy(
                                dest[:, a, tc * 128:(tc + 1) * 128], ps1[ai][:, :]))(a, tc, dest, ai),
                             reads=[f"ps1t_{ai}"], writes=[f"{destn}1"], tag=f"te{tg}_{a}")
                return
            ldq = "R" if (which == 0 and tc % 2 == 1) or which == 1 else "Q"
            s.op(ldq, (lambda xi, tc, xsrc: lambda g: g.dma_start(
                    out=xi[:, :], in_=xsrc.ap()[tc * 128:(tc + 1) * 128, :]))(xi, tc, xsrc),
                 reads=[xsn], writes=[xn], tag=f"ldx{tg}")
            s.op("V", (lambda xi, st1_: lambda v: v.reduce_sum(st1_[:, 0:1], xi[:, :], axis=AX))(xi, st1_),
                 reads=[xn], writes=[sn1], tag=f"sum{tg}")
            s.op("S", (lambda xi, st1_, xt: lambda sc: sc.activation(
                    xt[:, :], xi[:, :], AF.Square, accum_out=st1_[:, 1:2]))(xi, st1_, xt),
                 reads=[xn], writes=[tn, sn1], tag=f"sq{tg}")

            def fstat(st1_):
                def f(v):
                    v.tensor_scalar(st1_[:, 2:3], st1_[:, 0:1], 1.0 / D, None, op0=ALU.mult)
                    return v.tensor_scalar(st1_[:, 3:4], st1_[:, 1:2], 1.0 / D, None, op0=ALU.mult)
                return f
            s.op("V", fstat(st1_), reads=[sn1], writes=[sn1], tag=f"st{tg}")
            s.op("V", (lambda st1_: lambda v: v.tensor_tensor(
                    st1_[:, 4:5], st1_[:, 2:3], st1_[:, 2:3], op=ALU.mult))(st1_),
                 reads=[sn1], writes=[sn1], tag=f"st2{tg}")
            s.op("V", (lambda st1_: lambda v: v.tensor_tensor(
                    st1_[:, 3:4], st1_[:, 3:4], st1_[:, 4:5], op=ALU.subtract))(st1_),
                 reads=[sn1], writes=[sn1], tag=f"st3{tg}")
            s.op("S", (lambda st1_: lambda sc: sc.activation(
                    st1_[:, 5:6], st1_[:, 3:4], AF.Sqrt, bias=cpk[:, 19:20]))(st1_),
                 reads=[sn1, "cpk"], writes=[sn1], tag=f"sqr{tg}")
            s.op("V", (lambda st1_: lambda v: v.reciprocal(st1_[:, 6:7], st1_[:, 5:6]))(st1_),
                 reads=[sn1], writes=[sn1], tag=f"rc{tg}")
            s.op("V", (lambda xi, xt, st1_: lambda v: v.tensor_scalar(
                    xt[:, :], xi[:, :], st1_[:, 2:3], st1_[:, 6:7],
                    op0=ALU.subtract, op1=ALU.mult))(xi, xt, st1_),
                 reads=[xn, sn1], writes=[tn], tag=f"nm{tg}")
            if stage == 0:
                ln_chunk(which, tc, dest, destn, stage=2)

        CAST_SET = (0, 1, 2, 3, 4, 5, 6, 7, 12, 13, 14, 15)

        def emit_oc(oc):
                wb, wn = wb2[oc % 2], f"wb2_{oc%2}"
                if oc in CAST_SET:
                    for h4 in range(4):
                        s.op("Q", (lambda oc, h4: lambda g: g.dma_start(
                                out=wstg[:, :, :],
                                in_=bass.AP(wqk_d, oc * GR_E + h4 * 4 * 512,
                                            [[16 * 512, 128], [512, 4], [1, 512]])))(oc, h4),
                             reads=["wqk"], writes=["wstg"], tag=f"lws{oc}_{h4}")
                        s.op("V", (lambda wb, h4: lambda v: v.tensor_copy(
                                wb[:, 4 * h4:4 * h4 + 4, :], wstg[:, :, :]))(wb, h4),
                             reads=["wstg"], writes=[wn], tag=f"cst{oc}_{h4}")
                else:
                    s.op("G", (lambda oc, wb: lambda g: pay_dma(
                            g, wb[:, :, :], oc * GR_E, [[16 * 512, 128], [512, 16], [1, 512]],
                            base=wqk_d))(oc, wb),
                         reads=["wqk"], writes=[wn], tag=f"ldw{oc}")
                fam = ("q", "k", "v", "u")[oc // 4]
                if fam != "v":
                    srcs = [("o", xl, "xl")]
                    if fam == "k":
                        srcs.append(("p", xlp, "xlp"))
                    for tagp, xsrc, xsn in srcs:
                        for sub in range(4):
                            head = (oc % 4) * 4 + sub
                            sg, sn = stg[sub % 2], f"stg{sub%2}"
                            for tch in range(4):
                                pi = tch % 4

                                def fmm(wb, sub, xsrc, tch, pi):
                                    def f(t):
                                        r = None
                                        for a in range(16):
                                            r = t.matmul(psp[pi][:, :],
                                                         wb[:, a, sub * 128:(sub + 1) * 128],
                                                         xsrc[:, a, tch * 512:(tch + 1) * 512],
                                                         start=(a == 0), stop=(a == 15))
                                        return r
                                    return f
                                s.op("T", fmm(wb, sub, xsrc, tch, pi),
                                     reads=[wn, f"{xsn}0", f"{xsn}1"],
                                     writes=[f"psp{pi}"], tag=f"mm{oc}{tagp}_{sub}_{tch}")
                                func = AF.Silu if fam in ("q", "k") else AF.Identity
                                s.op("S", (lambda sg, tch, pi, func: lambda sc: sc.activation(
                                        sg[:, tch * 512:(tch + 1) * 512], psp[pi][:, :], func))(sg, tch, pi, func),
                                     reads=[f"psp{pi}"], writes=[sn], tag=f"ev{oc}{tagp}_{sub}_{tch}")
                            dst = {"q": q_sc, "k": k_sc, "u": u_sc}[fam]
                            dnm = fam + "_sc"
                            if tagp == "p":
                                dst, dnm = kp_sc, "kp_sc"
                            s.op("G", (lambda dst, head, sg: lambda g: g.dma_start(
                                    out=dst[head], in_=sg[:, :]))(dst, head, sg),
                                 reads=[sn], writes=[dnm], tag=f"sp{fam}{tagp}{head}")
                else:
                    vcc = oc - 8
                    for tagp, xsrc, xsn, dst, dnm in (("o", xl, "xl", v_sc, "v_sc"),
                                                      ("p", xlp, "xlp", vp_sc, "vp_sc")):
                        for tc in range(16):
                            pi = tc % 4

                            def fmmv(wb, tc, xsrc, pi):
                                def f(t):
                                    r = None
                                    for a in range(16):
                                        r = t.matmul(psp[pi][:, :],
                                                     xsrc[:, a, tc * 128:(tc + 1) * 128],
                                                     wb[:, a, :], start=(a == 0), stop=(a == 15))
                                    return r
                                return f
                            s.op("T", fmmv(wb, tc, xsrc, pi),
                                 reads=[wn, f"{xsn}0", f"{xsn}1"],
                                 writes=[f"psp{pi}"], tag=f"vm{oc}{tagp}_{tc}")
                            vs, vn = vst[tc % 4], f"vst{tc%4}"
                            s.op("S", (lambda vs, pi: lambda sc: sc.activation(
                                    vs[:, :], psp[pi][:, :], AF.Identity))(vs, pi),
                                 reads=[f"psp{pi}"], writes=[vn], tag=f"vev{oc}{tagp}_{tc}")
                            s.op("G", (lambda dst, vcc, tc, vs: lambda g: g.dma_start(
                                    out=dst[vcc, tc], in_=vs[:, :]))(dst, vcc, tc, vs),
                                 reads=[vn], writes=[dnm], tag=f"vsp{oc}{tagp}_{tc}")
        if phases >= 1:
            for tc in range(16):
                ln_chunk(0, tc, xl, "xl", stage=1)
                if tc >= 1:
                    ln_chunk(0, tc - 1, xl, "xl", stage=2)
            ln_chunk(0, 15, xl, "xl", stage=2)
        if phases >= 2:
            for k, oc in enumerate((0, 1, 2, 3)):
                emit_oc(oc)
                if phases >= 1:
                    for j in range(4):
                        tc = 4 * k + j
                        ln_chunk(1, tc, xlp, "xlp", stage=1)
                        if tc >= 1:
                            ln_chunk(1, tc - 1, xlp, "xlp", stage=2)
            if phases >= 1:
                ln_chunk(1, 15, xlp, "xlp", stage=2)
            for oc in (4, 5, 6, 7, 12, 13, 14, 15, 8, 9, 10, 11):
                emit_oc(oc)
        elif phases >= 1:
            for tc in range(16):
                ln_chunk(1, tc, xlp, "xlp")
        s.emit_phase(nc, ctx)
        ph1.close()
        ph2.close()
        sc12.close()
        s.note_free()

        sc5bc = ExitStack()
        wsd_row = sc5bc.enter_context(nc.sbuf_tensor("wsd_row", [1, D], F32))
        sc34 = ExitStack()
        og = sc34.enter_context(nc.sbuf_tensor("og", [128, 16, T], BF16))
        ph3 = ExitStack()
        if phases >= 3:
            arrq = ph3.enter_context(nc.sbuf_tensor("arrq_s", [128, 256], F32))
            cmask = ph3.enter_context(nc.sbuf_tensor("cmask_s", [128, 2, 256], F32))
            s.op("Q", lambda g: g.dma_start(out=arrq[:, :], in_=arrq_d[:, :]),
                 writes=["arrq"], tag="ld_arrq")
            s.op("Q", lambda g: g.dma_start(out=cmask[:, :, :],
                                            in_=cmask_d.ap().rearrange("a p t -> p a t")),
                 writes=["cmask"], tag="ld_cmask")
            qdec = ph3.enter_context(nc.sbuf_tensor("qdec", [128, 16, 256], BF16))
            mask = ph3.enter_context(nc.sbuf_tensor("mask", [128, 16, 2, 256], BF16))
            kdec = ph3.enter_context(nc.sbuf_tensor("kdec", [128, 16, 2], F32))
            bdc = ph3.enter_context(nc.sbuf_tensor("bdc", [128, 16], F32))
            tmpd = ph3.enter_context(nc.sbuf_tensor("tmpd", [128, 2, 256], F32))
            kv = ph3.enter_context(nc.sbuf_tensor("kv", [128, 16, 128], F32))
            kvb = [ph3.enter_context(nc.sbuf_tensor(f"kvb{i}", [128, 128], BF16)) for i in range(2)]
            qb = [ph3.enter_context(nc.sbuf_tensor(f"qb{i}", [128, 16, 256], BF16)) for i in range(2)]
            kb = [ph3.enter_context(nc.sbuf_tensor(f"kb{i}", [128, 16, 256], BF16)) for i in range(2)]
            vb = [ph3.enter_context(nc.sbuf_tensor(f"vb{i}", [128, 2, 2048], BF16)) for i in range(2)]
            ub = [ph3.enter_context(nc.sbuf_tensor("ub0", [128, 16, 256], BF16))]
            kqm = [ph3.enter_context(nc.sbuf_tensor(f"kqm{i}", [128, 2, 256], BF16)) for i in range(2)]
            ktda = [ph3.enter_context(nc.sbuf_tensor(f"ktda{i}", [128, 2, 16, 128], BF16)) for i in range(2)]
            qdq = [ph3.enter_context(nc.sbuf_tensor(f"qdq{i}", [128, 256], BF16)) for i in range(2)]
            obf = ph3.enter_context(nc.sbuf_tensor("obf", [128, 16, 256], BF16))
            osq = ph3.enter_context(nc.sbuf_tensor("osq", [128, 16, 256], BF16))
            rsq = ph3.enter_context(nc.sbuf_tensor("rsq", [128, 256], F32))
            ogt = ph3.enter_context(nc.sbuf_tensor("ogt", [128, 2, 256], F32))
            ps_kq = [ph3.enter_context(nc.psum_tensor(f"ps_kq{i}", [128, 512], F32)) for i in range(2)]
            ps_o = [ph3.enter_context(nc.psum_tensor(f"ps_o{i}", [128, 256], F32)) for i in range(2)]
            ps_kv = ph3.enter_context(nc.psum_tensor("ps_kv", [128, 128], F32))
            ps_kt = [ph3.enter_context(nc.psum_tensor(f"ps_kt{i}", [128, 256], BF16)) for i in range(2)]
            ps_rms = ph3.enter_context(nc.psum_tensor("ps_rms", [128, 256], F32))
            for n in ("ps_kq0", "ps_kq1", "ps_o0", "ps_o1", "ps_kv", "ps_kt0", "ps_kt1", "ps_rms"):
                s.buf(n, xread=True)

            for h in range(16):
                s.op("V", (lambda h: lambda v: v.tensor_scalar(
                        tmpd[:, 0, :], arrq[:, :], cpk[:, 2 + h:3 + h], None, op0=ALU.mult))(h),
                     reads=["arrq", "cpk"], writes=["tmpd"], tag=f"qd0_{h}")
                s.op("S", (lambda h: lambda sc: sc.activation(
                        qdec[:, h, :], tmpd[:, 0, :], AF.Exp, scale=-1.0))(h),
                     reads=["tmpd"], writes=["qdec"], tag=f"qd1_{h}")

                def fm(h):
                    def g(v):
                        v.tensor_scalar(tmpd[:, 0, :], cmask[:, 0, :], cpk[:, 2 + h:3 + h], None, op0=ALU.mult)
                        return v.tensor_scalar(tmpd[:, 1, :], cmask[:, 1, :], cpk[:, 2 + h:3 + h], None, op0=ALU.mult)
                    return g
                s.op("V", fm(h), reads=["cmask", "cpk"], writes=["tmpd"], tag=f"mk0_{h}")

                def fe(h):
                    def g(sc):
                        sc.activation(mask[:, h, 0, :], tmpd[:, 0, :], AF.Exp, scale=-1.0)
                        return sc.activation(mask[:, h, 1, :], tmpd[:, 1, :], AF.Exp, scale=-1.0)
                    return g
                s.op("S", fe(h), reads=["tmpd"], writes=["mask"], tag=f"mk1_{h}")
                s.op("V", (lambda h: lambda v: v.tensor_scalar(
                        tmpd[:, 0, 0:2], cpk[:, 0:2], cpk[:, 2 + h:3 + h], None, op0=ALU.mult))(h),
                     reads=["cpk"], writes=["tmpd"], tag=f"kd0_{h}")
                s.op("S", (lambda h: lambda sc: sc.activation(
                        kdec[:, h, :], tmpd[:, 0, 0:2], AF.Exp, scale=-1.0))(h),
                     reads=["tmpd"], writes=["kdec"], tag=f"kd1_{h}")
            s.op("S", lambda sc: sc.activation(bdc[:, :], cpk[:, 2:18], AF.Exp, scale=-256.0),
                 reads=["cpk"], writes=["bdc"], tag="bd")
            s.op("V", lambda v: v.memset(kv[:, :, :], 0.0),
                 writes=[f"kv_h{h}" for h in range(16)], tag="kv0")

            def load_blk(b, par, pre):
                ksrc, kn = (kp_sc, "kp_sc") if pre else (k_sc, "k_sc")
                vsrc, vn = (vp_sc, "vp_sc") if pre else (v_sc, "v_sc")
                if not pre:
                    s.op("Q", (lambda b, par: lambda g: g.dma_start(
                            out=qb[par][:, :, :],
                            in_=q_sc.ap()[:, :, b * 256:(b + 1) * 256].rearrange("h p t -> p h t")))(b, par),
                         reads=["q_sc"], writes=[f"qb{par}"], tag=f"lq{b}")
                    s.op("Q", (lambda b: lambda g: g.dma_start(
                            out=ub[0][:, :, :],
                            in_=u_sc.ap()[:, :, b * 256:(b + 1) * 256].rearrange("h p t -> p h t")))(b),
                         reads=["u_sc"], writes=["ub0"], tag=f"lu{b}")
                s.op("Q", (lambda b, par, ksrc: lambda g: g.dma_start(
                        out=kb[par][:, :, :],
                        in_=ksrc.ap()[:, :, b * 256:(b + 1) * 256].rearrange("h p t -> p h t")))(b, par, ksrc),
                     reads=[kn], writes=[f"kb{par}"], tag=f"lk{pre}{b}")
                for vcc in range(4):
                    s.op("Q", (lambda b, par, vcc, vsrc: lambda g: g.dma_start(
                            out=vb[par][:, :, vcc * 512:(vcc + 1) * 512],
                            in_=vsrc.ap()[vcc, 2 * b:2 * b + 2, :, :].rearrange("c p n -> p c n")))(b, par, vcc, vsrc),
                         reads=[vn], writes=[f"vb{par}"], tag=f"lv{pre}{b}_{vcc}")

            def ktrans_blk(b, par, pre):
                kn = "kp_sc" if pre else "k_sc"
                for h in range(16):
                    pk = ps_kt[(2 * b + h) % 2]
                    pkn = f"ps_kt{(2 * b + h) % 2}"

                    def ftr(h, par, pk):
                        def f(t):
                            t.transpose(pk[:, 0:128], kb[par][:, h, 0:128], eyebf[:, :])
                            return t.transpose(pk[:, 128:256], kb[par][:, h, 128:256], eyebf[:, :])
                        return f
                    s.op("T", ftr(h, par, pk), reads=[f"kb{par}", "eyebf"], writes=[pkn],
                         tag=f"kt{pre}{b}_{h}")

                    def fkd(h, par, pk):
                        def f(sc):
                            sc.activation(ktda[par][:, 0, h, :], pk[:, 0:128], AF.Identity,
                                          scale=kdec[:, h, 0:1])
                            return sc.activation(ktda[par][:, 1, h, :], pk[:, 128:256], AF.Identity,
                                                 scale=kdec[:, h, 1:2])
                        return f
                    s.op("S", fkd(h, par, pk), reads=[pkn, "kdec"], writes=[f"ktda{par}_h{h}"],
                         tag=f"kd{pre}{b}_{h}")

            def km_op(h, b, par, pre, pk, pkn):
                def fkv(h, par, pk):
                    def f(t):
                        t.matmul(pk[:, :], ktda[par][:, 0, h, :], vb[par][:, 0, h * 128:(h + 1) * 128],
                                 start=True, stop=False)
                        return t.matmul(pk[:, :], ktda[par][:, 1, h, :], vb[par][:, 1, h * 128:(h + 1) * 128],
                                        start=False, stop=True)
                    return f
                s.op("T", fkv(h, par, pk), reads=[f"ktda{par}_h{h}", f"vb{par}"], writes=[pkn],
                     tag=f"km{pre}{b}_{h}")

            def ku_op(h, b, pre, pk, pkn):
                s.op("V", (lambda h, pk: lambda v: v.scalar_tensor_tensor(
                        kv[:, h, :], kv[:, h, :], bdc[:, h:h + 1], pk[:, :],
                        op0=ALU.mult, op1=ALU.add))(h, pk),
                     reads=[pkn, "bdc", f"kv_h{h}"], writes=[f"kv_h{h}"], tag=f"ku{pre}{b}_{h}")

            pre_pk = lambda h: (ps_kq[h % 2][:, 0:128], f"ps_kq{h % 2}")
            for b in range(NB):
                load_blk(b, b % 2, True)
                ktrans_blk(b, b % 2, True)
                for h in range(16):
                    if h >= 1:
                        ku_op(h - 1, b, True, *pre_pk(h - 1))
                    km_op(h, b, b % 2, True, *pre_pk(h))
                ku_op(15, b, True, *pre_pk(15))
            s.op("V", lambda v: v.tensor_scalar(kv[:, :, :], kv[:, :, :], cpk[:, 18:19], None, op0=ALU.mult),
                 reads=["cpk"] + [f"kv_h{h}" for h in range(16)],
                 writes=[f"kv_h{h}" for h in range(16)], tag="kvflag")

            def kq_mm(h, b, par):
                i = h % 2
                def fkq(h, par, i):
                    def f(t):
                        t.matmul(ps_kq[i][:, 0:256], kb[par][:, h, 0:128], qb[par][:, h, :],
                                 start=True, stop=True)
                        return t.matmul(ps_kq[i][:, 256:512], kb[par][:, h, 128:256], qb[par][:, h, :],
                                        start=True, stop=True)
                    return f
                s.op("T", fkq(h, par, i), reads=[f"kb{par}", f"qb{par}"],
                     writes=[f"ps_kq{i}"], tag=f"kq{b}_{h}")

            def mkq_qdq(h, b, par):
                i = h % 2
                def fmk(h, i):
                    def f(v):
                        v.tensor_tensor(kqm[i][:, 0, :], ps_kq[i][:, 0:256], mask[:, h, 0, :], op=ALU.mult)
                        return v.tensor_tensor(kqm[i][:, 1, :], ps_kq[i][:, 256:512], mask[:, h, 1, :], op=ALU.mult)
                    return f
                s.op("V", fmk(h, i), reads=[f"ps_kq{i}", "mask"], writes=[f"kqm{i}"], tag=f"mq{b}_{h}")
                s.op("P", (lambda h, par, i: lambda g: g.tensor_tensor(
                        qdq[i][:, :], qb[par][:, h, :], qdec[:, h, :], op=ALU.mult))(h, par, i),
                     reads=[f"qb{par}", "qdec"], writes=[f"qdq{i}"], tag=f"qq{b}_{h}")

            def kc_op(h, b):
                i = h % 2
                s.op("S", (lambda h, i: lambda sc: sc.activation(
                        kvb[i][:, :], kv[:, h, :], AF.Identity))(h, i),
                     reads=[f"kv_h{h}"], writes=[f"kvb{i}"], tag=f"kc{b}_{h}")

            def o_mm(h, b, par):
                i = h % 2
                def fo(h, par, i):
                    def f(t):
                        t.matmul(ps_o[i][:, 0:256], kvb[i][:, :], qdq[i][:, :], start=True, stop=False)
                        t.matmul(ps_o[i][:, 0:256], vb[par][:, 0, h * 128:(h + 1) * 128],
                                 kqm[i][:, 0, :], start=False, stop=False)
                        return t.matmul(ps_o[i][:, 0:256], vb[par][:, 1, h * 128:(h + 1) * 128],
                                        kqm[i][:, 1, :], start=False, stop=True)
                    return f
                s.op("T", fo(h, par, i), reads=[f"kvb{i}", f"qdq{i}", f"vb{par}", f"kqm{i}"],
                     writes=[f"ps_o{i}"], tag=f"om{b}_{h}")

            def o_evac(h, b):
                i = h % 2
                s.op("S", (lambda h, i: lambda sc: sc.activation(obf[:, h, :], ps_o[i][:, 0:256],
                                                                 AF.Identity))(h, i),
                     reads=[f"ps_o{i}"], writes=[f"obf_h{h}"], tag=f"oe{b}_{h}")
                s.op("S", (lambda h, i: lambda sc: sc.activation(osq[:, h, :], ps_o[i][:, 0:256],
                                                                 AF.Square))(h, i),
                     reads=[f"ps_o{i}"], writes=[f"osq_h{h}"], tag=f"os{b}_{h}")

            for b in range(NB):
                par = b % 2
                load_blk(b, par, False)
                ktrans_blk(b, par, False)
                kq_mm(0, b, par)
                kc_op(0, b)
                for h in range(16):
                    mkq_qdq(h, b, par)
                    if h + 1 < 16:
                        kq_mm(h + 1, b, par)
                        kc_op(h + 1, b)
                    o_mm(h, b, par)
                    o_evac(h, b)
                    if h >= 1:
                        ku_op(h - 1, b, False, ps_kv, "ps_kv")
                    km_op(h, b, par, False, ps_kv, "ps_kv")
                ku_op(15, b, False, ps_kv, "ps_kv")

                def frms(t):
                    r = None
                    for ct in range(16):
                        r = t.matmul(ps_rms[:, :], onesbf[:, :], osq[:, ct, :],
                                     start=(ct == 0), stop=(ct == 15))
                    return r
                s.op("T", frms, reads=["onesbf"] + [f"osq_h{h}" for h in range(16)],
                     writes=["ps_rms"], tag=f"rm{b}")
                s.op("S", lambda sc: sc.activation(rsq[:, :], ps_rms[:, :], AF.Sqrt,
                                                   scale=1.0 / D, bias=cpk[:, 20:21]),
                     reads=["ps_rms", "cpk"], writes=["rsq"], tag=f"rs{b}")
                s.op("V", lambda v: v.reciprocal(rsq[:, :], rsq[:, :]),
                     reads=["rsq"], writes=["rsq"], tag=f"rr{b}")
                for ct in range(16):
                    s.op("V", (lambda ct, b: lambda v: v.tensor_tensor(
                            ogt[:, ct % 2, :], obf[:, ct, :], rsq[:, :], op=ALU.mult))(ct, b),
                         reads=[f"obf_h{ct}", "rsq"], writes=["ogt"], tag=f"ogA{b}_{ct}")
                    s.op("V", (lambda ct, b: lambda v: v.tensor_tensor(
                            og[:, ct, b * 256:(b + 1) * 256], ogt[:, ct % 2, :],
                            ub[0][:, ct, :], op=ALU.mult))(ct, b),
                         reads=["ogt", "ub0"], writes=["og"], tag=f"ogB{b}_{ct}")
        s.emit_phase(nc, ctx)
        ph3.close()
        s.note_free()

        ph4 = ExitStack()
        if phases >= 4:
            wo = ph4.enter_context(nc.sbuf_tensor("wo", [128, 16, D], BF16))
            x2s = [ph4.enter_context(nc.sbuf_tensor(f"x2s{i}", [128, D], F32)) for i in range(2)]
            ps4 = [ph4.enter_context(nc.psum_tensor(f"ps4_{i}", [128, 512], F32)) for i in range(4)]
            for i in range(4):
                s.buf(f"ps4_{i}", xread=True)
            for wi_ in range(4):
                s.op("G", (lambda wi_: lambda g: pay_dma(
                        g, wo[:, :, wi_ * 512:(wi_ + 1) * 512], GRAN_OFF[("wout", wi_)],
                        [[16 * 512, 128], [512, 16], [1, 512]]))(wi_),
                     reads=["pay"], writes=["wo"], tag=f"ldwo{wi_}")
            wdm = ph4.enter_context(nc.sbuf_tensor("wdm", [128, 512], F32))
            wdm2 = ph4.enter_context(nc.sbuf_tensor("wdm2", [128, 512], F32))
            wdi = [ph4.enter_context(nc.sbuf_tensor(f"wdi{i}", [128, 11, 512], F32)) for i in range(2)]
            ps4w = ph4.enter_context(nc.psum_tensor("ps4w", [128, 512], F32))
            s.buf("ps4w", xread=True)

            def wd_piece(idx):
                fc, g4 = idx // 4, idx % 4
                wp, wpn = wdi[idx % 2], f"wdi{idx % 2}"
                off = GRAN_OFF[("wd", fc * 4 + g4)]
                s.op("G", (lambda off, wp: lambda g: pay_dma(
                        g, wp[:, :, :], off, [[512, 128], [128 * 512, 11], [1, 512]]))(off, wp),
                     reads=["pay"], writes=[wpn], tag=f"wd{fc}_{g4}")

                for j in range(11):
                    if g4 == 0 and j == 0:
                        s.op("V", (lambda wp: lambda v: v.scalar_tensor_tensor(
                                wdm[:, :], wp[:, 0, :], -1.0, wp[:, 0, :],
                                op0=ALU.mult, op1=ALU.max))(wp),
                             reads=[wpn], writes=["wdm"], tag=f"wm{fc}_{g4}_0")
                    else:
                        s.op("V", (lambda j, wp: lambda v: v.scalar_tensor_tensor(
                                wdm2[:, :], wp[:, j, :], -1.0, wp[:, j, :],
                                op0=ALU.mult, op1=ALU.max))(j, wp),
                             reads=[wpn], writes=["wdm2"], tag=f"wmA{fc}_{g4}_{j}")
                        s.op("V", lambda v: v.tensor_tensor(
                                wdm[:, :], wdm[:, :], wdm2[:, :], op=ALU.max),
                             reads=["wdm", "wdm2"], writes=["wdm"], tag=f"wmB{fc}_{g4}_{j}")
                if g4 != 3:
                    return

                def ftr5(t):
                    r = None
                    for j in range(4):
                        r = t.transpose(ps4w[:, j * 128:(j + 1) * 128],
                                        wdm[:, j * 128:(j + 1) * 128], eye32[:, :])
                    return r
                s.op("T", ftr5, reads=["wdm", "eye32"], writes=["ps4w"], tag=f"wt{fc}")

                def fred(v):
                    r = None
                    for j in range(4):
                        r = v.reduce_max(wdm[:, j:j + 1], ps4w[:, j * 128:(j + 1) * 128], axis=AX)
                    return r
                s.op("V", fred, reads=["ps4w"], writes=["wdm"], tag=f"wr{fc}")
                s.op("V", lambda v: v.tensor_scalar(wdm[:, 0:4], wdm[:, 0:4], 1.0 / 448.0, None,
                                                    op0=ALU.mult),
                     reads=["wdm"], writes=["wdm"], tag=f"ws{fc}")
                for jj in range(4):
                    s.op("T", (lambda jj: lambda t: t.transpose(
                            ps4w[0:1, 256:384], wdm[:, jj:jj + 1], eye32[:, :]))(jj),
                         reads=["wdm", "eye32"], writes=["ps4w"], tag=f"wfT{fc}_{jj}")
                    s.op("V", (lambda fc, jj: lambda v: v.tensor_copy(
                            wsd_row[0:1, fc * 512 + jj * 128:fc * 512 + (jj + 1) * 128],
                            ps4w[0:1, 256:384]))(fc, jj),
                         reads=["ps4w"], writes=["wsd_row"], tag=f"wfC{fc}_{jj}")

            for tc in range(16):
                i2 = tc % 2
                s.op("Q", (lambda tc, i2: lambda g: g.dma_start(
                        out=x2s[i2][:, :], in_=xin_d.ap()[tc * 128:(tc + 1) * 128, :]))(tc, i2),
                     reads=["xin_d"], writes=[f"x2s{i2}"], tag=f"lr{tc}")
                wd_piece(tc)
                for fc in range(4):
                    pi = fc

                    def fmm(tc, fc, pi):
                        def f(t):
                            r = None
                            for cc in range(16):
                                r = t.matmul(ps4[pi][:, :], og[:, cc, tc * 128:(tc + 1) * 128],
                                             wo[:, cc, fc * 512:(fc + 1) * 512],
                                             start=(cc == 0), stop=(cc == 15))
                            return r
                        return f
                    s.op("T", fmm(tc, fc, pi), reads=["og", "wo"], writes=[f"ps4_{pi}"],
                         tag=f"o2_{tc}_{fc}")
                    s.op("V", (lambda tc, fc, pi, i2: lambda v: v.tensor_tensor(
                            x2s[i2][:, fc * 512:(fc + 1) * 512], ps4[pi][:, :],
                            x2s[i2][:, fc * 512:(fc + 1) * 512], op=ALU.add))(tc, fc, pi, i2),
                         reads=[f"ps4_{pi}", f"x2s{i2}"], writes=[f"x2s{i2}"], tag=f"x2_{tc}_{fc}")
                s.op("Q", (lambda tc, i2: lambda g: g.dma_start(
                        out=x2_sc[tc], in_=x2s[i2][:, :]))(tc, i2),
                     reads=[f"x2s{i2}"], writes=["x2_sc"], tag=f"sx2{tc}")
        s.emit_phase(nc, ctx)
        ph4.close()
        sc34.close()
        s.note_free()

        qt = sc5bc.enter_context(nc.sbuf_tensor("qt", [128, 44, T], FP8))
        sc5 = ExitStack()
        qx = sc5.enter_context(nc.sbuf_tensor("qx", [128, 16, T], FP8))
        ph5a = ExitStack()
        if phases >= 5:
            x2i = [ph5a.enter_context(nc.sbuf_tensor(f"x2i{i}", [128, D], F32)) for i in range(3)]
            xl2 = [ph5a.enter_context(nc.sbuf_tensor(f"xl2_{i}", [128, D], F32)) for i in range(3)]
            q8t = [ph5a.enter_context(nc.sbuf_tensor(f"q8t_{i}", [128, D], FP8)) for i in range(3)]
            st5 = [ph5a.enter_context(nc.sbuf_tensor(f"st5_{i}", [128, 8], F32)) for i in range(3)]
            ps5 = ph5a.enter_context(nc.psum_tensor("ps5", [128, 512], F32))
            ps5q = [ph5a.enter_context(nc.psum_tensor(f"ps5q_{i}", [128, 256], FP8)) for i in range(2)]
            s.buf("ps5", xread=True)
            s.buf("ps5q_0", xread=True)
            s.buf("ps5q_1", xread=True)

            def p5a_head(tc):
                i2 = tc % 3
                xi, xn = x2i[i2], f"x2i{i2}"
                sv, svn = st5[i2], f"st5_{i2}"
                xlv, xlvn = xl2[i2], f"xl2_{i2}"
                qv, qvn = q8t[i2], f"q8t_{i2}"
                s.op("Q", (lambda tc, xi: lambda g: g.dma_start(
                        out=xi[:, :], in_=x2_sc[tc]))(tc, xi),
                     reads=["x2_sc"], writes=[xn], tag=f"l2{tc}")
                s.op("V", (lambda xi, sv: lambda v: v.reduce_sum(sv[:, 0:1], xi[:, :], axis=AX))(xi, sv),
                     reads=[xn], writes=[svn], tag=f"a{tc}")
                s.op("S", (lambda xi, sv, xlv: lambda sc: sc.activation(
                        xlv[:, :], xi[:, :], AF.Square, accum_out=sv[:, 1:2]))(xi, sv, xlv),
                     reads=[xn], writes=[xlvn, svn], tag=f"b{tc}")

                def fst(sv):
                    def f(v):
                        v.tensor_scalar(sv[:, 2:3], sv[:, 0:1], 1.0 / D, None, op0=ALU.mult)
                        return v.tensor_scalar(sv[:, 3:4], sv[:, 1:2], 1.0 / D, None, op0=ALU.mult)
                    return f
                s.op("V", fst(sv), reads=[svn], writes=[svn], tag=f"c{tc}")
                s.op("V", (lambda sv: lambda v: v.tensor_tensor(
                        sv[:, 4:5], sv[:, 2:3], sv[:, 2:3], op=ALU.mult))(sv),
                     reads=[svn], writes=[svn], tag=f"c2{tc}")
                s.op("V", (lambda sv: lambda v: v.tensor_tensor(
                        sv[:, 3:4], sv[:, 3:4], sv[:, 4:5], op=ALU.subtract))(sv),
                     reads=[svn], writes=[svn], tag=f"c3{tc}")
                s.op("S", (lambda sv: lambda sc: sc.activation(
                        sv[:, 5:6], sv[:, 3:4], AF.Sqrt, bias=cpk[:, 19:20]))(sv),
                     reads=[svn, "cpk"], writes=[svn], tag=f"d{tc}")
                s.op("V", (lambda sv: lambda v: v.reciprocal(sv[:, 6:7], sv[:, 5:6]))(sv),
                     reads=[svn], writes=[svn], tag=f"e{tc}")
                s.op("V", (lambda xi, sv, xlv: lambda v: v.tensor_scalar(
                        xlv[:, :], xi[:, :], sv[:, 2:3], sv[:, 6:7],
                        op0=ALU.subtract, op1=ALU.mult))(xi, sv, xlv),
                     reads=[xn, svn], writes=[xlvn], tag=f"f{tc}")
                s.op("V", (lambda sv, xlv: lambda v: v.reduce_max(
                        sv[:, 0:1], xlv[:, :], axis=AX, apply_absolute_value=True))(sv, xlv),
                     reads=[xlvn], writes=[svn], tag=f"g{tc}")

                s.op("V", (lambda sv: lambda v: v.reciprocal(sv[:, 1:2], sv[:, 0:1]))(sv),
                     reads=[svn], writes=[svn], tag=f"h{tc}")
                s.op("V", (lambda sv: lambda v: v.tensor_scalar(
                        sv[:, 1:2], sv[:, 1:2], 224.0, None, op0=ALU.mult))(sv),
                     reads=[svn], writes=[svn], tag=f"h2{tc}")
                s.op("S", (lambda sv, xlv, qv: lambda sc: sc.activation(
                        qv[:, :], xlv[:, :], AF.Identity, scale=sv[:, 1:2]))(sv, xlv, qv),
                     reads=[xlvn, svn], writes=[qvn], tag=f"i{tc}")
                s.op("V", (lambda tc, sv: lambda v: v.tensor_copy(mxs[:, tc:tc + 1], sv[:, 0:1]))(tc, sv),
                     reads=[svn], writes=["mxs"], tag=f"j{tc}")

            def p5a_tail(tc):
                i2 = tc % 3
                qv, qvn = q8t[i2], f"q8t_{i2}"
                for a in range(16):
                    ai = a % 2
                    s.op("T", (lambda a, qv, ai: lambda t: t.transpose(
                            s2ap(ps5q[ai]), qv[:, a * 128:(a + 1) * 128], eye8[:, :]))(a, qv, ai),
                         reads=[qvn, "eye8"], writes=[f"ps5q_{ai}"], tag=f"k{tc}_{a}")
                    if ai == 0:
                        s.op("S", (lambda a, tc, ai: lambda sc: sc.activation(
                                qx[:, a, tc * 128:(tc + 1) * 128], s2ap(ps5q[ai]), AF.Identity))(a, tc, ai),
                             reads=[f"ps5q_{ai}"], writes=["qx0"], tag=f"l{tc}_{a}")
                    else:
                        s.op("V", (lambda a, tc, ai: lambda v: v.tensor_copy(
                                qx[:, a, tc * 128:(tc + 1) * 128], s2ap(ps5q[ai])))(a, tc, ai),
                             reads=[f"ps5q_{ai}"], writes=["qx1"], tag=f"l{tc}_{a}")

            for tc in range(16):
                p5a_head(tc)
                if tc >= 1:
                    p5a_tail(tc - 1)
            p5a_tail(15)
        s.emit_phase(nc, ctx)
        ph5a.close()
        s.note_free()

        ph5b = ExitStack()
        if phases >= 6:
            wmac = [ph5b.enter_context(nc.sbuf_tensor(f"wmac{i}", [128, 128], F32)) for i in range(2)]
            wabs = ph5b.enter_context(nc.sbuf_tensor("wabs", [128, 8, 128], F32))
            wred = ph5b.enter_context(nc.sbuf_tensor("wred", [128, 2], F32))
            wrow = ph5b.enter_context(nc.sbuf_tensor("wrow", [1, 512], F32))
            wibc = ph5b.enter_context(nc.sbuf_tensor("wibc", [128, 2, 512], F32))
            wsbc = ph5b.enter_context(nc.sbuf_tensor("wsbc", [128, 4, 512], F32))
            gtmp = [ph5b.enter_context(nc.sbuf_tensor(f"gtmp{i}", [128, 512], F32)) for i in range(2)]
            tstg = [ph5b.enter_context(nc.sbuf_tensor(f"tstg{i}", [128, 512], F32)) for i in range(2)]
            tq8 = [ph5b.enter_context(nc.sbuf_tensor(f"tq8_{i}", [128, 512], FP8))
                   for i in range(2)]
            mtmp = ph5b.enter_context(nc.sbuf_tensor("mtmp", [128, 2], F32))
            gup = ExitStack()
            wgi = gup.enter_context(nc.sbuf_tensor("wgi", [128, 16, 128], F32))
            wui = gup.enter_context(nc.sbuf_tensor("wui", [128, 16, 128], F32))
            qwg = [gup.enter_context(nc.sbuf_tensor(f"qwg{i}", [128, 16, 512], FP8)) for i in range(2)]
            qwu = [gup.enter_context(nc.sbuf_tensor(f"qwu{i}", [128, 16, 512], FP8)) for i in range(2)]
            wrowt = gup.enter_context(nc.sbuf_tensor("wrowt", [1, 256], F32))
            psg = [gup.enter_context(nc.psum_tensor(f"psg{i}", [128, 512], F32)) for i in range(3)]
            psu = [gup.enter_context(nc.psum_tensor(f"psu{i}", [128, 512], F32)) for i in range(3)]
            psb = gup.enter_context(nc.psum_tensor("psb", [128, 512], F32))
            psw = gup.enter_context(nc.psum_tensor("psw", [128, 512], F32))
            for n in ("psg0", "psg1", "psg2", "psu0", "psu1", "psu2", "psb", "psw"):
                s.buf(n, xread=True)

            s.op("V", lambda v: v.memset(m2s[:, :], 0.0), writes=["m2s"], tag="m2z")
            s.op("V", lambda v: v.tensor_scalar(rxs[:, :], mxs[:, :], 1.0 / 112.0, None, op0=ALU.mult),
                 reads=["mxs"], writes=["rxs"], tag="rxs")

            def wprep_sub(icc, sub, mat, wi, win, qw, qwn, slot, off, stage, wm, wmn):
                ib = icc % 2
                """Load one [2048f x 128ic] fp32 sub-chunk, compute its ws,
                bcast scale slices, quantize into qw[:, :, sub*128...]."""
                if stage == 2:
                    pass
                else:
                    s.op("G", (lambda off, wi: lambda g: pay_dma(
                            g, wi[:, :, :], off, [[16 * 512, 128], [512, 16], [1, 128]]))(off, wi),
                         reads=["pay"], writes=[win], tag=f"lw{mat}{icc}_{sub}")

                for rnd in ([] if stage == 2 else range(2)):
                    s.op("S", (lambda wi, rnd: lambda sc: sc.activation(
                            wabs[:, :, :], wi[:, 8 * rnd:8 * rnd + 8, :], AF.Abs))(wi, rnd),
                         reads=[win, wmn] if rnd else [win], writes=["wabs"],
                         tag=f"wx0{mat}{icc}_{sub}_{rnd}")
                    for lvl, st in ((1, 1), (2, 2)):
                        def ftree(st):
                            def f(v):
                                r = None
                                for a in range(0, 8, 2 * st):
                                    r = v.tensor_tensor(wabs[:, a, :], wabs[:, a, :],
                                                        wabs[:, a + st, :], op=ALU.max)
                                return r
                            return f
                        s.op("V", ftree(st), reads=["wabs"], writes=["wabs"],
                             tag=f"wx{lvl}{mat}{icc}_{sub}_{rnd}")
                    if rnd == 0:
                        s.op("V", (lambda wm: lambda v: v.tensor_tensor(wm[:, :], wabs[:, 0, :],
                                                                        wabs[:, 4, :], op=ALU.max))(wm),
                             reads=["wabs"], writes=[wmn], tag=f"wx3{mat}{icc}_{sub}_0")
                    else:
                        def flast(v):
                            v.tensor_tensor(wabs[:, 0, :], wabs[:, 0, :], wabs[:, 4, :], op=ALU.max)
                            return None
                        s.op("V", lambda v: v.tensor_tensor(wabs[:, 0, :], wabs[:, 0, :],
                                                            wabs[:, 4, :], op=ALU.max),
                             reads=["wabs"], writes=["wabs"], tag=f"wx3{mat}{icc}_{sub}_1")
                        s.op("V", (lambda wm: lambda v: v.tensor_tensor(wm[:, :], wm[:, :],
                                                                          wabs[:, 0, :], op=ALU.max))(wm),
                             reads=["wabs", wmn], writes=[wmn], tag=f"wx4{mat}{icc}_{sub}")
                if stage == 1:
                    return
                s.op("T", (lambda wm: lambda t: t.transpose(psw[:, 0:128], wm[:, :], eye32[:, :]))(wm),
                     reads=[wmn, "eye32"], writes=["psw"], tag=f"wt{mat}{icc}_{sub}")

                s.op("V", lambda v: v.reduce_max(wred[:, 0:1], psw[:, 0:128], axis=AX),
                     reads=["psw"], writes=["wred"], tag=f"wr{mat}{icc}_{sub}")
                s.op("V", lambda v: v.tensor_scalar(wred[:, 1:2], wred[:, 0:1], 1.0 / 448.0, None,
                                                    op0=ALU.mult),
                     reads=["wred"], writes=["wred"], tag=f"wr2{mat}{icc}_{sub}")
                s.op("T", lambda t: t.transpose(psw[0:1, 128:256], wred[:, 1:2], eye32[:, :]),
                     reads=["wred", "eye32"], writes=["psw"], tag=f"wfT{mat}{icc}_{sub}")
                s.op("V", (lambda sub: lambda v: v.tensor_copy(
                        wrow[0:1, sub * 128:(sub + 1) * 128], psw[0:1, 128:256]))(sub),
                     reads=["psw"], writes=["wrow"], tag=f"wfC{mat}{icc}_{sub}")

                s.op("V", (lambda sub: lambda v: v.reciprocal(
                        wrowt[0:1, 0:128], wrow[0:1, sub * 128:(sub + 1) * 128]))(sub),
                     reads=["wrow"], writes=["wrowt"], tag=f"wi{mat}{icc}_{sub}")
                s.op("V", lambda v: v.tensor_scalar(wrowt[0:1, 128:256], wrowt[0:1, 0:128], 0.5, None,
                                                    op0=ALU.mult),
                     reads=["wrowt"], writes=["wrowt"], tag=f"wi2{mat}{icc}_{sub}")

                def fbc(t):
                    t.matmul(psb[:, 0:128], ones32[0:1, 0:128], wrowt[0:1, 128:256], start=True, stop=True)
                    return t.matmul(psb[:, 128:256], ones32[0:1, 0:128],
                                    wrow[0:1, sub * 128:(sub + 1) * 128], start=True, stop=True)
                s.op("T", fbc, reads=["wrowt", "wrow", "ones32"], writes=["psb"], tag=f"wb{mat}{icc}_{sub}")

                sl = slot * 2 + ib

                def fev(v):
                    v.tensor_copy(wibc[:, slot, sub * 128:(sub + 1) * 128], psb[:, 0:128])
                    return v.tensor_copy(wsbc[:, sl, sub * 128:(sub + 1) * 128], psb[:, 128:256])
                s.op("V", fev, reads=["psb"], writes=["wibc", "wsbc"], tag=f"we{mat}{icc}_{sub}")

                def fq(wi, qw, sub, slot):
                    def f(v):
                        r = None
                        for a in range(16):
                            r = v.tensor_tensor(qw[:, a, sub * 128:(sub + 1) * 128], wi[:, a, :],
                                                wibc[:, slot, sub * 128:(sub + 1) * 128], op=ALU.mult)
                        return r
                    return f
                s.op("P", fq(wi, qw, sub, slot), reads=[win, "wibc"], writes=[qwn],
                     tag=f"wq{mat}{icc}_{sub}")

            def wprep_piece(icc2, j, stage=0):
                ib2 = icc2 % 2
                sub, mat = j // 2, ("g", "u")[j % 2]
                wm, wmn = wmac[j % 2], f"wmac{j % 2}"
                if mat == "g":
                    off = GRAN_OFF[("wg", icc2)] + sub * 128
                    wprep_sub(icc2, sub, "g", wgi, "wgi", qwg[ib2], f"qwg{ib2}", 0, off, stage, wm, wmn)
                else:
                    off = GRAN_OFF[("wu", icc2)] + sub * 128
                    wprep_sub(icc2, sub, "u", wui, "wui", qwu[ib2], f"qwu{ib2}", 1, off, stage, wm, wmn)

            for j in range(8):
                wprep_piece(0, j)
            for icc in range(11):
                ib = icc % 2
                def tail(tc):
                    pt = tc % 3
                    i2 = tc % 2
                    s.op("V", (lambda ib, i2: lambda v: v.tensor_tensor(
                            gtmp[i2][:, :], gtmp[i2][:, :], wsbc[:, 2 + ib, :], op=ALU.mult))(ib, i2),
                         reads=[f"gtmp{i2}", "wsbc"], writes=[f"gtmp{i2}"], tag=f"ue{icc}_{tc}")
                    s.op("V", (lambda tc, pt, i2: lambda v: v.scalar_tensor_tensor(
                            tstg[i2][:, :], psu[pt][:, :], rxs[:, tc:tc + 1], gtmp[i2][:, :],
                            op0=ALU.mult, op1=ALU.mult))(tc, pt, i2),
                         reads=[f"psu{pt}", "rxs", f"gtmp{i2}"], writes=[f"tstg{i2}"],
                         tag=f"tm{icc}_{tc}")
                    s.op("V", (lambda tc, i2: lambda v: v.reduce_max(
                            mtmp[:, i2:i2 + 1], tstg[i2][:, :], axis=AX,
                            apply_absolute_value=True))(tc, i2),
                         reads=[f"tstg{i2}"], writes=["mtmp"], tag=f"mxa{icc}_{tc}")
                    s.op("V", (lambda tc, i2: lambda v: v.tensor_tensor(
                            m2s[:, tc:tc + 1], m2s[:, tc:tc + 1], mtmp[:, i2:i2 + 1], op=ALU.max))(tc, i2),
                         reads=["mtmp", "m2s"], writes=["m2s"], tag=f"mxb{icc}_{tc}")
                    s.op("Q", (lambda icc, tc, i2: lambda g: g.dma_start(
                            out=t_sc.ap()[icc, :, tc, :], in_=tstg[i2][:, :]))(icc, tc, i2),
                         reads=[f"tstg{i2}"], writes=["t_sc"], tag=f"ts{icc}_{tc}")

                for tc in range(16):
                    pt = tc % 3
                    i2 = tc % 2
                    def fg(tc, ib, pt):
                        def f(t):
                            r = None
                            for a in range(8):
                                r = t.matmul(psg[pt][:, :], qx[:, 2 * a:2 * a + 2, tc * 128:(tc + 1) * 128],
                                             qwg[ib][:, 2 * a:2 * a + 2, :], start=(a == 0), stop=(a == 7),
                                             perf_mode=PM.DoubleRow)
                            return r
                        return f
                    s.op("T", fg(tc, ib, pt), reads=["qx0", "qx1", f"qwg{ib}"], writes=[f"psg{pt}"],
                         tag=f"gm{icc}_{tc}")

                    def fu(tc, ib, pt):
                        def f(t):
                            r = None
                            for a in range(8):
                                r = t.matmul(psu[pt][:, :], qx[:, 2 * a:2 * a + 2, tc * 128:(tc + 1) * 128],
                                             qwu[ib][:, 2 * a:2 * a + 2, :], start=(a == 0), stop=(a == 7),
                                             perf_mode=PM.DoubleRow)
                            return r
                        return f
                    s.op("T", fu(tc, ib, pt), reads=["qx0", "qx1", f"qwu{ib}"], writes=[f"psu{pt}"],
                         tag=f"um{icc}_{tc}")
                    s.op("V", (lambda tc, ib, pt, i2: lambda v: v.scalar_tensor_tensor(
                            gtmp[i2][:, :], psg[pt][:, :], rxs[:, tc:tc + 1], wsbc[:, ib, :],
                            op0=ALU.mult, op1=ALU.mult))(tc, ib, pt, i2),
                         reads=[f"psg{pt}", "rxs", "wsbc"], writes=[f"gtmp{i2}"], tag=f"ge{icc}_{tc}")
                    if tc % 2 == 1:
                        def gsp(sc):
                            sc.activation(gtmp[0][:, :], gtmp[0][:, :], AF.Silu)
                            return sc.activation(gtmp[1][:, :], gtmp[1][:, :], AF.Silu)
                        s.op("S", gsp, reads=["gtmp0", "gtmp1"],
                             writes=["gtmp0", "gtmp1"], tag=f"gs{icc}_{tc}")
                        tail(tc - 1)
                        tail(tc)
                    if icc + 1 < 11 and tc % 2 == 1:
                        j = tc // 2
                        wprep_piece(icc + 1, j, stage=1)
                        if j >= 1:
                            wprep_piece(icc + 1, j - 1, stage=2)
                if icc + 1 < 11:
                    wprep_piece(icc + 1, 7, stage=2)
            s.emit_phase(nc, ctx)
            gup.close()
            s.note_free()
            psw8 = [ph5b.enter_context(nc.psum_tensor(f"psw8_{i}", [128, 256], FP8)) for i in range(8)]
            for i in range(8):
                s.buf(f"psw8_{i}", xread=True)

            s.op("V", lambda v: v.reciprocal(tqs[:, :], m2s[:, :]),
                 reads=["m2s"], writes=["tqs"], tag="m2inv0")
            s.op("V", lambda v: v.tensor_scalar(tqs[:, :], tqs[:, :], 224.0, None, op0=ALU.mult),
                 reads=["tqs"], writes=["tqs"], tag="m2inv1")
            s.op("V", lambda v: v.tensor_scalar(m2f[:, :], m2s[:, :], 1.0 / 112.0, None, op0=ALU.mult),
                 reads=["m2s"], writes=["m2f"], tag="m2fold")

            tri = [ph5b.enter_context(nc.sbuf_tensor(f"tri{i}", [128, 8, 512], F32)) for i in range(2)]
            for icc in range(11):
                for half in range(2):
                    i2 = half
                    ti, tn = tri[i2], f"tri{i2}"
                    s.op("Q", (lambda icc, half, ti: lambda g: g.dma_start(
                            out=ti[:, :, :], in_=t_sc.ap()[icc, :, half * 8:(half + 1) * 8, :]))(icc, half, ti),
                         reads=["t_sc"], writes=[tn], tag=f"tl{icc}_{half}")
                    for tcl in range(8):
                        tc = half * 8 + tcl
                        kt8 = (icc * 16 + tc) % 2
                        bs = 4 * kt8
                        s.op("S", (lambda ti, tcl, tc, kt8: lambda sc: sc.activation(
                                tq8[kt8][:, :], ti[:, tcl, :], AF.Identity,
                                scale=tqs[:, tc:tc + 1]))(ti, tcl, tc, kt8),
                             reads=[tn, "tqs"], writes=[f"tq8_{kt8}"], tag=f"tq{icc}_{tc}")
                        for j in range(4):
                            s.op("T", (lambda j, kt8, bs: lambda t: t.transpose(
                                    s2ap(psw8[bs + j]), tq8[kt8][:, j * 128:(j + 1) * 128],
                                    eye8[:, :]))(j, kt8, bs),
                                 reads=[f"tq8_{kt8}", "eye8"], writes=[f"psw8_{bs + j}"],
                                 tag=f"tt{icc}_{tc}_{j}")

                        def fte(icc, tc, bs):
                            def f(v):
                                r = None
                                for j in range(4):
                                    r = v.tensor_copy(qt[:, 4 * icc + j, tc * 128:(tc + 1) * 128],
                                                      s2ap(psw8[bs + j]))
                                return r
                            return f
                        s.op("V", fte(icc, tc, bs),
                             reads=[f"psw8_{bs + j}" for j in range(4)],
                             writes=["qt"], tag=f"te{icc}_{tc}")
        s.emit_phase(nc, ctx)
        ph5b.close()
        sc5.close()
        s.note_free()

        ph5c = ExitStack()
        if phases >= 7:
            wdi2 = [ph5c.enter_context(nc.sbuf_tensor("wdi2_0", [128, 11, 512], F32))]
            wdq = [ph5c.enter_context(nc.sbuf_tensor(f"wdq{i}", [128, 44, 512], FP8)) for i in range(2)]
            ibcd = [ph5c.enter_context(nc.sbuf_tensor(f"ibcd{i}", [128, 512], F32)) for i in range(2)]
            sbcd = [ph5c.enter_context(nc.sbuf_tensor(f"sbcd{i}", [128, 512], F32)) for i in range(2)]
            rowt = ph5c.enter_context(nc.sbuf_tensor("rowt", [1, 512], F32))
            x2r = [ph5c.enter_context(nc.sbuf_tensor(f"x2r{i}", [128, 512], F32)) for i in range(2)]
            outs = [ph5c.enter_context(nc.sbuf_tensor(f"outs{i}", [128, 512], F32)) for i in range(2)]
            outt = [ph5c.enter_context(nc.sbuf_tensor(f"outt{i}", [128, 512], F32)) for i in range(2)]
            psd = [ph5c.enter_context(nc.psum_tensor(f"psd{i}", [128, 512], F32)) for i in range(4)]
            psb2 = ph5c.enter_context(nc.psum_tensor("psb2", [128, 512], F32))
            for i in range(4):
                s.buf(f"psd{i}", xread=True)
            s.buf("psb2", xread=True)

            for fc in range(4):
                fcp = fc % 2
                s.op("V", (lambda fc: lambda v: v.reciprocal(
                        rowt[0:1, :], wsd_row[0:1, fc * 512:(fc + 1) * 512]))(fc),
                     reads=["wsd_row"], writes=["rowt"], tag=f"div{fc}")
                s.op("V", lambda v: v.tensor_scalar(rowt[0:1, :], rowt[0:1, :], 0.5, None, op0=ALU.mult),
                     reads=["rowt"], writes=["rowt"], tag=f"div2{fc}")
                s.op("T", lambda t: t.matmul(psb2[:, :], ones32[0:1, 0:128], rowt[0:1, :],
                                             start=True, stop=True),
                     reads=["rowt", "ones32"], writes=["psb2"], tag=f"dbc{fc}")
                s.op("V", (lambda fcp: lambda v: v.tensor_copy(ibcd[fcp][:, :], psb2[:, :]))(fcp),
                     reads=["psb2"], writes=[f"ibcd{fcp}"], tag=f"dbe{fc}")
                s.op("V", (lambda fc: lambda v: v.tensor_copy(
                        rowt[0:1, :], wsd_row[0:1, fc * 512:(fc + 1) * 512]))(fc),
                     reads=["wsd_row"], writes=["rowt"], tag=f"dsv{fc}")
                s.op("T", lambda t: t.matmul(psb2[:, :], ones32[0:1, 0:128], rowt[0:1, :],
                                             start=True, stop=True),
                     reads=["rowt", "ones32"], writes=["psb2"], tag=f"dbc2{fc}")
                s.op("V", (lambda fcp: lambda v: v.tensor_copy(sbcd[fcp][:, :], psb2[:, :]))(fcp),
                     reads=["psb2"], writes=[f"sbcd{fcp}"], tag=f"dbe2{fc}")
                for g4 in range(4):
                    gp = 0
                    off = GRAN_OFF[("wd", fc * 4 + g4)]
                    s.op("G", (lambda off: lambda g: pay_dma(
                            g, wdi2[0][:, :, :], off, [[512, 128], [128 * 512, 11], [1, 512]]))(off),
                         reads=["pay"], writes=["wdi2_0"], tag=f"dl{fc}_{g4}")

                    def fqd(g4, gp, fcp):
                        def f(v):
                            r = None
                            for j in range(11):
                                r = v.tensor_tensor(wdq[fcp][:, g4 * 11 + j, :], wdi2[gp][:, j, :],
                                                    ibcd[fcp][:, :], op=ALU.mult)
                            return r
                        return f
                    s.op("V", fqd(g4, gp, fcp), reads=[f"wdi2_{gp}", f"ibcd{fcp}"],
                         writes=[f"wdq{fcp}"], tag=f"dq{fc}_{g4}")
                for tc in range(16):
                    pi = tc % 4
                    i2 = tc % 2

                    def fdm(fc, tc, pi, fcp):
                        def f(t):
                            r = None
                            for icc in range(22):
                                r = t.matmul(psd[pi][:, :],
                                             qt[:, 2 * icc:2 * icc + 2, tc * 128:(tc + 1) * 128],
                                             wdq[fcp][:, 2 * icc:2 * icc + 2, :],
                                             start=(icc == 0), stop=(icc == 21),
                                             perf_mode=PM.DoubleRow)
                            return r
                        return f
                    s.op("T", fdm(fc, tc, pi, fcp), reads=["qt", f"wdq{fcp}"], writes=[f"psd{pi}"],
                         tag=f"dm{fc}_{tc}")
                    s.op("Q", (lambda fc, tc, i2: lambda g: g.dma_start(
                            out=x2r[i2][:, :],
                            in_=x2_sc.ap()[tc, :, fc * 512:(fc + 1) * 512]))(fc, tc, i2),
                         reads=["x2_sc"], writes=[f"x2r{i2}"], tag=f"dx{fc}_{tc}")

                    s.op("V", (lambda tc, pi, i2, fcp: lambda v: v.scalar_tensor_tensor(
                            outt[i2][:, :], psd[pi][:, :], m2f[:, tc:tc + 1],
                            sbcd[fcp][:, :], op0=ALU.mult, op1=ALU.mult))(tc, pi, i2, fcp),
                         reads=[f"psd{pi}", "m2f", f"sbcd{fcp}"], writes=[f"outt{i2}"], tag=f"deA{fc}_{tc}")
                    s.op("V", (lambda pi, i2: lambda v: v.tensor_tensor(
                            outs[i2][:, :], outt[i2][:, :], x2r[i2][:, :], op=ALU.add))(pi, i2),
                         reads=[f"outt{i2}", f"x2r{i2}"], writes=[f"outs{i2}"], tag=f"deB{fc}_{tc}")
                    s.op("G", (lambda fc, tc, i2: lambda g: g.dma_start(
                            out=out_d.ap()[tc * 128:(tc + 1) * 128, fc * 512:(fc + 1) * 512],
                            in_=outs[i2][:, :]))(fc, tc, i2),
                         reads=[f"outs{i2}"], writes=["outdone"], tag=f"do{fc}_{tc}")
            s.op("G", lambda g: g.nop(), reads=["outdone"], tag="final", barrier=True)
        s.emit_phase(nc, ctx)
        ph5c.close()
        sc5bc.close()
        s.note_free()

        s.emit_phase(nc, ctx)
    return nc


def _pack_payload(x, wqkvu, wout, wg_p, wu_p, wd):
    pay = np.zeros((8, S_SH), np.float32)
    flat = pay.reshape(-1)

    def put(key, arr):
        off = GRAN_OFF[key]
        flat[off:off + arr.size] = arr.ravel()

    wq = np.ascontiguousarray(wqkvu.T).reshape(16, 128, 16, 512).transpose(2, 1, 0, 3)
    wqk = np.ascontiguousarray(wq).reshape(1, N_WQK * GR_E)
    wo = np.ascontiguousarray(wout.T).reshape(16, 128, D).transpose(1, 0, 2)
    wo4 = wo.reshape(128, 16, 4, 512).transpose(2, 0, 1, 3)
    for i in range(4):
        put(("wout", i), wo4[i])
    for base, w in (("wg", wg_p), ("wu", wu_p)):
        wt = np.ascontiguousarray(w.T).reshape(16, 128, 11, 512).transpose(2, 1, 0, 3)
        for i in range(11):
            put((base, i), wt[i])
    wdt = np.ascontiguousarray(wd.T).reshape(44, 128, 4, 512).transpose(2, 0, 1, 3)
    for fc in range(4):
        for g4 in range(4):
            put(("wd", fc * 4 + g4), wdt[fc, 11 * g4:11 * (g4 + 1)])
    return pay, wqk


def _consts():
    arrq = np.broadcast_to(np.arange(1, 257, dtype=np.float32), (128, 256)).copy()
    p = np.arange(128, dtype=np.float32)
    kdc = np.stack([255.0 - p, 127.0 - p], axis=1).astype(np.float32)
    cm = np.empty((2, 128, 256), np.float32)
    for st in range(2):
        sidx = st * 128 + np.arange(128)[:, None]
        tidx = np.arange(256)[None, :]
        d = tidx - sidx
        cm[st] = np.where(d >= 0, d, 1.0e9).astype(np.float32)
    eye32 = np.eye(128, dtype=np.float32)
    eyebf = np.eye(128, dtype=ml_dtypes.bfloat16)
    eye8 = np.eye(128, dtype=ml_dtypes.float8_e4m3)
    onesbf = np.ones((128, 128), ml_dtypes.bfloat16)
    ones32 = np.ones((1, 128), np.float32)
    return arrq, kdc, cm, eye32, eyebf, eye8, onesbf, ones32


def kernel(L0_attn_x, L0_attn_slope_rate, L0_attn_W_qkvu, L0_attn_W_out,
           L0_ffn_w_gate, L0_ffn_w_up, L0_ffn_w_down,
           L0_ffn_up_reorder_index, L0_ffn_down_reorder_index):
    x = np.asarray(L0_attn_x, np.float32)
    wqkvu = np.asarray(L0_attn_W_qkvu, np.float32)
    wout = np.asarray(L0_attn_W_out, np.float32)
    didx = np.asarray(L0_ffn_down_reorder_index).astype(np.int64)
    wg_p = np.take(np.asarray(L0_ffn_w_gate, np.float32), didx, axis=0)
    wu_p = np.take(np.asarray(L0_ffn_w_up, np.float32), didx, axis=0)
    wd = np.asarray(L0_ffn_w_down, np.float32)
    slope = np.asarray(L0_attn_slope_rate, np.float32).reshape(1, 16)
    slope_b = np.broadcast_to(slope, (128, 16)).copy()

    pay, wqk = _pack_payload(x, wqkvu, wout, wg_p, wu_p, wd)
    arrq, kdc, cm, eye32, eyebf, eye8, onesbf, ones32 = _consts()

    nc = build_program()
    in_maps = []
    for c in range(8):
        seq, half = c // 2, c % 2
        x_own = np.ascontiguousarray(x[seq, half * T:(half + 1) * T], np.float32)
        x_prev = np.ascontiguousarray(x[seq, 0:T], np.float32)
        cpkv = np.zeros((128, 24), np.float32)
        cpkv[:, 0:2] = kdc
        cpkv[:, 2:18] = slope_b
        cpkv[:, 18] = 1.0 if half == 1 else 0.0
        cpkv[:, 19] = EPS1
        cpkv[:, 20] = EPS2
        in_maps.append({
            "shard": pay[c:c + 1], "wqk": wqk, "xin": x_own, "xpr": x_prev,
            "cpk": cpkv,
            "arrq": arrq, "cmask": cm,
            "eye32": eye32, "eyebf": eyebf, "eye8": eye8,
            "onesbf": onesbf, "ones32": ones32,
        })
    res = run_bass_kernel_spmd(nc, in_maps, core_ids=list(range(8)))
    out = np.empty((B, N, D), np.float32)
    for c in range(8):
        seq, half = c // 2, c % 2
        out[seq, half * T:(half + 1) * T] = res.results[c]["out"]
    return out
